# version 06
# baseline (speedup 1.0000x reference)
import math
import numpy as np
import concourse.bass as bass
import concourse.mybir as mybir
from concourse.bass_utils import run_bass_kernel_spmd

F32 = mybir.dt.float32
AF = mybir.ActivationFunctionType
ALU = mybir.AluOpType

B, S, DM = 2, 4096, 2048
HK = 16
DK = DV = 128
KC = 4
CH = 64
LCH = 128
NCORE = 8
HPC = 4
TPC = 1024
TT = 512
INTER0 = 8192
INTER1 = 5632
NEXP = 8

ENGINES = ("tensor", "vector", "scalar", "gpsimd", "sync")


class Op:
    __slots__ = ("idx", "engine", "fn", "ins", "outs", "dma", "token", "waits",
                 "inc_needed", "tag", "_deps", "pre_drain")

    def __init__(self, idx, engine, fn, ins, outs, dma, tag):
        self.idx = idx
        self.engine = engine
        self.fn = fn
        self.ins = ins
        self.outs = outs
        self.dma = dma
        self.token = None
        self.waits = []
        self.inc_needed = False
        self.pre_drain = False
        self.tag = tag


class Sched:

    def __init__(self):
        self.ops = []
        self.last_writer = {}
        self.readers = {}
        self.fence_set = []

    def fence(self):
        last = {}
        for op in self.ops:
            if op.dma:
                nm = op.outs[0][0] if isinstance(op.outs[0], tuple) else op.outs[0]
                key = ("dma", nm)
            else:
                key = ("c", op.engine)
            last[key] = op
        self.fence_set = list(last.values())

    def add(self, engine, fn, ins=(), outs=(), dma=False, tag="", force_inc=False):
        op = Op(len(self.ops), engine, fn,
                [k for x in ins for k in self._r(x)],
                [k for x in outs for k in self._r(x)], dma, tag)
        op.inc_needed = force_inc
        deps = []
        for r in op.ins:
            w = self.last_writer.get(r)
            if w is not None:
                deps.append(w)
        for r in op.outs:
            w = self.last_writer.get(r)
            if w is not None:
                deps.append(w)
            for rd in self.readers.get(r, ()):
                if rd is not op:
                    deps.append(rd)
        for r in op.ins:
            self.readers.setdefault(r, []).append(op)
        for r in op.outs:
            self.last_writer[r] = op
            self.readers[r] = []
        deps.extend(self.fence_set)
        op._deps = deps
        self.ops.append(op)
        return op

    @staticmethod
    def _r(x):
        if isinstance(x, str):
            return [x]
        t = getattr(x, "tensor", None)
        if t is None:
            return [x.name]
        try:
            sbuf = str(x.space).endswith("SBUF")
        except Exception:
            sbuf = False
        if not sbuf:
            return [t.name]
        try:
            base = x.base_partition()
            npart = x.shape[0]
        except Exception:
            return [t.name]
        keys = []
        if base < 64:
            keys.append((t.name, 0))
        if base + npart > 64:
            keys.append((t.name, 1))
        return keys

    def resolve_and_emit(self, nc):
        sems = {}
        for e in ENGINES:
            sems[("c", e)] = nc.alloc_semaphore(name=f"sc_{e}")
        counts = {}

        for op in self.ops:
            for d in op._deps:
                if d.dma:
                    continue
                if d.engine != op.engine:
                    d.inc_needed = True
        for op in self.ops:
            if op.dma:
                nm = op.outs[0][0] if isinstance(op.outs[0], tuple) else op.outs[0]
                key = ("d", nm)
                if key not in sems:
                    sems[key] = nc.alloc_semaphore(name=f"sd_{nm}"[:30])
                    counts[key] = 0
                counts[key] += 16
                op.token = (key, counts[key])
            else:
                key = ("c", op.engine)
                if op.inc_needed:
                    counts[key] = counts.get(key, 0) + 1
                op.token = (key, counts.get(key, 0))

        last_drain = {}
        running = {}
        for op in self.ops:
            need = {}
            for d in op._deps:
                if d.engine == op.engine and not d.dma:
                    if (not op.dma and op.engine != "tensor"
                            and d.idx >= last_drain.get(op.engine, -1)):
                        op.pre_drain = True
                    continue
                if (d.dma and op.dma and d.token[0] == op.token[0]
                        and op.engine != "gpsimd" and d.engine != "gpsimd"):
                    continue
                k, v = d.token
                if d.dma:
                    v = running.get(k, 0)
                if v > 0 and need.get(k, 0) < v:
                    need[k] = v
            if op.pre_drain:
                last_drain[op.engine] = op.idx
            op.waits = sorted(need.items())
            if op.dma:
                running[op.token[0]] = op.token[1]

        hi = {}
        for op in self.ops:
            w2 = []
            for k, v in op.waits:
                kk = (op.engine, k)
                if hi.get(kk, -1) >= v:
                    continue
                hi[kk] = v
                w2.append((k, v))
            op.waits = w2

        by_eng = {e: [] for e in ENGINES}
        for op in self.ops:
            by_eng[op.engine].append(op)

        dma_finals = [(k, v) for k, v in counts.items() if k[0] == "d"]

        with nc.Block() as block:
            def mk(e):
                def run(eng):
                    for op in by_eng[e]:
                        if op.pre_drain:
                            eng.drain()
                        for k, v in op.waits:
                            eng.wait_ge(sems[k], v)
                        ins = op.fn(eng)
                        if op.dma:
                            ins.then_inc(sems[op.token[0]], 16)
                        elif op.inc_needed:
                            ins.then_inc(sems[("c", e)], 1)
                    if e == "gpsimd":
                        for k, v in dma_finals:
                            eng.wait_ge(sems[k], v)
                return run
            for e in ENGINES:
                if by_eng[e] or e == "gpsimd":
                    getattr(block, e)(mk(e))
        return sems, counts

def repack_lhsT(w):
    K, M = w.shape
    nK, nM = K // 128, M // 128
    t = w.reshape(nK, 128, nM, 128)
    return np.ascontiguousarray(t.transpose(2, 0, 1, 3))


class Bld:

    def __init__(self, nc):
        self.nc = nc
        self.s = Sched()
        self.t = {}
        self.guards = {}
        self._wq_map = {}
        self.uid = 0
        self.banks = [nc.alloc_psum_tensor(f"bank{i}", [128, 512], F32).ap()
                      for i in range(8)]
        self.bank_i = 0
        self.held = set()

    def bank(self):
        for _ in range(16):
            ap = self.banks[self.bank_i % 8]
            self.bank_i += 1
            if ap.tensor.name not in self.held:
                return ap
        raise RuntimeError("all banks held")

    def bank_hold(self):
        ap = self.bank()
        self.held.add(ap.tensor.name)
        return ap

    def bank_release(self, ap):
        self.held.discard(ap.tensor.name)

    _WQ = ("sync", "scalar", "gpsimd")

    def wq(self, tensor_name=None):
        if tensor_name is not None:
            if tensor_name not in self._wq_map:
                self._wq_map[tensor_name] = self._WQ[len(self._wq_map) % 3]
            return self._wq_map[tensor_name]
        self.uid += 1
        return self._WQ[self.uid % 3]

    def sb(self, name, shape, dtype=F32):
        g = self.nc.sbuf_tensor(name, list(shape), dtype)
        h = g.__enter__()
        self.guards[name] = (self.uid, g)
        self.uid += 1
        ap = h.ap()
        self.t[name] = ap
        return ap

    def free(self, *names):
        for n in sorted(names, key=lambda n: -self.guards[n][0]):
            _, g = self.guards.pop(n)
            g.__exit__(None, None, None)

    def free_prefix(self, *prefixes):
        self.free(*[n for n in self.guards if any(n.startswith(p) for p in prefixes)])

    def dram(self, name, shape, dtype=F32, kind="Internal"):
        ap = self.nc.dram_tensor(name, list(shape), dtype, kind=kind).ap()
        self.t[name] = ap
        return ap

    def inp(self, name, shape, dtype=F32):
        return self.dram(name, shape, dtype, kind="ExternalInput")

    def out(self, name, shape, dtype=F32):
        return self.dram(name, shape, dtype, kind="ExternalOutput")

    def dma(self, q, out, in_, tag=""):
        self.s.add(q, lambda e, o=out, i=in_: e.dma_start(out=o, in_=i),
                   ins=[in_], outs=[out], dma=True, tag=tag)

    def mm(self, out, lhsT, rhs, start=True, stop=True, tag="", tp=None):
        self.s.add("tensor",
                   lambda e, o=out, l=lhsT, r=rhs, st=start, sp=stop, t=tp:
                       e.matmul(o, l, r, start=st, stop=sp, tile_position=t),
                   ins=[lhsT, rhs], outs=[out], tag=tag)

    def trans(self, out_psum, in_sb, ident, tag="", tp=None):
        self.s.add("tensor",
                   lambda e, o=out_psum, i=in_sb, d=ident, t=tp: e.transpose(o, i, d, tile_position=t),
                   ins=[in_sb, ident], outs=[out_psum], tag=tag)

    def act(self, out, in_, func, bias=None, scale=None, accum_out=None, eng="scalar", tag=""):
        kw = {}
        if bias is not None:
            kw["bias"] = bias
        if scale is not None:
            kw["scale"] = scale
        if accum_out is not None:
            kw["accum_out"] = accum_out
        ins = [in_] + [x for x in (bias, scale) if hasattr(x, "tensor")]
        outs = [out] + ([accum_out] if accum_out is not None else [])
        self.s.add(eng, lambda e, o=out, i=in_, f=func, k=dict(kw): e.activation(out=o, in_=i, func=f, **k),
                   ins=ins, outs=outs, tag=tag)

    def tt(self, out, a, b, op, eng="vector", tag=""):
        self.s.add(eng, lambda e, o=out, x=a, y=b, p=op: e.tensor_tensor(out=o, in0=x, in1=y, op=p),
                   ins=[a, b], outs=[out], tag=tag)

    def ts(self, out, a, s1, op0, s2=None, op1=None, eng="vector", tag=""):
        ins = [a] + [x for x in (s1, s2) if hasattr(x, "tensor")]
        if op1 is None:
            self.s.add(eng, lambda e, o=out, x=a, s=s1, p=op0:
                       e.tensor_single_scalar(out=o, in_=x, scalar=s, op=p),
                       ins=ins, outs=[out], tag=tag)
        else:
            self.s.add(eng, lambda e, o=out, x=a, sa=s1, sb=s2, p0=op0, p1=op1:
                       e.tensor_scalar(out=o, in0=x, scalar1=sa, scalar2=sb, op0=p0, op1=p1),
                       ins=ins, outs=[out], tag=tag)

    def stt(self, out, in0, scalar, in1, op0, op1, eng="vector", tag=""):
        ins = [in0, in1] + ([scalar] if hasattr(scalar, "tensor") else [])
        self.s.add(eng, lambda e, o=out, a=in0, s=scalar, b=in1, p0=op0, p1=op1:
                   e.scalar_tensor_tensor(out=o, in0=a, scalar=s, in1=b, op0=p0, op1=p1),
                   ins=ins, outs=[out], tag=tag)

    def copy(self, out, in_, eng="vector", tag=""):
        self.s.add(eng, lambda e, o=out, i=in_: e.tensor_copy(out=o, in_=i),
                   ins=[in_], outs=[out], tag=tag)

    def scopy(self, out, in_, tag=""):
        self.s.add("scalar", lambda e, o=out, i=in_: e.copy(out=o, in_=i),
                   ins=[in_], outs=[out], tag=tag)

    def memset(self, ap, val, eng="vector", tag=""):
        self.s.add(eng, lambda e, o=ap, v=val: e.memset(o, v), ins=[], outs=[ap], tag=tag)

    def scan(self, out, d0, d1, initial, op0, op1, tag=""):
        self.s.add("vector", lambda e, o=out, a=d0, b=d1, i=initial, p0=op0, p1=op1:
                   e.tensor_tensor_scan(out=o, data0=a, data1=b, initial=i, op0=p0, op1=p1),
                   ins=[d0, d1], outs=[out], tag=tag)

    def reduce(self, out, in_, op, eng="vector", tag=""):
        self.s.add(eng, lambda e, o=out, i=in_, p=op:
                   e.tensor_reduce(out=o, in_=i, axis=mybir.AxisListType.X, op=p),
                   ins=[in_], outs=[out], tag=tag)

    def cc(self, kind, ins_ap, outs_ap, groups, tag=""):
        self.s.add("gpsimd",
                   lambda e, k=kind, i=ins_ap, o=outs_ap, g=groups:
                   e.collective_compute(k, ALU.bypass, replica_groups=g, ins=[i], outs=[o]),
                   ins=[ins_ap], outs=[outs_ap], tag=tag, force_inc=True)


GROUPS4 = [[0, 1, 2, 3], [4, 5, 6, 7]]
NT0 = S // TT
LN_EPS = 1e-5


class MK:

    def __init__(self, nc, dump=()):
        self.nc = nc
        self.b = Bld(nc)
        self.dump = set(dump)
        b = self.b
        self.xT = b.inp("xT", [DM, S])
        self.xresT = b.inp("xresT", [DM, TPC])
        self.tokoff = b.inp("tokoff", [1, 32], mybir.dt.int32)
        self.wqkvz = b.inp("wqkvz", [16, 128, 2048])
        self.wba = b.inp("wba", [16, 128, 8])
        self.convw = b.inp("convw", [12, 128, 4])
        self.dtb = b.inp("dtb", [4, 1])
        self.alog = b.inp("alog", [4, 1])
        self.normw = b.inp("normw", [128, 1])
        self.wout = b.inp("wout", [16, 128, 2048])
        self.gatew = b.inp("gatew", [16, 128, 8])
        self.upw = b.inp("upw", [64, 128, 2048])
        self.downw = b.inp("downw", [16, 4, 128, 2048])
        self.expw = b.inp("expw", [16, 128, 2048])
        self.lngb = b.inp("lngb", [128, 32])
        self.wq1 = b.inp("wq1", [4, 128, 2048])
        self.wk1 = b.inp("wk1", [4, 128, 2048])
        self.wv1cols = b.inp("wv1cols", [16, 128, 512])
        self.wo1 = b.inp("wo1", [16, 128, 2048])
        self.rmsw = b.inp("rmsw", [128, 16])
        self.w1g = b.inp("w1g", [44, 128, 2048])
        self.w1u = b.inp("w1u", [44, 128, 2048])
        self.w1d = b.inp("w1d", [16, 2, 128, 2816])
        self.s1g = b.inp("s1g", [128, 44])
        self.s1u = b.inp("s1u", [128, 44])
        self.s1d = b.inp("s1d", [128, 16])
        self.c_maskS = b.inp("c_maskS", [128, 64])
        self.c_maskI = b.inp("c_maskI", [128, 64])
        self.c_i128 = b.inp("c_i128", [128, 128])
        self.c_rotcos = b.inp("c_rotcos", [64, S])
        self.c_rotsin = b.inp("c_rotsin", [64, S])
        self.c_qdec = b.inp("c_qdec", [4, 128, 128])
        self.c_dmaskT = b.inp("c_dmaskT", [4, 128, 128])
        self.c_kdec = b.inp("c_kdec", [128, 4])
        self.c_rC = b.inp("c_rC", [128, 4])
        self.c_oh64 = b.inp("c_oh64", [4, 4 * 64])
        self.c_ohc = b.inp("c_ohc", [4, 4 * 128])
        self.c_nohc = b.inp("c_nohc", [4, 4 * 128])
        self.c_oh128 = b.inp("c_oh128", [128, 16])
        self.cc1in = [b.dram("cc1inA", [512, S // 2]), b.dram("cc1inB", [512, S // 2])]
        self.cc1out = b.dram("cc1out", [2, DM, S // 2])
        self.x1d = b.dram("x1d", [DM, TPC])
        self.cc2in = [b.dram("cc2inA", [DM, TT]), b.dram("cc2inB", [DM, TT])]
        self.cc2out = [b.dram("cc2outA", [4 * DM, TT]), b.dram("cc2outB", [4 * DM, TT])]
        self.cc3in = [b.dram("cc3inA", [512, S // 2]), b.dram("cc3inB", [512, S // 2])]
        self.cc3out = b.dram("cc3out", [2, DM, S // 2])
        self.x3d = b.dram("x3d", [DM, TPC])
        self.yT = b.out("yT", [DM, TPC])
        self.dmp = {}
        for name, shape in [
            ("d_conv", [1536, S]), ("d_gates", [8, S]), ("d_mix", [512, S]),
            ("d_x1", [DM, TPC]), ("d_x2", [DM, TPC]), ("d_l1qkv", [3 * 512, S]),
            ("d_l1mix", [512, S]), ("d_x3", [DM, TPC]),
        ]:
            if name in self.dump:
                self.dmp[name] = b.out(name, shape)

    def setup_consts(self):
        b = self.b
        self.ones_col = b.sb("ones_col", [128, 1])
        b.memset(self.ones_col, 1.0)
        self.ones_row = b.sb("ones_row", [1, 128])
        b.memset(self.ones_row, 1.0)
        self.zero_11 = b.sb("zero_11", [1, 1])
        b.memset(self.zero_11, 0.0)
        self.half_neg = b.sb("half_neg", [128, 1])
        b.memset(self.half_neg, -0.5)
        self.neg_col = b.sb("neg_col", [128, 1])
        b.memset(self.neg_col, -1.0)
        self.eps_l2 = b.sb("eps_l2", [128, 1])
        b.memset(self.eps_l2, 128 * 1e-6)
        self.i128 = b.sb("i128", [128, 128])
        b.dma("sync", self.i128, self.c_i128)
        self.maskS = b.sb("maskS", [128, 64])
        b.dma("sync", self.maskS, self.c_maskS)
        self.maskI = b.sb("maskI", [128, 64])
        b.dma("sync", self.maskI, self.c_maskI)
        self.toksb = b.sb("toksb", [1, 32], mybir.dt.int32)
        b.dma("sync", self.toksb, self.tokoff)
        self.dtb_sb = b.sb("dtb_sb", [4, 1])
        b.dma("sync", self.dtb_sb, self.dtb)
        self.alog_sb = b.sb("alog_sb", [4, 1])
        b.dma("sync", self.alog_sb, self.alog)
        self.negexpA = b.sb("negexpA", [4, 1])
        b.act(self.negexpA, self.alog_sb, AF.Exp)
        b.ts(self.negexpA, self.negexpA, -1.0, ALU.mult)
        self.normw_sb = b.sb("normw_sb", [128, 1])
        b.dma("sync", self.normw_sb, self.normw)

    def setup_l0(self):
        b = self.b
        self.xt = [b.sb(f"xt{i}", [128, 16 * TT]) for i in range(2)]
        self.wqz_sl = [b.sb(f"wqz{i}", [128, 16 * 128]) for i in range(2)]
        self.wba_sb = b.sb("wba_sb", [128, 16 * 8])
        for k in range(16):
            b.dma("sync", self.wba_sb[:, k * 8:(k + 1) * 8], self.wba[k])
        self.convw_sb = b.sb("convw_sb", [128, 12 * 4])
        for c in range(12):
            b.dma("sync", self.convw_sb[:, c * 4:(c + 1) * 4], self.convw[c])
        self.cvin = [b.sb(f"cvin{c}", [128, TT + 3]) for c in range(12)]
        for c in range(12):
            b.memset(self.cvin[c][:, 0:3], 0.0)
        self.zgw = b.sb("zgw", [128, 4 * TT])
        self.Sst = [b.sb(f"Sst{h}", [128, 128]) for h in range(4)]
        for h in range(4):
            b.memset(self.Sst[h], 0.0)
        for nm in ["gb_b", "gb_a", "gb_lnb", "gb_beta", "gb_gc",
                   "gb_lnq", "gb_lnk", "gb_rowA", "gb_rowB",
                   "gb_rowC", "gb_t0", "gb_t1", "q_rs", "q_c1", "q_c2"]:
            setattr(self, nm, b.sb(nm, [4, TT]))
        self.oh64 = b.sb("oh64", [4, 4 * 64])
        b.dma("sync", self.oh64, self.c_oh64)
        self.ohc = b.sb("ohc", [4, 4 * 128])
        b.dma("sync", self.ohc, self.c_ohc)
        self.nohc = b.sb("nohc", [4, 4 * 128])
        b.dma("sync", self.nohc, self.c_nohc)
        self.oh128 = b.sb("oh128", [128, 16])
        b.dma("sync", self.oh128, self.c_oh128)
        self.sc128_4 = b.sb("sc128_4", [4, 1]); b.memset(self.sc128_4, 128.0)
        self.bi128_4 = b.sb("bi128_4", [4, 1]); b.memset(self.bi128_4, 128 * 1e-6)
        self.one4 = b.sb("one4", [4, 1]); b.memset(self.one4, 1.0)
        self.e6_4 = b.sb("e6_4", [4, 1]); b.memset(self.e6_4, 1e-6)
        self.negcol4 = b.sb("negcol4", [4, 1]); b.memset(self.negcol4, -1.0)
        self.onecol4 = b.sb("onecol4", [4, 1]); b.memset(self.onecol4, 1.0)
        self.sc128 = b.sb("sc128", [1, 1]); b.memset(self.sc128, 128.0)
        self.bi128 = b.sb("bi128", [1, 1]); b.memset(self.bi128, 128 * 1e-6)
        self.one11 = b.sb("one11", [1, 1]); b.memset(self.one11, 1.0)
        self.e6_11 = b.sb("e6_11", [1, 1]); b.memset(self.e6_11, 1e-6)
        self.colq = {q: b.sb(f"colq_{q}", [128, 16]) for q in ("rs", "c1", "c2", "be")}
        self.eg_bc = [b.sb(f"eg_bc{h}", [128, 8]) for h in range(4)]
        self.conv_sc = [b.sb(f"conv_sc{j}", [128, TT]) for j in range(4)]
        self.l2ps = [self.b.bank_hold(), self.b.bank_hold()]
        self.tsave = b.sb("tsave", [128, 36])
        self.sq_sc = b.sb("sq_sc", [128, TT])
        def pairbuf(nm, w):
            ts2 = [b.sb(f"u_{nm}{s}", [128, w]) for s in range(2)]
            return lambda h, ts2=ts2: ts2[h // 2][64 * (h % 2):64 * (h % 2) + 64, :]
        self.u_pwN = pairbuf("pwN", 64 * 6)
        self.u_pwP = pairbuf("pwP", 64 * 6)
        self.u_Dt = pairbuf("Dt", 64)
        self.u_Da = pairbuf("Da", 64)
        self.u_sc64 = pairbuf("sc64", 64)
        self.u_TmT = pairbuf("TmT", 64)
        self.u_attnT = pairbuf("attnT", 64)
        self.u_vtm = pairbuf("vtm", 128)
        self.u_ktm = pairbuf("ktm", 128)
        self.u_y0 = pairbuf("y0", 128)
        self.u_y1 = pairbuf("y1", 128)
        self.u_out = pairbuf("out", 128)
        self.u_sq = pairbuf("sq", 128)
        self.u_vv = pairbuf("vv", 3)
        self.inv128c = b.sb("inv128c", [128, 1]); b.memset(self.inv128c, 1.0 / 128.0)
        self.epsc = b.sb("epsc", [128, 1]); b.memset(self.epsc, 1e-6)
        self.nhalf64 = b.sb("nhalf64", [128, 1]); b.memset(self.nhalf64, -0.5)
        self.mix = b.sb("mix", [128, 4 * TT])

    def l0_tile(self, t):
        b = self.b
        xt = self.xt[t % 2]
        c0 = t * TT
        for k in range(16):
            b.dma("sync", xt[:, k * TT:(k + 1) * TT],
                  self.xT[k * 128:(k + 1) * 128, c0:c0 + TT], tag=f"xt{t}")
        psbb = b.bank()
        for k in range(16):
            b.mm(psbb[:4, 0:TT], self.wba_sb[:, k * 8:k * 8 + 4],
                 xt[:, k * TT:(k + 1) * TT], start=(k == 0), stop=(k == 15))
        psba = b.bank()
        for k in range(16):
            b.mm(psba[:4, 0:TT], self.wba_sb[:, k * 8 + 4:k * 8 + 8],
                 xt[:, k * TT:(k + 1) * TT], start=(k == 0), stop=(k == 15))
        b.copy(self.gb_b[:], psbb[0:4, 0:TT])
        b.copy(self.gb_a[:], psba[0:4, 0:TT])
        gb = self
        b.act(gb.gb_t0, gb.gb_b, AF.Abs)
        b.act(gb.gb_t0, gb.gb_t0, AF.Exp, scale=self.negcol4)
        b.act(gb.gb_t0, gb.gb_t0, AF.Ln, bias=self.onecol4)
        b.ts(gb.gb_t1, gb.gb_b, 0.0, ALU.min)
        b.tt(gb.gb_lnb, gb.gb_t1, gb.gb_t0, ALU.subtract)
        b.act(gb.gb_beta, gb.gb_lnb, AF.Exp)
        b.ts(gb.gb_t1, gb.gb_a, self.dtb_sb, ALU.add)
        b.act(gb.gb_t0, gb.gb_t1, AF.Abs)
        b.act(gb.gb_t0, gb.gb_t0, AF.Exp, scale=self.negcol4)
        b.act(gb.gb_t0, gb.gb_t0, AF.Ln, bias=self.onecol4)
        b.ts(gb.gb_t1, gb.gb_t1, 0.0, ALU.max)
        b.tt(gb.gb_t1, gb.gb_t1, gb.gb_t0, ALU.add)
        b.ts(gb.gb_t1, gb.gb_t1, self.negexpA, ALU.mult)
        b.scan(gb.gb_gc, gb.gb_t1, gb.gb_t1, 0.0, ALU.add, ALU.bypass)
        for h in range(4):
            pse = b.bank()
            b.mm(pse[:, 0:8], self.ohc[:, h * 128:(h + 1) * 128],
                 gb.gb_gc[:, 63::64], start=True, stop=False, tag="egl1")
            b.mm(pse[:, 1:8], self.nohc[:, h * 128:(h + 1) * 128],
                 gb.gb_gc[:, 63:TT - 64:64], start=False, stop=True, tag="egl2")
            b.act(self.eg_bc[h], pse[:, 0:8], AF.Exp, tag="egl3")
        cw = self.convw_sb
        for m in range(16):
            wsl = self.wqz_sl[m % 2]
            b.dma(b.wq(wsl.tensor.name), wsl, self.wqkvz[m], tag=f"wq{t}_{m}")
            ps = b.bank()
            for k in range(16):
                b.mm(ps[:, 0:TT], wsl[:, k * 128:(k + 1) * 128],
                     xt[:, k * TT:(k + 1) * TT], start=(k == 0), stop=(k == 15))
            h, tau = m // 4, m % 4
            if tau < 3:
                cc = 3 * h + tau
                if t > 0:
                    b.copy(self.cvin[cc][:, 0:3], self.tsave[:, cc * 3:cc * 3 + 3], tag="halo")
                b.copy(self.cvin[cc][:, 3:TT + 3], ps[:, 0:TT], tag="cvev")
                b.copy(self.tsave[:, cc * 3:cc * 3 + 3], self.cvin[cc][:, TT:TT + 3], tag="tsv")
            else:
                b.act(self.zgw[:, h * TT:(h + 1) * TT], ps[:, 0:TT], AF.Silu, tag="zsilu")
                b.ts(self.zgw[:, h * TT:(h + 1) * TT], self.zgw[:, h * TT:(h + 1) * TT],
                     self.normw_sb, ALU.mult, tag="zw")
                ccs = [3 * h + 0, 3 * h + 1, 3 * h + 2]
                for j, c in enumerate(ccs):
                    b.ts(self.conv_sc[j], self.cvin[c][:, 0:TT],
                         cw[:, c * 4 + 0:c * 4 + 1], ALU.mult, tag="cv0")
                for tap in (1, 2, 3):
                    for j, c in enumerate(ccs):
                        b.stt(self.conv_sc[j], self.cvin[c][:, tap:TT + tap],
                              cw[:, c * 4 + tap:c * 4 + tap + 1], self.conv_sc[j],
                              ALU.mult, ALU.add, tag=f"cv{tap}")
                for j, c in enumerate(ccs):
                    b.act(self.cvin[c][:, 3:TT + 3], self.conv_sc[j], AF.Silu, tag="cvsilu")
                for (tau2, blk, sc_, bi_) in ((0, gb.gb_lnq, self.sc128_4, self.bi128_4),
                                              (1, gb.gb_lnk, self.one4, self.e6_4)):
                    cc2 = 3 * h + tau2
                    b.act(self.sq_sc, self.cvin[cc2][:, 3:TT + 3], AF.Square, tag="l2sq")
                    psn = self.l2ps[tau2]
                    b.mm(psn[0:4, 0:TT], self.oh128[:, 4 * h:4 * h + 4], self.sq_sc,
                         start=(h == 0), stop=(h == 3), tag="l2mm")
                    if h == 3:
                        b.act(blk, psn[0:4, 0:TT], AF.Ln, scale=sc_, bias=bi_, tag="l2ln")
        b.ts(gb.gb_lnq, gb.gb_lnq, -0.5, ALU.mult)
        b.ts(gb.gb_lnk, gb.gb_lnk, -0.5, ALU.mult)
        b.stt(gb.gb_rowA, gb.gb_gc, -1.0, gb.gb_lnk, ALU.mult, ALU.add)
        b.tt(gb.gb_rowB, gb.gb_gc, gb.gb_lnq, ALU.add)
        b.tt(gb.gb_t0, gb.gb_gc, gb.gb_lnk, ALU.add)
        b.tt(gb.gb_rowC, gb.gb_t0, gb.gb_lnb, ALU.add)
        for h in range(4):
            pse = b.bank()
            b.mm(pse[:, 0:8], self.ohc[:, h * 128:(h + 1) * 128],
                 gb.gb_gc[:, 63::64], start=True, stop=False, tag="egl1")
            b.mm(pse[:, 1:8], self.nohc[:, h * 128:(h + 1) * 128],
                 gb.gb_gc[:, 63:TT - 64:64], start=False, stop=True, tag="egl2")
            b.act(self.eg_bc[h], pse[:, 0:8], AF.Exp, tag="egl3")
        for nl in range(8):
            cs = slice(nl * 64, nl * 64 + 64)
            gl = gb.gb_gc[:, nl * 64 + 63:nl * 64 + 64]
            if nl == 0:
                b.copy(self.q_rs[:, cs], gb.gb_rowB[:, cs])
                b.copy(self.q_c1[:, cs], gb.gb_rowC[:, cs])
            else:
                gp = gb.gb_gc[:, nl * 64 - 1:nl * 64]
                b.ts(self.q_rs[:, cs], gb.gb_rowB[:, cs], gp, ALU.subtract)
                b.ts(self.q_c1[:, cs], gb.gb_rowC[:, cs], gp, ALU.subtract)
            b.ts(self.q_c2[:, cs], gb.gb_rowA[:, cs], gl, ALU.add)
        for q in ("rs", "c1", "c2"):
            blk = {"rs": self.q_rs, "c1": self.q_c1, "c2": self.q_c2}[q]
            b.act(blk, blk, AF.Exp, tag="qexp")
        for nl in range(8):
            for q in ("rs", "c1", "c2", "be"):
                blk = {"rs": self.q_rs, "c1": self.q_c1, "c2": self.q_c2,
                       "be": self.gb_beta}[q]
                pst = b.bank()
                b.mm(pst[0:64, 0:4], blk[:, nl * 64:(nl + 1) * 64],
                     self.i128[:4, :4], tag="stgT", tp=(0, 0))
                b.mm(pst[64:128, 4:8], blk[:, nl * 64:(nl + 1) * 64],
                     self.i128[:4, :4], tag="stgT2", tp=(0, 64))
                cq = self.colq[q]
                for s in range(2):
                    b.copy(cq[0:64, 2 * nl + s:2 * nl + s + 1],
                           pst[0:64, 2 * s:2 * s + 1], tag="cqc")
                    b.copy(cq[64:128, 2 * nl + s:2 * nl + s + 1],
                           pst[64:128, 4 + 2 * s + 1:4 + 2 * s + 2], tag="cqc")
        for nl in range(8):
            for h in range(4):
                for _ in self.delta_unit(t, h, nl):
                    pass
        cin = self.cc1in[t // 4]
        cx = (t % 4) * TT
        for h in range(4):
            b.dma("gpsimd", cin[h * 128:(h + 1) * 128, cx:cx + TT],
                  self.mix[:, h * TT:(h + 1) * TT], tag="mixout")
        if "d_conv" in self.dmp:
            for c in range(12):
                b.dma("gpsimd", self.dmp["d_conv"][c * 128:(c + 1) * 128, c0:c0 + TT],
                      self.cvin[c][:, 3:TT + 3])
        if "d_gates" in self.dmp:
            b.dma("gpsimd", self.dmp["d_gates"][0:4, c0:c0 + TT], self.gb_beta[:])
            b.dma("gpsimd", self.dmp["d_gates"][4:8, c0:c0 + TT], self.gb_gc[:])

    def delta_unit(self, t, h, nl):
        b = self.b
        s, pi = h // 2, h % 2
        o = 64 * pi
        sl = slice(o, o + 64)
        cs = slice(nl * 64, nl * 64 + 64)
        qf = self.cvin[3 * h + 0][:, 3 + nl * 64:3 + nl * 64 + 64]
        kf = self.cvin[3 * h + 1][:, 3 + nl * 64:3 + nl * 64 + 64]
        vf = self.cvin[3 * h + 2][:, 3 + nl * 64:3 + nl * 64 + 64]
        def cv(q):
            return self.colq[q][sl, 2 * nl + s:2 * nl + s + 1]
        S = self.Sst[h]
        I64 = self.i128[:64, :64]
        maskS = self.maskS[sl, :]
        maskI = self.maskI[sl, :]
        psG = b.bank(); b.mm(psG[sl, 0:64], kf, kf, tag="gramK", tp=(0, o))
        psA = b.bank(); b.mm(psA[sl, 0:64], kf, qf, tag="gramA", tp=(0, o))
        yield
        oh = self.oh64[:, h * 64:(h + 1) * 64]
        psMt = b.bank()
        b.mm(psMt[sl, 0:64], self.gb_rowA[:, cs], oh, start=True, stop=False, tp=(0, o))
        b.mm(psMt[sl, 0:64], oh, self.gb_rowC[:, cs], start=False, stop=True, tp=(0, o))
        yield
        psMa = b.bank()
        b.mm(psMa[sl, 0:64], self.gb_rowA[:, cs], oh, start=True, stop=False, tp=(0, o))
        b.mm(psMa[sl, 0:64], oh, self.gb_rowB[:, cs], start=False, stop=True, tp=(0, o))
        yield
        Dt, Da, sc64 = self.u_Dt(h), self.u_Da(h), self.u_sc64(h)
        b.tt(sc64, psMt[sl, 0:64], maskS, ALU.add)
        b.act(Dt, sc64, AF.Exp)
        yield
        b.tt(sc64, psMa[sl, 0:64], maskI, ALU.add)
        b.act(Da, sc64, AF.Exp)
        yield
        TmT, attnT = self.u_TmT(h), self.u_attnT(h)
        b.stt(TmT, psG[sl, 0:64], -1.0, Dt, ALU.mult, ALU.mult)
        yield
        b.stt(attnT, psA[sl, 0:64], -1.0, Da, ALU.mult, ALU.mult)
        yield
        pwN, pwP = self.u_pwN(h), self.u_pwP(h)
        psT = b.bank()
        b.mm(psT[sl, 0:64], TmT, self.i128[sl, sl], tag="TmTt", tp=(o, o))
        b.copy(pwN[:, 0:64], psT[sl, 0:64])
        b.copy(pwP[:, 0:64], TmT)
        for k in range(5):
            yield
            Nk = pwN[:, k * 64:(k + 1) * 64]
            Pk = pwP[:, k * 64:(k + 1) * 64]
            ps1 = b.bank()
            b.mm(ps1[sl, 0:64], Pk, Nk, tag="powN", tp=(o, o))
            if k < 4:
                b.copy(pwN[:, (k + 1) * 64:(k + 2) * 64], ps1[sl, 0:64])
            ps2 = b.bank()
            b.mm(ps2[sl, 0:64], Nk, Pk, tag="powP", tp=(o, o))
            b.copy(pwP[:, (k + 1) * 64:(k + 2) * 64], ps2[sl, 0:64])
        yield
        psv = b.bank()
        b.mm(psv[sl, 0:128], vf, self.i128, tag="vT", tp=(0, o))
        vtm = self.u_vtm(h)
        b.ts(vtm, psv[sl, 0:128], cv("be"), ALU.mult)
        yield
        psk = b.bank()
        b.mm(psk[sl, 0:128], kf, S, tag="kS", tp=(0, o))
        y0, y1 = self.u_y0(h), self.u_y1(h)
        b.stt(y0, psk[sl, 0:128], cv("c1"), vtm, ALU.mult, ALU.subtract)
        cur, nxt = y0, y1
        for k in range(5, -1, -1):
            yield
            Pk = pwP[:, k * 64:(k + 1) * 64]
            psy = b.bank()
            b.mm(psy[sl, 0:128], Pk, cur, tag="applyP", tp=(o, o))
            b.tt(nxt, psy[sl, 0:128], cur, ALU.add)
            cur, nxt = nxt, cur
        ym = cur
        yield
        pso = b.bank()
        b.mm(pso[sl, 0:128], attnT, ym, tag="attnV", tp=(o, o))
        out_sb = self.u_out(h)
        b.copy(out_sb, pso[sl, 0:128])
        yield
        psq = b.bank()
        b.mm(psq[sl, 0:128], qf, S, tag="qS", tp=(0, o))
        b.stt(out_sb, psq[sl, 0:128], cv("rs"), out_sb, ALU.mult, ALU.add)
        yield
        sq, vv = self.u_sq(h), self.u_vv(h)
        b.act(sq, out_sb, AF.Square, accum_out=vv[:, 0:1])
        b.act(vv[:, 1:2], vv[:, 0:1], AF.Ln, scale=self.inv128c[sl, :], bias=self.epsc[sl, :])
        b.act(vv[:, 2:3], vv[:, 1:2], AF.Exp, scale=self.nhalf64[sl, :])
        yield
        b.ts(sq, out_sb, vv[:, 2:3], ALU.mult)
        psn = b.bank()
        b.mm(psn[:, 0:64], sq, self.i128[sl, sl], tag="outT", tp=(o, 0))
        b.tt(self.mix[:, h * TT + nl * 64:h * TT + nl * 64 + 64], psn[:, 0:64],
             self.zgw[:, h * TT + nl * 64:h * TT + nl * 64 + 64], ALU.mult, tag="mixw")
        yield
        pskt = b.bank()
        b.mm(pskt[sl, 0:128], kf, self.i128, tag="kT", tp=(0, o))
        ktm = self.u_ktm(h)
        b.ts(ktm, pskt[sl, 0:128], cv("c2"), ALU.mult)
        yield
        psu = b.bank()
        b.mm(psu[:, 0:128], ktm, ym, tag="Supd", tp=(o, 0))
        egl = self.eg_bc[h][:, nl:nl + 1]
        b.stt(S, S, egl, psu[:, 0:128], ALU.mult, ALU.subtract)

    def ln_kit(self, pref):
        b = self.b
        k = {}
        k["sq"] = [b.sb(f"{pref}_sq{i}", [128, TT]) for i in range(2)]
        k["st"] = b.sb(f"{pref}_st", [128, 12])
        k["rrow"] = b.sb(f"{pref}_rrow", [1, TT])
        k["nrow"] = b.sb(f"{pref}_nrow", [1, TT])
        k["rbc"] = b.sb(f"{pref}_rbc", [128, TT])
        k["nbc"] = b.sb(f"{pref}_nbc", [128, TT])
        k["eps"] = b.sb(f"{pref}_eps", [128, 1])
        b.memset(k["eps"], LN_EPS)
        k["nh"] = b.sb(f"{pref}_nh", [128, 1])
        b.memset(k["nh"], -0.5)
        k["inv"] = b.sb(f"{pref}_inv", [128, 1])
        b.memset(k["inv"], 1.0 / DM)
        return k

    def ln_tile(self, kit, acc, out, gamma=None, beta=None, tag="ln"):
        b = self.b
        bs = [b.bank() for _ in range(8)]
        for kk in range(16):
            ch = acc[:, kk * TT:(kk + 1) * TT]
            sq = kit["sq"][kk % 2]
            b.act(sq, ch, AF.Square, tag=tag + "sq")
            for tg in range(4):
                b.mm(bs[tg][:, 0:1], ch[:, tg * 128:(tg + 1) * 128], self.ones_col,
                     start=(kk == 0), stop=(kk == 15), tag=tag + "s")
                b.mm(bs[4 + tg][:, 0:1], sq[:, tg * 128:(tg + 1) * 128], self.ones_col,
                     start=(kk == 0), stop=(kk == 15), tag=tag + "q")
        st = kit["st"]
        for tg in range(4):
            c = st[:, tg * 3:tg * 3 + 3]
            b.ts(c[:, 0:1], bs[tg][:, 0:1], 1.0 / DM, ALU.mult)
            b.ts(c[:, 1:2], bs[4 + tg][:, 0:1], 1.0 / DM, ALU.mult)
        for tg in range(4):
            c = st[:, tg * 3:tg * 3 + 3]
            b.tt(c[:, 2:3], c[:, 0:1], c[:, 0:1], ALU.mult)
            b.tt(c[:, 1:2], c[:, 1:2], c[:, 2:3], ALU.subtract)
            b.act(c[:, 1:2], c[:, 1:2], AF.Ln, bias=kit["eps"])
            b.act(c[:, 1:2], c[:, 1:2], AF.Exp, scale=kit["nh"])
            b.tt(c[:, 2:3], c[:, 0:1], c[:, 1:2], ALU.mult)
        for tg in range(4):
            c = st[:, tg * 3:tg * 3 + 3]
            pst = b.bank()
            b.trans(pst[:1, 0:128], c[:, 1:2], self.i128, tag=tag + "T1")
            b.copy(kit["rrow"][:, tg * 128:(tg + 1) * 128], pst[:1, 0:128])
            pst2 = b.bank()
            b.trans(pst2[:1, 0:128], c[:, 2:3], self.i128, tag=tag + "T2")
            b.copy(kit["nrow"][:, tg * 128:(tg + 1) * 128], pst2[:1, 0:128])
        psb = b.bank()
        b.mm(psb[:, 0:TT], self.ones_row, kit["rrow"], tag=tag + "bc1")
        b.copy(kit["rbc"], psb[:, 0:TT])
        psb2 = b.bank()
        b.mm(psb2[:, 0:TT], self.ones_row, kit["nrow"], tag=tag + "bc2")
        b.copy(kit["nbc"], psb2[:, 0:TT])
        for kk in range(16):
            b.tt(out[:, kk * TT:(kk + 1) * TT], acc[:, kk * TT:(kk + 1) * TT],
                 kit["rbc"], ALU.mult, tag=tag + "n1")
        for kk in range(16):
            och = out[:, kk * TT:(kk + 1) * TT]
            b.tt(och, och, kit["nbc"], ALU.subtract, tag=tag + "n2")
        if gamma is not None:
            for kk in range(16):
                och = out[:, kk * TT:(kk + 1) * TT]
                b.ts(och, och, gamma[:, kk:kk + 1], ALU.mult,
                     beta[:, kk:kk + 1], ALU.add, tag=tag + "gb")

    def p3(self):
        b = self.b
        gbuf = b.sb("p3_gbuf", [128, 16 * TT])
        rbuf = b.sb("p3_rbuf", [128, 16 * TT])
        acc = b.sb("p3_acc", [128, 16 * TT])
        wsl = [b.sb(f"p3_w{i}", [128, 2048]) for i in range(2)]
        kit = self.ln_kit("p3")
        for tt2 in range(2):
            for k0 in range(0, 16, 4):
                idx = k0 * 2 + tt2
                dst3 = gbuf[:, k0 * TT:(k0 + 4) * TT].rearrange("p (k c) -> p k c", k=4)
                src3 = self.cc1out[0, k0 * 128:(k0 + 4) * 128, 0:TT].rearrange(
                    "(k p) c -> p k c", p=128)
                def ldfn(e, idx=idx, dst=dst3, s3=src3):
                    v = e.value_load(self.toksb[0:1, idx:idx + 1])
                    import dataclasses as _dc
                    return e.dma_start(out=dst, in_=_dc.replace(s3, offset=v))
                b.s.add("sync", ldfn, ins=[self.toksb, self.cc1out],
                        outs=[gbuf], dma=True, tag="p3g")
            for kk in range(16):
                b.dma("sync", rbuf[:, kk * TT:(kk + 1) * TT],
                      self.xresT[kk * 128:(kk + 1) * 128, tt2 * TT:(tt2 + 1) * TT], tag="p3r")
            for m in range(16):
                w = wsl[m % 2]
                b.dma(b.wq(w.tensor.name), w, self.wout[m], tag="p3w")
                ps = b.bank()
                for kk in range(16):
                    b.mm(ps[:, 0:TT], w[:, kk * 128:(kk + 1) * 128],
                         gbuf[:, kk * TT:(kk + 1) * TT], start=(kk == 0), stop=(kk == 15))
                b.tt(acc[:, m * TT:(m + 1) * TT], ps[:, 0:TT],
                     rbuf[:, m * TT:(m + 1) * TT], ALU.add, tag="p3res")
            self.ln_tile(kit, acc, acc, tag="p3ln")
            for m in range(16):
                b.dma("gpsimd", self.x1d[m * 128:(m + 1) * 128, tt2 * TT:(tt2 + 1) * TT],
                      acc[:, m * TT:(m + 1) * TT], tag="p3o")
        b.free_prefix("p3_")

    def ffn0(self):
        b = self.b
        x1b = b.sb("f0_x1", [128, 16 * TT])
        accb = b.sb("f0_acc", [128, 16 * TT])
        kit = self.ln_kit("f0k")
        gatew_sb = b.sb("f0_gw", [128, 16 * 8])
        for kk in range(16):
            b.dma("sync", gatew_sb[:, kk * 8:(kk + 1) * 8], self.gatew[kk])
        lngam = b.sb("f0_lng", [128, 16])
        lnbet = b.sb("f0_lnb", [128, 16])
        b.dma("sync", lngam, self.lngb[:, 0:16])
        b.dma("sync", lnbet, self.lngb[:, 16:32])
        lg = b.sb("f0_lg", [128, 8])
        eqt = b.sb("f0_eq", [128, 8])
        pfa = b.sb("f0_pfa", [128, 8])
        pfb = b.sb("f0_pfb", [128, 8])
        m1 = b.sb("f0_m1", [128, 1])
        m2 = b.sb("f0_m2", [128, 1])
        srow = b.sb("f0_srow", [1, TT])
        sbc = b.sb("f0_sbc", [128, TT])
        esl = [b.sb(f"f0_ew{i}", [128, 2048]) for i in range(2)]
        usl = [b.sb(f"f0_uw{i}", [128, 2048]) for i in range(2)]
        dsl = [b.sb(f"f0_dw{i}", [128, 2048]) for i in range(2)]
        hbuf = b.sb("f0_h", [128, 16 * TT])
        for tt2 in range(2):
            for kk in range(16):
                b.dma("sync", x1b[:, kk * TT:(kk + 1) * TT],
                      self.x1d[kk * 128:(kk + 1) * 128, tt2 * TT:(tt2 + 1) * TT], tag="f0x")
            for tg in range(4):
                psg = b.bank()
                for kk in range(16):
                    b.mm(psg[:, 0:8], x1b[:, kk * TT + tg * 128:kk * TT + (tg + 1) * 128],
                         gatew_sb[:, kk * 8:(kk + 1) * 8], start=(kk == 0), stop=(kk == 15),
                         tag="f0gmm")
                b.copy(lg, psg[:, 0:8])
                b.reduce(m1, lg, ALU.max)
                b.ts(eqt, lg, m1, ALU.is_ge)
                b.scan(pfa, eqt, eqt, 0.0, ALU.add, ALU.bypass)
                b.tt(pfa, pfa, eqt, ALU.subtract)
                b.ts(pfb, pfa, 0.0, ALU.is_equal)
                b.tt(eqt, eqt, pfb, ALU.mult)
                b.stt(lg, eqt, -3e38, lg, ALU.mult, ALU.add)
                b.reduce(m2, lg, ALU.max)
                b.tt(m1, m1, m2, ALU.add)
                pst = b.bank()
                b.trans(pst[:1, 0:128], m1, self.i128, tag="f0T")
                b.copy(srow[:, tg * 128:(tg + 1) * 128], pst[:1, 0:128])
            psb = b.bank()
            b.mm(psb[:, 0:TT], self.ones_row, srow, tag="f0bc")
            b.copy(sbc, psb[:, 0:TT])
            for m in range(16):
                w = esl[m % 2]
                b.dma(b.wq(w.tensor.name), w, self.expw[m], tag="f0ew")
                ps = b.bank()
                for kk in range(16):
                    b.mm(ps[:, 0:TT], w[:, kk * 128:(kk + 1) * 128],
                         x1b[:, kk * TT:(kk + 1) * TT], start=(kk == 0), stop=(kk == 15))
                b.tt(accb[:, m * TT:(m + 1) * TT], ps[:, 0:TT], sbc, ALU.mult, tag="f0moe")
            for p in range(4):
                for mi in range(16):
                    m = p * 16 + mi
                    w = usl[mi % 2]
                    b.dma(b.wq(w.tensor.name), w, self.upw[m], tag="f0uw")
                    ps = b.bank()
                    for kk in range(16):
                        b.mm(ps[:, 0:TT], w[:, kk * 128:(kk + 1) * 128],
                             x1b[:, kk * TT:(kk + 1) * TT], start=(kk == 0), stop=(kk == 15))
                    hch = hbuf[:, mi * TT:(mi + 1) * TT]
                    b.act(hch, ps[:, 0:TT], AF.Relu, tag="f0relu")
                    b.act(hch, hch, AF.Square, tag="f0sq")
                for mo in range(16):
                    w = dsl[mo % 2]
                    b.dma(b.wq(w.tensor.name), w, self.downw[mo, p], tag="f0dw")
                    ps = b.bank()
                    for kk in range(16):
                        b.mm(ps[:, 0:TT], w[:, kk * 128:(kk + 1) * 128],
                             hbuf[:, kk * TT:(kk + 1) * TT], start=(kk == 0), stop=(kk == 15))
                    och = accb[:, mo * TT:(mo + 1) * TT]
                    b.tt(och, och, ps[:, 0:TT], ALU.add, tag="f0dacc")
            self.ln_tile(kit, accb, accb, gamma=lngam, beta=lnbet, tag="f0l1")
            for kk in range(16):
                b.tt(accb[:, kk * TT:(kk + 1) * TT], accb[:, kk * TT:(kk + 1) * TT],
                     x1b[:, kk * TT:(kk + 1) * TT], ALU.add, tag="f0r")
            self.ln_tile(kit, accb, accb, tag="f0l2")
            for m in range(16):
                b.dma("gpsimd", self.cc2in[tt2][m * 128:(m + 1) * 128, 0:TT],
                      accb[:, m * TT:(m + 1) * TT], tag="f0o")
            if tt2 == 0:
                b.cc("AllGather", self.cc2in[0][:], self.cc2out[0][:], GROUPS4, tag="E2a")
            if "d_x2" in self.dmp:
                for m in range(16):
                    b.dma("gpsimd", self.dmp["d_x2"][m * 128:(m + 1) * 128, tt2 * TT:(tt2 + 1) * TT],
                          accb[:, m * TT:(m + 1) * TT])
        b.free_prefix("f0_", "f0k_")

    def setup_l1(self):
        b = self.b
        self.xg = b.sb("l1_xg", [128, 16 * TT])
        self.wq1s = b.sb("l1_wq", [128, 4 * 2048])
        self.wk1s = b.sb("l1_wk", [128, 4 * 2048])
        self.wv1s = b.sb("l1_wv", [128, 16 * 512])
        for m in range(4):
            b.dma("sync", self.wq1s[:, m * 2048:(m + 1) * 2048], self.wq1[m])
            b.dma("sync", self.wk1s[:, m * 2048:(m + 1) * 2048], self.wk1[m])
        for kk in range(16):
            b.dma("sync", self.wv1s[:, kk * 512:(kk + 1) * 512], self.wv1cols[kk])
        self.qb = b.sb("l1_qb", [128, 4 * TT])
        self.kb = b.sb("l1_kb", [128, 4 * TT])
        self.vb = b.sb("l1_vb", [128, 4 * TT])
        self.cosb = b.sb("l1_cos", [64, TT])
        self.sinb = b.sb("l1_sin", [64, TT])
        self.rot = [b.sb(f"l1_rot{j}", [64, TT]) for j in range(4)]
        self.kv = [b.sb(f"l1_kv{h}", [128, 128]) for h in range(4)]
        for h in range(4):
            b.memset(self.kv[h], 0.0)
        self.qdecs = b.sb("l1_qdec", [128, 4 * 128])
        self.dmks = b.sb("l1_dmk", [128, 4 * 128])
        self.kdecs = b.sb("l1_kdec", [128, 4])
        self.rcs = b.sb("l1_rc", [128, 4])
        for h in range(4):
            b.dma("sync", self.qdecs[:, h * 128:(h + 1) * 128], self.c_qdec[h])
            b.dma("sync", self.dmks[:, h * 128:(h + 1) * 128], self.c_dmaskT[h])
        b.dma("sync", self.kdecs, self.c_kdec)
        b.dma("sync", self.rcs, self.c_rC)
        self.qd_sc = b.sb("l1_qd", [128, 128])
        self.sT_sc = b.sb("l1_sT", [128, 128])
        self.otm_sc = b.sb("l1_otm", [128, 128])
        self.ktm_sc = b.sb("l1_ktm", [128, 128])
        self.o3 = b.sb("l1_o3", [128, 4 * TT])

    def l1_tile(self, t):
        b = self.b
        r, half = t // 2, t % 2
        for kk in range(16):
            b.dma("sync", self.xg[:, kk * TT:(kk + 1) * TT],
                  self.cc2out[half][r * DM + kk * 128: r * DM + (kk + 1) * 128, 0:TT],
                  tag="l1x")
        c0 = t * TT
        b.dma("sync", self.cosb, self.c_rotcos[:, c0:c0 + TT], tag="l1cs")
        b.dma("sync", self.sinb, self.c_rotsin[:, c0:c0 + TT], tag="l1cs")
        for (wsb, dst) in ((self.wq1s, self.qb), (self.wk1s, self.kb)):
            for m in range(4):
                ps = b.bank()
                for kk in range(16):
                    b.mm(ps[:, 0:TT], wsb[:, m * 2048 + kk * 128:m * 2048 + (kk + 1) * 128],
                         self.xg[:, kk * TT:(kk + 1) * TT], start=(kk == 0), stop=(kk == 15),
                         tag="l1qk")
                dch = dst[:, m * TT:(m + 1) * TT]
                b.tt(self.rot[0], ps[0:64, 0:TT], self.cosb, ALU.mult, tag="l1r1")
                b.tt(self.rot[1], ps[64:128, 0:TT], self.sinb, ALU.mult, tag="l1r2")
                b.tt(self.rot[2], ps[64:128, 0:TT], self.cosb, ALU.mult, tag="l1r4")
                b.tt(self.rot[3], ps[0:64, 0:TT], self.sinb, ALU.mult, tag="l1r5")
                b.tt(dch[0:64, :], self.rot[0], self.rot[1], ALU.add, tag="l1r3")
                b.tt(dch[64:128, :], self.rot[2], self.rot[3], ALU.subtract, tag="l1r6")
        for tg in range(4):
            ps = b.bank()
            for kk in range(16):
                b.mm(ps[:, 0:TT], self.xg[:, kk * TT + tg * 128:kk * TT + (tg + 1) * 128],
                     self.wv1s[:, kk * 512:(kk + 1) * 512], start=(kk == 0), stop=(kk == 15),
                     tag="l1v")
            b.copy(self.vb[:, tg * TT:(tg + 1) * TT], ps[:, 0:TT])
        for cj in range(4):
            for h in range(4):
                qf = self.qb[:, h * TT + cj * 128:h * TT + (cj + 1) * 128]
                kf = self.kb[:, h * TT + cj * 128:h * TT + (cj + 1) * 128]
                vtm = self.vb[:, cj * TT + h * 128:cj * TT + (h + 1) * 128]
                kv = self.kv[h]
                b.tt(self.qd_sc, qf, self.qdecs[:, h * 128:(h + 1) * 128], ALU.mult, tag="l1qd")
                psS = b.bank()
                b.mm(psS[:, 0:128], kf, qf, tag="l1sc")
                b.tt(self.sT_sc, psS[:, 0:128], self.dmks[:, h * 128:(h + 1) * 128],
                     ALU.mult, tag="l1dm")
                pso = b.bank()
                b.mm(pso[:, 0:128], self.sT_sc, vtm, start=True, stop=False, tag="l1intra")
                b.mm(pso[:, 0:128], self.qd_sc, kv, start=False, stop=True, tag="l1inter")
                b.copy(self.otm_sc, pso[:, 0:128])
                psT = b.bank()
                b.trans(psT[:, 0:128], kf, self.i128, tag="l1kT")
                b.ts(self.ktm_sc, psT[:, 0:128], self.kdecs[:, h:h + 1], ALU.mult, tag="l1kd")
                psU = b.bank()
                b.mm(psU[:, 0:128], self.ktm_sc, vtm, tag="l1kv")
                b.stt(kv, kv, self.rcs[:, h:h + 1], psU[:, 0:128], ALU.mult, ALU.add, tag="l1kvu")
                psF = b.bank()
                b.trans(psF[:, 0:128], self.otm_sc, self.i128, tag="l1oT")
                b.copy(self.o3[:, h * TT + cj * 128:h * TT + (cj + 1) * 128],
                       psF[:, 0:128], tag="l1oev")
        cin3 = self.cc3in[t // 4]
        cx3 = (t % 4) * TT
        for h in range(4):
            b.dma("gpsimd", cin3[h * 128:(h + 1) * 128, cx3:cx3 + TT],
                  self.o3[:, h * TT:(h + 1) * TT], tag="l1out")
        if "d_l1qkv" in self.dmp:
            for h in range(4):
                b.dma("gpsimd", self.dmp["d_l1qkv"][h * 128:(h + 1) * 128, c0:c0 + TT],
                      self.qb[:, h * TT:(h + 1) * TT])
                b.dma("gpsimd", self.dmp["d_l1qkv"][512 + h * 128:512 + (h + 1) * 128, c0:c0 + TT],
                      self.kb[:, h * TT:(h + 1) * TT])
            for tg in range(4):
                b.dma("gpsimd", self.dmp["d_l1qkv"][1024 + tg * 128:1024 + (tg + 1) * 128, c0:c0 + TT],
                      self.vb[:, tg * TT:(tg + 1) * TT])

    def p7(self):
        b = self.b
        gbuf = b.sb("p7_gb", [128, 16 * TT])
        nrm = b.sb("p7_n", [128, 16 * TT])
        rbuf = b.sb("p7_rb", [128, 16 * TT])
        acc = b.sb("p7_a", [128, 16 * TT])
        wsl = [b.sb(f"p7_w{i}", [128, 2048]) for i in range(2)]
        sq = b.sb("p7_sq", [128, TT])
        stt_ = b.sb("p7_st", [128, 8])
        rrow = b.sb("p7_rrow", [1, TT])
        rbc = b.sb("p7_rbc", [128, TT])
        epsr = b.sb("p7_eps", [128, 1])
        b.memset(epsr, LN_EPS)
        nh = b.sb("p7_nh", [128, 1])
        b.memset(nh, -0.5)
        rmsw_sb = b.sb("p7_rw", [128, 16])
        b.dma("sync", rmsw_sb, self.rmsw)
        kit = self.ln_kit("p7k")
        for tt2 in range(2):
            for k0 in range(0, 16, 4):
                idx = k0 * 2 + tt2
                dst3 = gbuf[:, k0 * TT:(k0 + 4) * TT].rearrange("p (k c) -> p k c", k=4)
                src3 = self.cc3out[0, k0 * 128:(k0 + 4) * 128, 0:TT].rearrange(
                    "(k p) c -> p k c", p=128)
                def ldfn(e, idx=idx, dst=dst3, s3=src3):
                    v = e.value_load(self.toksb[0:1, idx:idx + 1])
                    import dataclasses as _dc
                    return e.dma_start(out=dst, in_=_dc.replace(s3, offset=v))
                b.s.add("sync", ldfn, ins=[self.toksb, self.cc3out],
                        outs=[gbuf], dma=True, tag="p7g")
            bs = [b.bank() for _ in range(4)]
            for kk in range(16):
                ch = gbuf[:, kk * TT:(kk + 1) * TT]
                b.act(sq, ch, AF.Square, tag="p7sq")
                for tg in range(4):
                    b.mm(bs[tg][:, 0:1], sq[:, tg * 128:(tg + 1) * 128], self.ones_col,
                         start=(kk == 0), stop=(kk == 15), tag="p7s")
            for tg in range(4):
                c = stt_[:, tg * 2:tg * 2 + 2]
                b.ts(c[:, 0:1], bs[tg][:, 0:1], 1.0 / DM, ALU.mult)
            for tg in range(4):
                c = stt_[:, tg * 2:tg * 2 + 2]
                b.act(c[:, 0:1], c[:, 0:1], AF.Ln, bias=epsr)
                b.act(c[:, 0:1], c[:, 0:1], AF.Exp, scale=nh)
                pst = b.bank()
                b.trans(pst[:1, 0:128], c[:, 0:1], self.i128, tag="p7T")
                b.copy(rrow[:, tg * 128:(tg + 1) * 128], pst[:1, 0:128])
            psb = b.bank()
            b.mm(psb[:, 0:TT], self.ones_row, rrow, tag="p7bc")
            b.copy(rbc, psb[:, 0:TT])
            for kk in range(16):
                b.tt(nrm[:, kk * TT:(kk + 1) * TT], gbuf[:, kk * TT:(kk + 1) * TT],
                     rbc, ALU.mult, tag="p7n1")
            for kk in range(16):
                b.ts(nrm[:, kk * TT:(kk + 1) * TT], nrm[:, kk * TT:(kk + 1) * TT],
                     rmsw_sb[:, kk:kk + 1], ALU.mult, tag="p7n2")
            for kk in range(16):
                b.dma("sync", rbuf[:, kk * TT:(kk + 1) * TT],
                      self.cc2in[tt2][kk * 128:(kk + 1) * 128, 0:TT], tag="p7r")
            for m in range(16):
                w = wsl[m % 2]
                b.dma(b.wq(w.tensor.name), w, self.wo1[m], tag="p7w")
                ps = b.bank()
                for kk in range(16):
                    b.mm(ps[:, 0:TT], w[:, kk * 128:(kk + 1) * 128],
                         nrm[:, kk * TT:(kk + 1) * TT], start=(kk == 0), stop=(kk == 15))
                b.tt(acc[:, m * TT:(m + 1) * TT], ps[:, 0:TT],
                     rbuf[:, m * TT:(m + 1) * TT], ALU.add, tag="p7res")
            self.ln_tile(kit, acc, acc, tag="p7ln")
            for m in range(16):
                b.dma("gpsimd", self.x3d[m * 128:(m + 1) * 128, tt2 * TT:(tt2 + 1) * TT],
                      acc[:, m * TT:(m + 1) * TT], tag="p7o")
            if "d_x3" in self.dmp:
                for m in range(16):
                    b.dma("gpsimd", self.dmp["d_x3"][m * 128:(m + 1) * 128, tt2 * TT:(tt2 + 1) * TT],
                          acc[:, m * TT:(m + 1) * TT])
        b.free_prefix("p7_", "p7k_")

    def ffn1(self):
        b = self.b
        x3b = b.sb("f1_x3", [128, 16 * TT])
        accb = b.sb("f1_acc", [128, 16 * TT])
        hbuf = b.sb("f1_h", [128, 22 * TT])
        gsl = [b.sb(f"f1_gw{i}", [128, 2048]) for i in range(2)]
        uslb = [b.sb(f"f1_uw{i}", [128, 2048]) for i in range(2)]
        dslb = [b.sb(f"f1_dw{i}", [128, 2816]) for i in range(2)]
        s1g_sb = b.sb("f1_sg", [128, 44])
        s1u_sb = b.sb("f1_su", [128, 44])
        s1d_sb = b.sb("f1_sd", [128, 16])
        b.dma("sync", s1g_sb, self.s1g)
        b.dma("sync", s1u_sb, self.s1u)
        b.dma("sync", s1d_sb, self.s1d)
        tgu = b.sb("f1_tg", [128, TT])
        kit = self.ln_kit("f1k")
        for tt2 in range(2):
            for kk in range(16):
                b.dma("sync", x3b[:, kk * TT:(kk + 1) * TT],
                      self.x3d[kk * 128:(kk + 1) * 128, tt2 * TT:(tt2 + 1) * TT], tag="f1x")
            for p in range(2):
                nmi = 22
                for mi in range(nmi):
                    m = p * 22 + mi
                    wg = gsl[mi % 2]
                    wu = uslb[mi % 2]
                    b.dma(b.wq(wg.tensor.name), wg, self.w1g[m], tag="f1gw")
                    b.dma(b.wq(wu.tensor.name), wu, self.w1u[m], tag="f1uw")
                    psg = b.bank()
                    for kk in range(16):
                        b.mm(psg[:, 0:TT], wg[:, kk * 128:(kk + 1) * 128],
                             x3b[:, kk * TT:(kk + 1) * TT], start=(kk == 0), stop=(kk == 15))
                    psu = b.bank()
                    for kk in range(16):
                        b.mm(psu[:, 0:TT], wu[:, kk * 128:(kk + 1) * 128],
                             x3b[:, kk * TT:(kk + 1) * TT], start=(kk == 0), stop=(kk == 15))
                    hch = hbuf[:, mi * TT:(mi + 1) * TT]
                    b.act(hch, psg[:, 0:TT], AF.Silu, scale=s1g_sb[:, m:m + 1], tag="f1si")
                    b.ts(tgu, psu[:, 0:TT], s1u_sb[:, m:m + 1], ALU.mult, tag="f1us")
                    b.tt(hch, hch, tgu, ALU.mult, tag="f1h")
                for mo in range(16):
                    w = dslb[mo % 2]
                    b.dma(b.wq(w.tensor.name), w, self.w1d[mo, p], tag="f1dw")
                    ps = b.bank()
                    for kk in range(22):
                        b.mm(ps[:, 0:TT], w[:, kk * 128:(kk + 1) * 128],
                             hbuf[:, kk * TT:(kk + 1) * TT], start=(kk == 0), stop=(kk == 21))
                    och = accb[:, mo * TT:(mo + 1) * TT]
                    if p == 0:
                        b.copy(och, ps[:, 0:TT], tag="f1d0")
                    else:
                        b.tt(och, och, ps[:, 0:TT], ALU.add, tag="f1d1")
            for kk in range(16):
                och = accb[:, kk * TT:(kk + 1) * TT]
                b.stt(och, och, s1d_sb[:, kk:kk + 1], x3b[:, kk * TT:(kk + 1) * TT],
                      ALU.mult, ALU.add, tag="f1res")
            self.ln_tile(kit, accb, accb, tag="f1ln")
            for m in range(16):
                b.dma("gpsimd", self.yT[m * 128:(m + 1) * 128, tt2 * TT:(tt2 + 1) * TT],
                      accb[:, m * TT:(m + 1) * TT], tag="f1o")
        b.free_prefix("f1_", "f1k_")

    def build(self, l0_only_tiles=None):
        b = self.b
        self.setup_consts()
        self.setup_l0()
        if l0_only_tiles is not None:
            for t in range(l0_only_tiles):
                self.l0_tile(t)
            return b.s.resolve_and_emit(self.nc)
        for t in range(NT0):
            self.l0_tile(t)
            if t == 3:
                self.b.cc("AllGather", self.cc1in[0][:], self.cc1out[0], GROUPS4, tag="E1a")
        for ap in self.l2ps:
            b.bank_release(ap)
        b.free_prefix("xt", "wqz", "cvin", "zgw", "Sst", "gb_", "negcol4", "onecol4", "normw_sb",
                      "sc128", "bi128", "one11", "e6_11", "q_rs", "q_c1", "q_c2",
                      "colq_", "eg_bc", "u_", "inv128c", "oh64", "ohc", "nohc", "oh128", "sc128_4", "bi128_4", "one4", "e6_4", "tsave",
                      "epsc", "nhalf64", "mix", "conv_sc0", "conv_sc1", "conv_sc2", "conv_sc3", "sq_sc", "wba_sb", "convw_sb",
                      "dtb_sb", "alog_sb", "negexpA")
        self.b.cc("AllGather", self.cc1in[1][:], self.cc1out[1], GROUPS4, tag="E1b")
        b.s.fence()
        self.p3()
        b.s.fence()
        if "d_x1" in self.dmp:
            b.dma("gpsimd", self.dmp["d_x1"], self.x1d)
            b.s.fence()
        self.ffn0()
        b.s.fence()
        self.b.cc("AllGather", self.cc2in[1][:], self.cc2out[1][:], GROUPS4, tag="E2b")
        self.setup_l1()
        for t in range(NT0):
            self.l1_tile(t)
            if t == 3:
                self.b.cc("AllGather", self.cc3in[0][:], self.cc3out[0], GROUPS4, tag="E3a")
        self.b.cc("AllGather", self.cc3in[1][:], self.cc3out[1], GROUPS4, tag="E3b")
        b.free_prefix("l1_")
        b.s.fence()
        self.p7()
        b.s.fence()
        self.ffn1()
        return b.s.resolve_and_emit(self.nc)


def _slab(w):
    K, M = w.shape
    nK, nM = K // 128, M // 128
    return np.ascontiguousarray(
        w.reshape(nK, 128, nM, 128).transpose(2, 1, 0, 3).reshape(nM, 128, nK * 128))


def _build_slopes(n):
    def pow2(nn):
        start = 2.0 ** (-2.0 ** (-(math.log2(nn) - 3)))
        return [start * start ** i for i in range(nn)]
    if math.log2(n).is_integer():
        s = pow2(n)
    else:
        closest = 2 ** math.floor(math.log2(n))
        s = pow2(closest) + pow2(2 * closest)[0::2][:n - closest]
    return np.asarray(s, dtype=np.float64)


def make_in_maps(inputs):
    f32 = np.float32
    x = np.asarray(inputs["L0_attn_hidden_states"], f32)
    Wqkvz = np.asarray(inputs["L0_attn_W_qkvz"], f32)
    Wba = np.asarray(inputs["L0_attn_W_ba"], f32)
    convw = np.asarray(inputs["L0_attn_conv_weight"], f32)
    dtb = np.asarray(inputs["L0_attn_dt_bias"], f32)
    alog = np.asarray(inputs["L0_attn_A_log"], f32)
    normw = np.asarray(inputs["L0_attn_norm_weight"], f32)
    Wout = np.asarray(inputs["L0_attn_W_out"], f32)
    gatew = np.asarray(inputs["L0_ffn_gate_w"], f32)
    upw = np.asarray(inputs["L0_ffn_up_w"], f32)
    downw = np.asarray(inputs["L0_ffn_down_w"], f32)
    expw = np.asarray(inputs["L0_ffn_expert_weight"], f32)
    lng = np.asarray(inputs["L0_ffn_ln_gamma"], f32)
    lnb = np.asarray(inputs["L0_ffn_ln_beta"], f32)
    Wq1 = np.asarray(inputs["L1_attn_Wq"], f32)
    Wk1 = np.asarray(inputs["L1_attn_Wk"], f32)
    Wv1 = np.asarray(inputs["L1_attn_Wv"], f32)
    Wo1 = np.asarray(inputs["L1_attn_Wo"], f32)
    rmsw = np.asarray(inputs["L1_attn_rms_w"], f32)
    g1 = np.asarray(inputs["L1_ffn_gate_w"], f32)
    u1 = np.asarray(inputs["L1_ffn_up_w"], f32)
    d1 = np.asarray(inputs["L1_ffn_down_w"], f32)
    s1g = np.asarray(inputs["L1_ffn_gate_s"], f32)
    s1u = np.asarray(inputs["L1_ffn_up_s"], f32)
    s1d = np.asarray(inputs["L1_ffn_down_s"], f32)

    wout_s = _slab(Wout)
    upw_s = _slab(upw)
    dw = _slab(downw)
    downw_s = np.ascontiguousarray(dw.reshape(16, 128, 4, 2048).transpose(0, 2, 1, 3))
    expw_s = _slab(expw)
    gatew_s = np.ascontiguousarray(gatew.reshape(16, 128, 8))
    lngb = np.concatenate([lng.reshape(16, 128).T, lnb.reshape(16, 128).T], axis=1)
    lngb = np.ascontiguousarray(lngb)
    wo1_s = _slab(Wo1)
    rmsw_h = np.ascontiguousarray(rmsw.reshape(16, 128).T)
    w1g_s = _slab(np.ascontiguousarray(g1.T))
    w1u_s = _slab(np.ascontiguousarray(u1.T))
    dT = _slab(np.ascontiguousarray(d1.T))
    w1d_s = np.ascontiguousarray(dT.reshape(16, 128, 2, 2816).transpose(0, 2, 1, 3))
    s1g_h = np.ascontiguousarray(s1g.reshape(44, 128).T)
    s1u_h = np.ascontiguousarray(s1u.reshape(44, 128).T)
    s1d_h = np.ascontiguousarray(s1d.reshape(16, 128).T)
    idx = np.arange(64)
    maskS = np.where(idx[None, :] > idx[:, None], 0.0, -3e38).astype(f32)
    maskI = np.where(idx[None, :] >= idx[:, None], 0.0, -3e38).astype(f32)
    maskS = np.tile(maskS, (2, 1))
    maskI = np.tile(maskI, (2, 1))
    i128 = np.eye(128, dtype=f32)
    freq = (1.0 / 1024.0) ** np.linspace(0.0, 1.0, 32, dtype=f32)
    freq = np.concatenate([freq, np.zeros(32, dtype=f32)])
    tpos = np.arange(S, dtype=np.float64)
    theta = tpos[:, None] * freq[None, :].astype(np.float64)
    rotcos = np.ascontiguousarray(np.cos(theta).T.astype(f32))
    rotsin = np.ascontiguousarray(np.sin(theta).T.astype(f32))
    slopes = _build_slopes(16)
    rr = np.exp(-slopes)
    ii = np.arange(128, dtype=np.float64)
    xT = [np.ascontiguousarray(x[bb].T) for bb in range(2)]
    oh128c = np.zeros((128, 16), f32)
    for h in range(4):
        oh128c[:, 4 * h + h] = 0.0
        oh128c[:, 4 * h + h] = 1.0
    oh64 = np.zeros((4, 4 * 64), f32)
    ohc = np.zeros((4, 4 * 128), f32)
    for h in range(4):
        oh64[h, h * 64:(h + 1) * 64] = 1.0
        ohc[h, h * 128:(h + 1) * 128] = 1.0

    in_maps = []
    for c in range(NCORE):
        bb, q = c // 4, c % 4
        hg = [4 * q + j for j in range(4)]
        tok0 = 1024 * q
        toktab = np.zeros((1, 32), np.int32)
        half_w = S // 2
        for kk in range(16):
            for tt2 in range(2):
                pos = tok0 + tt2 * TT
                toktab[0, kk * 2 + tt2] = ((pos // half_w) * DM * half_w
                                           + kk * 128 * half_w + pos % half_w)
        selq = Wqkvz[:, 2048 * q:2048 * q + 2048]
        wba_sel = np.concatenate([Wba[:, [2 * h for h in hg]],
                                  Wba[:, [2 * h + 1 for h in hg]]], axis=1)
        convsel = np.stack([convw[tau * 2048 + hh * 128: tau * 2048 + (hh + 1) * 128, :]
                            for j, hh in enumerate(hg) for tau in range(3)])
        qdec = np.stack([np.tile((rr[hh] ** (ii + 1.0)).astype(f32)[None, :], (128, 1))
                         for hh in hg])
        dmT = []
        for hh in hg:
            diff = ii[:, None] - ii[None, :]
            dm = np.where(diff >= 0, rr[hh] ** np.where(diff >= 0, diff, 0.0), 0.0)
            dmT.append(dm.T.astype(f32))
        kdec = np.stack([(rr[hh] ** (127.0 - ii)).astype(f32) for hh in hg], axis=1)
        rc = np.stack([np.full(128, rr[hh] ** 128.0, dtype=f32) for hh in hg], axis=1)
        m = {
            "xT": xT[bb],
            "xresT": np.ascontiguousarray(xT[bb][:, tok0:tok0 + TPC]),
            "tokoff": toktab,
            "wqkvz": _slab(selq),
            "wba": np.ascontiguousarray(wba_sel.reshape(16, 128, 8)),
            "convw": np.ascontiguousarray(convsel),
            "dtb": np.asarray([[dtb[hh]] for hh in hg], f32),
            "alog": np.asarray([[alog[hh]] for hh in hg], f32),
            "normw": np.ascontiguousarray(normw[:, None]),
            "wout": wout_s, "gatew": gatew_s, "upw": upw_s, "downw": downw_s,
            "expw": expw_s, "lngb": lngb,
            "wq1": _slab(Wq1[:, 512 * q:512 * q + 512]),
            "wk1": _slab(Wk1[:, 512 * q:512 * q + 512]),
            "wv1cols": np.ascontiguousarray(
                Wv1[:, 512 * q:512 * q + 512].reshape(16, 128, 512)),
            "wo1": wo1_s, "rmsw": rmsw_h,
            "w1g": w1g_s, "w1u": w1u_s, "w1d": w1d_s,
            "s1g": s1g_h, "s1u": s1u_h, "s1d": s1d_h,
            "c_maskS": maskS, "c_maskI": maskI, "c_i128": i128,
            "c_rotcos": rotcos, "c_rotsin": rotsin,
            "c_qdec": qdec, "c_dmaskT": np.stack(dmT),
            "c_kdec": np.ascontiguousarray(kdec), "c_rC": np.ascontiguousarray(rc),
            "c_oh64": oh64, "c_ohc": ohc, "c_nohc": -ohc, "c_oh128": oh128c,
        }
        in_maps.append(m)
    return in_maps


_NC_CACHE = {}


def build_nc(dump=()):
    key = tuple(sorted(dump))
    if key not in _NC_CACHE:
        nc = bass.Bass("TRN2", target_bir_lowering=False)
        mk = MK(nc, dump=dump)
        mk.build()
        _NC_CACHE[key] = nc
    return _NC_CACHE[key]


def kernel(**inputs):
    in_maps = make_in_maps(inputs)
    nc = build_nc()
    res = run_bass_kernel_spmd(nc, in_maps, core_ids=list(range(NCORE)))
    y = np.zeros((B, S, DM), np.float32)
    for c in range(NCORE):
        bb, q = c // 4, c % 4
        y[bb, 1024 * q:1024 * (q + 1), :] = res.results[c]["yT"].T
    return y


# version 07
# speedup vs baseline: 1.3346x; 1.3346x over previous
import math
import numpy as np
import concourse.bass as bass
import concourse.mybir as mybir
from concourse.bass_utils import run_bass_kernel_spmd

F32 = mybir.dt.float32
AF = mybir.ActivationFunctionType
ALU = mybir.AluOpType

B, S, DM = 2, 4096, 2048
HK = 16
DK = DV = 128
KC = 4
CH = 64
LCH = 128
NCORE = 8
HPC = 4
TPC = 1024
TT = 512
INTER0 = 8192
INTER1 = 5632
NEXP = 8

ENGINES = ("tensor", "vector", "scalar", "gpsimd", "sync")


class Op:
    __slots__ = ("idx", "engine", "fn", "ins", "outs", "dma", "token", "waits",
                 "inc_needed", "tag", "_deps", "pre_drain")

    def __init__(self, idx, engine, fn, ins, outs, dma, tag):
        self.idx = idx
        self.engine = engine
        self.fn = fn
        self.ins = ins
        self.outs = outs
        self.dma = dma
        self.token = None
        self.waits = []
        self.inc_needed = False
        self.pre_drain = False
        self.tag = tag


class Sched:

    def __init__(self):
        self.ops = []
        self.last_writer = {}
        self.readers = {}
        self.fence_set = []

    def fence(self):
        last = {}
        for op in self.ops:
            if op.dma:
                nm = op.outs[0][0] if isinstance(op.outs[0], tuple) else op.outs[0]
                key = ("dma", nm)
            else:
                key = ("c", op.engine)
            last[key] = op
        self.fence_set = list(last.values())

    def add(self, engine, fn, ins=(), outs=(), dma=False, tag="", force_inc=False):
        op = Op(len(self.ops), engine, fn,
                [k for x in ins for k in self._r(x)],
                [k for x in outs for k in self._r(x)], dma, tag)
        op.inc_needed = force_inc
        deps = []
        for r in op.ins:
            w = self.last_writer.get(r)
            if w is not None:
                deps.append(w)
        for r in op.outs:
            w = self.last_writer.get(r)
            if w is not None:
                deps.append(w)
            for rd in self.readers.get(r, ()):
                if rd is not op:
                    deps.append(rd)
        for r in op.ins:
            self.readers.setdefault(r, []).append(op)
        for r in op.outs:
            self.last_writer[r] = op
            self.readers[r] = []
        deps.extend(self.fence_set)
        op._deps = deps
        self.ops.append(op)
        return op

    @staticmethod
    def _r(x):
        if isinstance(x, str):
            return [x]
        t = getattr(x, "tensor", None)
        if t is None:
            return [x.name]
        try:
            sbuf = str(x.space).endswith("SBUF")
        except Exception:
            sbuf = False
        if not sbuf:
            return [t.name]
        try:
            base = x.base_partition()
            npart = x.shape[0]
        except Exception:
            return [t.name]
        keys = []
        if base < 64:
            keys.append((t.name, 0))
        if base + npart > 64:
            keys.append((t.name, 1))
        return keys

    def resolve_and_emit(self, nc):
        sems = {}
        for e in ENGINES:
            sems[("c", e)] = nc.alloc_semaphore(name=f"sc_{e}")
        counts = {}

        for op in self.ops:
            for d in op._deps:
                if d.dma:
                    continue
                if d.engine != op.engine:
                    d.inc_needed = True
        for op in self.ops:
            if op.dma:
                nm = op.outs[0][0] if isinstance(op.outs[0], tuple) else op.outs[0]
                key = ("d", nm)
                if key not in sems:
                    sems[key] = nc.alloc_semaphore(name=f"sd_{nm}"[:30])
                    counts[key] = 0
                counts[key] += 16
                op.token = (key, counts[key])
            else:
                key = ("c", op.engine)
                if op.inc_needed:
                    counts[key] = counts.get(key, 0) + 1
                op.token = (key, counts.get(key, 0))

        last_drain = {}
        running = {}
        for op in self.ops:
            need = {}
            for d in op._deps:
                if d.engine == op.engine and not d.dma:
                    if (not op.dma and op.engine != "tensor"
                            and d.idx >= last_drain.get(op.engine, -1)):
                        op.pre_drain = True
                    continue
                if (d.dma and op.dma and d.token[0] == op.token[0]
                        and op.engine != "gpsimd" and d.engine != "gpsimd"):
                    continue
                k, v = d.token
                if d.dma:
                    v = running.get(k, 0)
                if v > 0 and need.get(k, 0) < v:
                    need[k] = v
            if op.pre_drain:
                last_drain[op.engine] = op.idx
            op.waits = sorted(need.items())
            if op.dma:
                running[op.token[0]] = op.token[1]

        hi = {}
        for op in self.ops:
            w2 = []
            for k, v in op.waits:
                kk = (op.engine, k)
                if hi.get(kk, -1) >= v:
                    continue
                hi[kk] = v
                w2.append((k, v))
            op.waits = w2

        by_eng = {e: [] for e in ENGINES}
        for op in self.ops:
            by_eng[op.engine].append(op)

        dma_finals = [(k, v) for k, v in counts.items() if k[0] == "d"]

        with nc.Block() as block:
            def mk(e):
                def run(eng):
                    for op in by_eng[e]:
                        if op.pre_drain:
                            eng.drain()
                        for k, v in op.waits:
                            eng.wait_ge(sems[k], v)
                        ins = op.fn(eng)
                        if op.dma:
                            ins.then_inc(sems[op.token[0]], 16)
                        elif op.inc_needed:
                            ins.then_inc(sems[("c", e)], 1)
                    if e == "gpsimd":
                        for k, v in dma_finals:
                            eng.wait_ge(sems[k], v)
                return run
            for e in ENGINES:
                if by_eng[e] or e == "gpsimd":
                    getattr(block, e)(mk(e))
        return sems, counts

def repack_lhsT(w):
    K, M = w.shape
    nK, nM = K // 128, M // 128
    t = w.reshape(nK, 128, nM, 128)
    return np.ascontiguousarray(t.transpose(2, 0, 1, 3))


class Bld:

    def __init__(self, nc):
        self.nc = nc
        self.s = Sched()
        self.t = {}
        self.guards = {}
        self._wq_map = {}
        self.uid = 0
        self.banks = [nc.alloc_psum_tensor(f"bank{i}", [128, 512], F32).ap()
                      for i in range(8)]
        self.bank_i = 0
        self.held = set()

    def bank(self):
        for _ in range(16):
            ap = self.banks[self.bank_i % 8]
            self.bank_i += 1
            if ap.tensor.name not in self.held:
                return ap
        raise RuntimeError("all banks held")

    def bank_hold(self):
        ap = self.bank()
        self.held.add(ap.tensor.name)
        return ap

    def bank_release(self, ap):
        self.held.discard(ap.tensor.name)

    _WQ = ("sync", "scalar", "gpsimd")

    def wq(self, tensor_name=None):
        if tensor_name is not None:
            if tensor_name not in self._wq_map:
                self._wq_map[tensor_name] = self._WQ[len(self._wq_map) % 3]
            return self._wq_map[tensor_name]
        self.uid += 1
        return self._WQ[self.uid % 3]

    def sb(self, name, shape, dtype=F32):
        g = self.nc.sbuf_tensor(name, list(shape), dtype)
        h = g.__enter__()
        self.guards[name] = (self.uid, g)
        self.uid += 1
        ap = h.ap()
        self.t[name] = ap
        return ap

    def free(self, *names):
        for n in sorted(names, key=lambda n: -self.guards[n][0]):
            _, g = self.guards.pop(n)
            g.__exit__(None, None, None)

    def free_prefix(self, *prefixes):
        self.free(*[n for n in self.guards if any(n.startswith(p) for p in prefixes)])

    def dram(self, name, shape, dtype=F32, kind="Internal"):
        ap = self.nc.dram_tensor(name, list(shape), dtype, kind=kind).ap()
        self.t[name] = ap
        return ap

    def inp(self, name, shape, dtype=F32):
        return self.dram(name, shape, dtype, kind="ExternalInput")

    def out(self, name, shape, dtype=F32):
        return self.dram(name, shape, dtype, kind="ExternalOutput")

    def dma(self, q, out, in_, tag=""):
        self.s.add(q, lambda e, o=out, i=in_: e.dma_start(out=o, in_=i),
                   ins=[in_], outs=[out], dma=True, tag=tag)

    def mm(self, out, lhsT, rhs, start=True, stop=True, tag="", tp=None):
        self.s.add("tensor",
                   lambda e, o=out, l=lhsT, r=rhs, st=start, sp=stop, t=tp:
                       e.matmul(o, l, r, start=st, stop=sp, tile_position=t),
                   ins=[lhsT, rhs], outs=[out], tag=tag)

    def trans(self, out_psum, in_sb, ident, tag="", tp=None):
        self.s.add("tensor",
                   lambda e, o=out_psum, i=in_sb, d=ident, t=tp: e.transpose(o, i, d, tile_position=t),
                   ins=[in_sb, ident], outs=[out_psum], tag=tag)

    def act(self, out, in_, func, bias=None, scale=None, accum_out=None, eng="scalar", tag=""):
        kw = {}
        if bias is not None:
            kw["bias"] = bias
        if scale is not None:
            kw["scale"] = scale
        if accum_out is not None:
            kw["accum_out"] = accum_out
        ins = [in_] + [x for x in (bias, scale) if hasattr(x, "tensor")]
        outs = [out] + ([accum_out] if accum_out is not None else [])
        self.s.add(eng, lambda e, o=out, i=in_, f=func, k=dict(kw): e.activation(out=o, in_=i, func=f, **k),
                   ins=ins, outs=outs, tag=tag)

    def tt(self, out, a, b, op, eng="vector", tag=""):
        self.s.add(eng, lambda e, o=out, x=a, y=b, p=op: e.tensor_tensor(out=o, in0=x, in1=y, op=p),
                   ins=[a, b], outs=[out], tag=tag)

    def ts(self, out, a, s1, op0, s2=None, op1=None, eng="vector", tag=""):
        ins = [a] + [x for x in (s1, s2) if hasattr(x, "tensor")]
        if op1 is None:
            self.s.add(eng, lambda e, o=out, x=a, s=s1, p=op0:
                       e.tensor_single_scalar(out=o, in_=x, scalar=s, op=p),
                       ins=ins, outs=[out], tag=tag)
        else:
            self.s.add(eng, lambda e, o=out, x=a, sa=s1, sb=s2, p0=op0, p1=op1:
                       e.tensor_scalar(out=o, in0=x, scalar1=sa, scalar2=sb, op0=p0, op1=p1),
                       ins=ins, outs=[out], tag=tag)

    def stt(self, out, in0, scalar, in1, op0, op1, eng="vector", tag=""):
        ins = [in0, in1] + ([scalar] if hasattr(scalar, "tensor") else [])
        self.s.add(eng, lambda e, o=out, a=in0, s=scalar, b=in1, p0=op0, p1=op1:
                   e.scalar_tensor_tensor(out=o, in0=a, scalar=s, in1=b, op0=p0, op1=p1),
                   ins=ins, outs=[out], tag=tag)

    def copy(self, out, in_, eng="vector", tag=""):
        self.s.add(eng, lambda e, o=out, i=in_: e.tensor_copy(out=o, in_=i),
                   ins=[in_], outs=[out], tag=tag)

    def scopy(self, out, in_, tag=""):
        self.s.add("scalar", lambda e, o=out, i=in_: e.copy(out=o, in_=i),
                   ins=[in_], outs=[out], tag=tag)

    def memset(self, ap, val, eng="vector", tag=""):
        self.s.add(eng, lambda e, o=ap, v=val: e.memset(o, v), ins=[], outs=[ap], tag=tag)

    def scan(self, out, d0, d1, initial, op0, op1, tag=""):
        self.s.add("vector", lambda e, o=out, a=d0, b=d1, i=initial, p0=op0, p1=op1:
                   e.tensor_tensor_scan(out=o, data0=a, data1=b, initial=i, op0=p0, op1=p1),
                   ins=[d0, d1], outs=[out], tag=tag)

    def reduce(self, out, in_, op, eng="vector", tag=""):
        self.s.add(eng, lambda e, o=out, i=in_, p=op:
                   e.tensor_reduce(out=o, in_=i, axis=mybir.AxisListType.X, op=p),
                   ins=[in_], outs=[out], tag=tag)

    def cc(self, kind, ins_ap, outs_ap, groups, tag=""):
        self.s.add("gpsimd",
                   lambda e, k=kind, i=ins_ap, o=outs_ap, g=groups:
                   e.collective_compute(k, ALU.bypass, replica_groups=g, ins=[i], outs=[o]),
                   ins=[ins_ap], outs=[outs_ap], tag=tag, force_inc=True)


GROUPS4 = [[0, 1, 2, 3], [4, 5, 6, 7]]
NT0 = S // TT
LN_EPS = 1e-5


class MK:

    def __init__(self, nc, dump=()):
        self.nc = nc
        self.b = Bld(nc)
        self.dump = set(dump)
        b = self.b
        self.xT = b.inp("xT", [DM, S])
        self.xresT = b.inp("xresT", [DM, TPC])
        self.tokoff = b.inp("tokoff", [1, 32], mybir.dt.int32)
        self.wqkvz = b.inp("wqkvz", [16, 128, 2048])
        self.wba = b.inp("wba", [16, 128, 8])
        self.convw = b.inp("convw", [12, 128, 4])
        self.dtb = b.inp("dtb", [4, 1])
        self.alog = b.inp("alog", [4, 1])
        self.normw = b.inp("normw", [128, 1])
        self.wout = b.inp("wout", [16, 128, 2048])
        self.gatew = b.inp("gatew", [16, 128, 8])
        self.upw = b.inp("upw", [64, 128, 2048])
        self.downw = b.inp("downw", [16, 4, 128, 2048])
        self.expw = b.inp("expw", [16, 128, 2048])
        self.lngb = b.inp("lngb", [128, 32])
        self.wq1 = b.inp("wq1", [4, 128, 2048])
        self.wk1 = b.inp("wk1", [4, 128, 2048])
        self.wv1cols = b.inp("wv1cols", [16, 128, 512])
        self.wo1 = b.inp("wo1", [16, 128, 2048])
        self.rmsw = b.inp("rmsw", [128, 16])
        self.w1g = b.inp("w1g", [44, 128, 2048])
        self.w1u = b.inp("w1u", [44, 128, 2048])
        self.w1d = b.inp("w1d", [16, 2, 128, 2816])
        self.s1g = b.inp("s1g", [128, 44])
        self.s1u = b.inp("s1u", [128, 44])
        self.s1d = b.inp("s1d", [128, 16])
        self.c_maskS = b.inp("c_maskS", [128, 64])
        self.c_maskI = b.inp("c_maskI", [128, 64])
        self.c_i128 = b.inp("c_i128", [128, 128])
        self.c_rotcos = b.inp("c_rotcos", [64, S])
        self.c_rotsin = b.inp("c_rotsin", [64, S])
        self.c_qdec = b.inp("c_qdec", [4, 128, 128])
        self.c_dmaskT = b.inp("c_dmaskT", [4, 128, 128])
        self.c_kdec = b.inp("c_kdec", [128, 4])
        self.c_rC = b.inp("c_rC", [128, 4])
        self.c_oh64 = b.inp("c_oh64", [4, 4 * 64])
        self.c_ohc = b.inp("c_ohc", [4, 4 * 128])
        self.c_nohc = b.inp("c_nohc", [4, 4 * 128])
        self.c_oh128 = b.inp("c_oh128", [128, 16])
        self.cc1in = [b.dram("cc1inA", [512, S // 2]), b.dram("cc1inB", [512, S // 2])]
        self.cc1out = b.dram("cc1out", [2, DM, S // 2])
        self.x1d = b.dram("x1d", [DM, TPC])
        self.cc2in = [b.dram("cc2inA", [DM, TT]), b.dram("cc2inB", [DM, TT])]
        self.cc2out = [b.dram("cc2outA", [4 * DM, TT]), b.dram("cc2outB", [4 * DM, TT])]
        self.cc3in = [b.dram("cc3inA", [512, S // 2]), b.dram("cc3inB", [512, S // 2])]
        self.cc3out = b.dram("cc3out", [2, DM, S // 2])
        self.x3d = b.dram("x3d", [DM, TPC])
        self.yT = b.out("yT", [DM, TPC])
        self.dmp = {}
        for name, shape in [
            ("d_conv", [1536, S]), ("d_gates", [8, S]), ("d_mix", [512, S]),
            ("d_x1", [DM, TPC]), ("d_x2", [DM, TPC]), ("d_l1qkv", [3 * 512, S]),
            ("d_l1mix", [512, S]), ("d_x3", [DM, TPC]),
        ]:
            if name in self.dump:
                self.dmp[name] = b.out(name, shape)

    def setup_consts(self):
        b = self.b
        self.ones_col = b.sb("ones_col", [128, 1])
        b.memset(self.ones_col, 1.0)
        self.ones_row = b.sb("ones_row", [1, 128])
        b.memset(self.ones_row, 1.0)
        self.zero_11 = b.sb("zero_11", [1, 1])
        b.memset(self.zero_11, 0.0)
        self.half_neg = b.sb("half_neg", [128, 1])
        b.memset(self.half_neg, -0.5)
        self.neg_col = b.sb("neg_col", [128, 1])
        b.memset(self.neg_col, -1.0)
        self.eps_l2 = b.sb("eps_l2", [128, 1])
        b.memset(self.eps_l2, 128 * 1e-6)
        self.i128 = b.sb("i128", [128, 128])
        b.dma("sync", self.i128, self.c_i128)
        self.maskS = b.sb("maskS", [128, 64])
        b.dma("sync", self.maskS, self.c_maskS)
        self.maskI = b.sb("maskI", [128, 64])
        b.dma("sync", self.maskI, self.c_maskI)
        self.toksb = b.sb("toksb", [1, 32], mybir.dt.int32)
        b.dma("sync", self.toksb, self.tokoff)
        self.dtb_sb = b.sb("dtb_sb", [4, 1])
        b.dma("sync", self.dtb_sb, self.dtb)
        self.alog_sb = b.sb("alog_sb", [4, 1])
        b.dma("sync", self.alog_sb, self.alog)
        self.negexpA = b.sb("negexpA", [4, 1])
        b.act(self.negexpA, self.alog_sb, AF.Exp)
        b.ts(self.negexpA, self.negexpA, -1.0, ALU.mult)
        self.normw_sb = b.sb("normw_sb", [128, 1])
        b.dma("sync", self.normw_sb, self.normw)

    def setup_l0(self):
        b = self.b
        self.xt = [b.sb(f"xt{i}", [128, 16 * TT]) for i in range(2)]
        self.wqz_sl = [b.sb(f"wqz{i}", [128, 16 * 128]) for i in range(2)]
        self.wba_sb = b.sb("wba_sb", [128, 16 * 8])
        for k in range(16):
            b.dma("sync", self.wba_sb[:, k * 8:(k + 1) * 8], self.wba[k])
        self.convw_sb = b.sb("convw_sb", [128, 12 * 4])
        for c in range(12):
            b.dma("sync", self.convw_sb[:, c * 4:(c + 1) * 4], self.convw[c])
        self.cvin = [b.sb(f"cvin{c}", [128, TT + 3]) for c in range(12)]
        for c in range(12):
            b.memset(self.cvin[c][:, 0:3], 0.0)
        self.zgw = b.sb("zgw", [128, 4 * TT])
        self.Sst = [b.sb(f"Sst{h}", [128, 128]) for h in range(4)]
        for h in range(4):
            b.memset(self.Sst[h], 0.0)
        for nm in ["gb_b", "gb_a", "gb_lnb", "gb_beta", "gb_gc",
                   "gb_lnq", "gb_lnk", "gb_rowA", "gb_rowB",
                   "gb_rowC", "gb_t0", "gb_t1", "q_rs", "q_c1", "q_c2"]:
            setattr(self, nm, b.sb(nm, [4, TT]))
        self.oh64 = b.sb("oh64", [4, 4 * 64])
        b.dma("sync", self.oh64, self.c_oh64)
        self.ohc = b.sb("ohc", [4, 4 * 128])
        b.dma("sync", self.ohc, self.c_ohc)
        self.nohc = b.sb("nohc", [4, 4 * 128])
        b.dma("sync", self.nohc, self.c_nohc)
        self.oh128 = b.sb("oh128", [128, 16])
        b.dma("sync", self.oh128, self.c_oh128)
        self.sc128_4 = b.sb("sc128_4", [4, 1]); b.memset(self.sc128_4, 128.0)
        self.bi128_4 = b.sb("bi128_4", [4, 1]); b.memset(self.bi128_4, 128 * 1e-6)
        self.one4 = b.sb("one4", [4, 1]); b.memset(self.one4, 1.0)
        self.e6_4 = b.sb("e6_4", [4, 1]); b.memset(self.e6_4, 1e-6)
        self.negcol4 = b.sb("negcol4", [4, 1]); b.memset(self.negcol4, -1.0)
        self.onecol4 = b.sb("onecol4", [4, 1]); b.memset(self.onecol4, 1.0)
        self.sc128 = b.sb("sc128", [1, 1]); b.memset(self.sc128, 128.0)
        self.bi128 = b.sb("bi128", [1, 1]); b.memset(self.bi128, 128 * 1e-6)
        self.one11 = b.sb("one11", [1, 1]); b.memset(self.one11, 1.0)
        self.e6_11 = b.sb("e6_11", [1, 1]); b.memset(self.e6_11, 1e-6)
        self.colq = {q: b.sb(f"colq_{q}", [128, 16]) for q in ("rs", "c1", "c2", "be")}
        self.eg_bc = [b.sb(f"eg_bc{h}", [128, 8]) for h in range(4)]
        self.conv_sc = [b.sb(f"conv_sc{j}", [128, TT]) for j in range(4)]
        self.l2ps = [self.b.bank_hold(), self.b.bank_hold()]
        self.tsave = b.sb("tsave", [128, 36])
        self.sq_sc = b.sb("sq_sc", [128, TT])
        self.t = {}
        for nm, w in (("pwN", 384), ("pwP", 384), ("Dt", 64), ("Da", 64), ("sc64", 64),
                      ("TmT", 64), ("attnT", 64), ("vtm", 128), ("ktm", 128),
                      ("y0", 128), ("y1", 128), ("out", 128), ("sq", 128), ("vv", 3)):
            for s in range(2):
                self.t[f"u_{nm}{s}"] = b.sb(f"u_{nm}{s}", [128, w])
        self.inv128c = b.sb("inv128c", [128, 1]); b.memset(self.inv128c, 1.0 / 128.0)
        self.epsc = b.sb("epsc", [128, 1]); b.memset(self.epsc, 1e-6)
        self.nhalf64 = b.sb("nhalf64", [128, 1]); b.memset(self.nhalf64, -0.5)
        self.mix = b.sb("mix", [128, 4 * TT])

    def l0_tile(self, t):
        b = self.b
        xt = self.xt[t % 2]
        c0 = t * TT
        for k in range(16):
            b.dma("sync", xt[:, k * TT:(k + 1) * TT],
                  self.xT[k * 128:(k + 1) * 128, c0:c0 + TT], tag=f"xt{t}")
        psbb = b.bank()
        for k in range(16):
            b.mm(psbb[:4, 0:TT], self.wba_sb[:, k * 8:k * 8 + 4],
                 xt[:, k * TT:(k + 1) * TT], start=(k == 0), stop=(k == 15))
        psba = b.bank()
        for k in range(16):
            b.mm(psba[:4, 0:TT], self.wba_sb[:, k * 8 + 4:k * 8 + 8],
                 xt[:, k * TT:(k + 1) * TT], start=(k == 0), stop=(k == 15))
        b.copy(self.gb_b[:], psbb[0:4, 0:TT])
        b.copy(self.gb_a[:], psba[0:4, 0:TT])
        gb = self
        b.act(gb.gb_t0, gb.gb_b, AF.Abs)
        b.act(gb.gb_t0, gb.gb_t0, AF.Exp, scale=self.negcol4)
        b.act(gb.gb_t0, gb.gb_t0, AF.Ln, bias=self.onecol4)
        b.ts(gb.gb_t1, gb.gb_b, 0.0, ALU.min)
        b.tt(gb.gb_lnb, gb.gb_t1, gb.gb_t0, ALU.subtract)
        b.act(gb.gb_beta, gb.gb_lnb, AF.Exp)
        b.ts(gb.gb_t1, gb.gb_a, self.dtb_sb, ALU.add)
        b.act(gb.gb_t0, gb.gb_t1, AF.Abs)
        b.act(gb.gb_t0, gb.gb_t0, AF.Exp, scale=self.negcol4)
        b.act(gb.gb_t0, gb.gb_t0, AF.Ln, bias=self.onecol4)
        b.ts(gb.gb_t1, gb.gb_t1, 0.0, ALU.max)
        b.tt(gb.gb_t1, gb.gb_t1, gb.gb_t0, ALU.add)
        b.ts(gb.gb_t1, gb.gb_t1, self.negexpA, ALU.mult)
        b.scan(gb.gb_gc, gb.gb_t1, gb.gb_t1, 0.0, ALU.add, ALU.bypass)
        for h in range(4):
            pse = b.bank()
            b.mm(pse[:, 0:8], self.ohc[:, h * 128:(h + 1) * 128],
                 gb.gb_gc[:, 63::64], start=True, stop=False, tag="egl1")
            b.mm(pse[:, 1:8], self.nohc[:, h * 128:(h + 1) * 128],
                 gb.gb_gc[:, 63:TT - 64:64], start=False, stop=True, tag="egl2")
            b.act(self.eg_bc[h], pse[:, 0:8], AF.Exp, tag="egl3")
        cw = self.convw_sb
        for m in range(16):
            wsl = self.wqz_sl[m % 2]
            b.dma(b.wq(wsl.tensor.name), wsl, self.wqkvz[m], tag=f"wq{t}_{m}")
            ps = b.bank()
            for k in range(16):
                b.mm(ps[:, 0:TT], wsl[:, k * 128:(k + 1) * 128],
                     xt[:, k * TT:(k + 1) * TT], start=(k == 0), stop=(k == 15))
            h, tau = m // 4, m % 4
            if tau < 3:
                cc = 3 * h + tau
                if t > 0:
                    b.copy(self.cvin[cc][:, 0:3], self.tsave[:, cc * 3:cc * 3 + 3], tag="halo")
                b.copy(self.cvin[cc][:, 3:TT + 3], ps[:, 0:TT], tag="cvev")
                b.copy(self.tsave[:, cc * 3:cc * 3 + 3], self.cvin[cc][:, TT:TT + 3], tag="tsv")
            else:
                b.act(self.zgw[:, h * TT:(h + 1) * TT], ps[:, 0:TT], AF.Silu, tag="zsilu")
                b.ts(self.zgw[:, h * TT:(h + 1) * TT], self.zgw[:, h * TT:(h + 1) * TT],
                     self.normw_sb, ALU.mult, tag="zw")
                ccs = [3 * h + 0, 3 * h + 1, 3 * h + 2]
                for j, c in enumerate(ccs):
                    b.ts(self.conv_sc[j], self.cvin[c][:, 0:TT],
                         cw[:, c * 4 + 0:c * 4 + 1], ALU.mult, tag="cv0")
                for tap in (1, 2, 3):
                    for j, c in enumerate(ccs):
                        b.stt(self.conv_sc[j], self.cvin[c][:, tap:TT + tap],
                              cw[:, c * 4 + tap:c * 4 + tap + 1], self.conv_sc[j],
                              ALU.mult, ALU.add, tag=f"cv{tap}")
                for j, c in enumerate(ccs):
                    b.act(self.cvin[c][:, 3:TT + 3], self.conv_sc[j], AF.Silu, tag="cvsilu")
                for (tau2, blk, sc_, bi_) in ((0, gb.gb_lnq, self.sc128_4, self.bi128_4),
                                              (1, gb.gb_lnk, self.one4, self.e6_4)):
                    cc2 = 3 * h + tau2
                    b.act(self.sq_sc, self.cvin[cc2][:, 3:TT + 3], AF.Square, tag="l2sq")
                    psn = self.l2ps[tau2]
                    b.mm(psn[0:4, 0:TT], self.oh128[:, 4 * h:4 * h + 4], self.sq_sc,
                         start=(h == 0), stop=(h == 3), tag="l2mm")
                    if h == 3:
                        b.act(blk, psn[0:4, 0:TT], AF.Ln, scale=sc_, bias=bi_, tag="l2ln")
        b.ts(gb.gb_lnq, gb.gb_lnq, -0.5, ALU.mult)
        b.ts(gb.gb_lnk, gb.gb_lnk, -0.5, ALU.mult)
        b.stt(gb.gb_rowA, gb.gb_gc, -1.0, gb.gb_lnk, ALU.mult, ALU.add)
        b.tt(gb.gb_rowB, gb.gb_gc, gb.gb_lnq, ALU.add)
        b.tt(gb.gb_t0, gb.gb_gc, gb.gb_lnk, ALU.add)
        b.tt(gb.gb_rowC, gb.gb_t0, gb.gb_lnb, ALU.add)
        for h in range(4):
            pse = b.bank()
            b.mm(pse[:, 0:8], self.ohc[:, h * 128:(h + 1) * 128],
                 gb.gb_gc[:, 63::64], start=True, stop=False, tag="egl1")
            b.mm(pse[:, 1:8], self.nohc[:, h * 128:(h + 1) * 128],
                 gb.gb_gc[:, 63:TT - 64:64], start=False, stop=True, tag="egl2")
            b.act(self.eg_bc[h], pse[:, 0:8], AF.Exp, tag="egl3")
        for nl in range(8):
            cs = slice(nl * 64, nl * 64 + 64)
            gl = gb.gb_gc[:, nl * 64 + 63:nl * 64 + 64]
            if nl == 0:
                b.copy(self.q_rs[:, cs], gb.gb_rowB[:, cs])
                b.copy(self.q_c1[:, cs], gb.gb_rowC[:, cs])
            else:
                gp = gb.gb_gc[:, nl * 64 - 1:nl * 64]
                b.ts(self.q_rs[:, cs], gb.gb_rowB[:, cs], gp, ALU.subtract)
                b.ts(self.q_c1[:, cs], gb.gb_rowC[:, cs], gp, ALU.subtract)
            b.ts(self.q_c2[:, cs], gb.gb_rowA[:, cs], gl, ALU.add)
        for q in ("rs", "c1", "c2"):
            blk = {"rs": self.q_rs, "c1": self.q_c1, "c2": self.q_c2}[q]
            b.act(blk, blk, AF.Exp, tag="qexp")
        for nl in range(8):
            for q in ("rs", "c1", "c2", "be"):
                blk = {"rs": self.q_rs, "c1": self.q_c1, "c2": self.q_c2,
                       "be": self.gb_beta}[q]
                pst = b.bank()
                b.mm(pst[0:64, 0:4], blk[:, nl * 64:(nl + 1) * 64],
                     self.i128[:4, :4], tag="stgT", tp=(0, 0))
                b.mm(pst[64:128, 4:8], blk[:, nl * 64:(nl + 1) * 64],
                     self.i128[:4, :4], tag="stgT2", tp=(0, 64))
                cq = self.colq[q]
                for s in range(2):
                    b.copy(cq[0:64, 2 * nl + s:2 * nl + s + 1],
                           pst[0:64, 2 * s:2 * s + 1], tag="cqc")
                    b.copy(cq[64:128, 2 * nl + s:2 * nl + s + 1],
                           pst[64:128, 4 + 2 * s + 1:4 + 2 * s + 2], tag="cqc")
        for nl in range(8):
            for hp in range(2):
                for _ in self.delta_unit(t, hp, nl):
                    pass
        cin = self.cc1in[t // 4]
        cx = (t % 4) * TT
        for h in range(4):
            b.dma("gpsimd", cin[h * 128:(h + 1) * 128, cx:cx + TT],
                  self.mix[:, h * TT:(h + 1) * TT], tag="mixout")
        if "d_mix" in self.dmp:
            for h in range(4):
                b.dma("gpsimd", self.dmp["d_mix"][h * 128:(h + 1) * 128, c0:c0 + TT],
                      self.mix[:, h * TT:(h + 1) * TT])
        if "d_conv" in self.dmp:
            for c in range(12):
                b.dma("gpsimd", self.dmp["d_conv"][c * 128:(c + 1) * 128, c0:c0 + TT],
                      self.cvin[c][:, 3:TT + 3])
        if "d_gates" in self.dmp:
            b.dma("gpsimd", self.dmp["d_gates"][0:4, c0:c0 + TT], self.gb_beta[:])
            b.dma("gpsimd", self.dmp["d_gates"][4:8, c0:c0 + TT], self.gb_gc[:])

    def delta_unit(self, t, hp, nl):
        b = self.b
        h0, h1 = 2 * hp, 2 * hp + 1
        cs = slice(nl * 64, nl * 64 + 64)
        qf = [self.cvin[3 * h + 0][:, 3 + nl * 64:3 + nl * 64 + 64] for h in (h0, h1)]
        kf = [self.cvin[3 * h + 1][:, 3 + nl * 64:3 + nl * 64 + 64] for h in (h0, h1)]
        vf = [self.cvin[3 * h + 2][:, 3 + nl * 64:3 + nl * 64 + 64] for h in (h0, h1)]
        def cv(q):
            return self.colq[q][:, 2 * nl + hp:2 * nl + hp + 1]
        S = [self.Sst[h] for h in (h0, h1)]
        halves = (slice(0, 64), slice(64, 128))
        offs = (0, 64)
        pwN = self.t[f"u_pwN{hp}"]
        pwP = self.t[f"u_pwP{hp}"]
        Dt = self.t[f"u_Dt{hp}"]; Da = self.t[f"u_Da{hp}"]; sc64 = self.t[f"u_sc64{hp}"]
        TmT = self.t[f"u_TmT{hp}"]; attnT = self.t[f"u_attnT{hp}"]
        vtm = self.t[f"u_vtm{hp}"]; ktm = self.t[f"u_ktm{hp}"]
        y0 = self.t[f"u_y0{hp}"]; y1 = self.t[f"u_y1{hp}"]
        out_sb = self.t[f"u_out{hp}"]; sq = self.t[f"u_sq{hp}"]; vv = self.t[f"u_vv{hp}"]
        psG = b.bank()
        for j in range(2):
            b.mm(psG[halves[j], 0:64], kf[j], kf[j], tag="gramK", tp=(0, offs[j]))
        psA = b.bank()
        for j in range(2):
            b.mm(psA[halves[j], 0:64], kf[j], qf[j], tag="gramA", tp=(0, offs[j]))
        yield
        psMt = b.bank()
        psMa = b.bank()
        for j, h in enumerate((h0, h1)):
            oh = self.oh64[:, h * 64:(h + 1) * 64]
            b.mm(psMt[halves[j], 0:64], self.gb_rowA[:, cs], oh, start=True, stop=False,
                 tp=(0, offs[j]))
            b.mm(psMt[halves[j], 0:64], oh, self.gb_rowC[:, cs], start=False, stop=True,
                 tp=(0, offs[j]))
            b.mm(psMa[halves[j], 0:64], self.gb_rowA[:, cs], oh, start=True, stop=False,
                 tp=(0, offs[j]))
            b.mm(psMa[halves[j], 0:64], oh, self.gb_rowB[:, cs], start=False, stop=True,
                 tp=(0, offs[j]))
        yield
        b.tt(sc64, psMt[:, 0:64], self.maskS, ALU.add)
        b.act(Dt, sc64, AF.Exp)
        yield
        b.tt(sc64, psMa[:, 0:64], self.maskI, ALU.add)
        b.act(Da, sc64, AF.Exp)
        yield
        b.stt(TmT, psG[:, 0:64], -1.0, Dt, ALU.mult, ALU.mult)
        yield
        b.stt(attnT, psA[:, 0:64], -1.0, Da, ALU.mult, ALU.mult)
        yield
        psT = b.bank()
        for j in range(2):
            b.mm(psT[halves[j], 0:64], TmT[halves[j], :], self.i128[halves[j], halves[j]],
                 tag="TmTt", tp=(offs[j], offs[j]))
        b.copy(pwN[:, 0:64], psT[:, 0:64])
        b.copy(pwP[:, 0:64], TmT)
        for k in range(5):
            yield
            ps1 = b.bank()
            ps2 = b.bank()
            for j in range(2):
                Nk = pwN[halves[j], k * 64:(k + 1) * 64]
                Pk = pwP[halves[j], k * 64:(k + 1) * 64]
                b.mm(ps1[halves[j], 0:64], Pk, Nk, tag="powN", tp=(offs[j], offs[j]))
                b.mm(ps2[halves[j], 0:64], Nk, Pk, tag="powP", tp=(offs[j], offs[j]))
            if k < 4:
                b.copy(pwN[:, (k + 1) * 64:(k + 2) * 64], ps1[:, 0:64])
            b.copy(pwP[:, (k + 1) * 64:(k + 2) * 64], ps2[:, 0:64])
        yield
        psv = b.bank()
        for j in range(2):
            b.mm(psv[halves[j], 0:128], vf[j], self.i128, tag="vT", tp=(0, offs[j]))
        b.ts(vtm, psv[:, 0:128], cv("be"), ALU.mult)
        yield
        psk = b.bank()
        for j in range(2):
            b.mm(psk[halves[j], 0:128], kf[j], S[j], tag="kS", tp=(0, offs[j]))
        b.stt(y0, psk[:, 0:128], cv("c1"), vtm, ALU.mult, ALU.subtract)
        cur, nxt = y0, y1
        for k in range(5, -1, -1):
            yield
            psy = b.bank()
            for j in range(2):
                Pk = pwP[halves[j], k * 64:(k + 1) * 64]
                b.mm(psy[halves[j], 0:128], Pk, cur[halves[j], :], tag="applyP",
                     tp=(offs[j], offs[j]))
            b.tt(nxt, psy[:, 0:128], cur, ALU.add)
            cur, nxt = nxt, cur
        ym = cur
        yield
        pso = b.bank()
        for j in range(2):
            b.mm(pso[halves[j], 0:128], attnT[halves[j], :], ym[halves[j], :],
                 tag="attnV", tp=(offs[j], offs[j]))
        b.copy(out_sb, pso[:, 0:128])
        yield
        psq = b.bank()
        for j in range(2):
            b.mm(psq[halves[j], 0:128], qf[j], S[j], tag="qS", tp=(0, offs[j]))
        b.stt(out_sb, psq[:, 0:128], cv("rs"), out_sb, ALU.mult, ALU.add)
        yield
        b.act(sq, out_sb, AF.Square, accum_out=vv[:, 0:1])
        b.act(vv[:, 1:2], vv[:, 0:1], AF.Ln, scale=self.inv128c, bias=self.epsc)
        b.act(vv[:, 2:3], vv[:, 1:2], AF.Exp, scale=self.nhalf64)
        yield
        b.ts(sq, out_sb, vv[:, 2:3], ALU.mult)
        psn = b.bank()
        psn2 = b.bank()
        for j, h in enumerate((h0, h1)):
            pp = psn if j == 0 else psn2
            b.mm(pp[:, 0:64], sq[halves[j], :], self.i128[halves[j], halves[j]],
                 tag="outT", tp=(offs[j], 0))
            b.tt(self.mix[:, h * TT + nl * 64:h * TT + nl * 64 + 64], pp[:, 0:64],
                 self.zgw[:, h * TT + nl * 64:h * TT + nl * 64 + 64], ALU.mult, tag="mixw")
        yield
        pskt = b.bank()
        for j in range(2):
            b.mm(pskt[halves[j], 0:128], kf[j], self.i128, tag="kT", tp=(0, offs[j]))
        b.ts(ktm, pskt[:, 0:128], cv("c2"), ALU.mult)
        yield
        psu = b.bank()
        psu2 = b.bank()
        for j, h in enumerate((h0, h1)):
            pp = psu if j == 0 else psu2
            b.mm(pp[:, 0:128], ktm[halves[j], :], ym[halves[j], :], tag="Supd",
                 tp=(offs[j], 0))
            egl = self.eg_bc[h][:, nl:nl + 1]
            b.stt(S[j], S[j], egl, pp[:, 0:128], ALU.mult, ALU.subtract)

    def ln_kit(self, pref):
        b = self.b
        k = {}
        k["sq"] = [b.sb(f"{pref}_sq{i}", [128, TT]) for i in range(2)]
        k["st"] = b.sb(f"{pref}_st", [128, 12])
        k["rrow"] = b.sb(f"{pref}_rrow", [1, TT])
        k["nrow"] = b.sb(f"{pref}_nrow", [1, TT])
        k["rbc"] = b.sb(f"{pref}_rbc", [128, TT])
        k["nbc"] = b.sb(f"{pref}_nbc", [128, TT])
        k["eps"] = b.sb(f"{pref}_eps", [128, 1])
        b.memset(k["eps"], LN_EPS)
        k["nh"] = b.sb(f"{pref}_nh", [128, 1])
        b.memset(k["nh"], -0.5)
        k["inv"] = b.sb(f"{pref}_inv", [128, 1])
        b.memset(k["inv"], 1.0 / DM)
        return k

    def ln_tile(self, kit, acc, out, gamma=None, beta=None, tag="ln"):
        b = self.b
        bs = [b.bank() for _ in range(8)]
        for kk in range(16):
            ch = acc[:, kk * TT:(kk + 1) * TT]
            sq = kit["sq"][kk % 2]
            b.act(sq, ch, AF.Square, tag=tag + "sq")
            for tg in range(4):
                b.mm(bs[tg][:, 0:1], ch[:, tg * 128:(tg + 1) * 128], self.ones_col,
                     start=(kk == 0), stop=(kk == 15), tag=tag + "s")
                b.mm(bs[4 + tg][:, 0:1], sq[:, tg * 128:(tg + 1) * 128], self.ones_col,
                     start=(kk == 0), stop=(kk == 15), tag=tag + "q")
        st = kit["st"]
        for tg in range(4):
            c = st[:, tg * 3:tg * 3 + 3]
            b.ts(c[:, 0:1], bs[tg][:, 0:1], 1.0 / DM, ALU.mult)
            b.ts(c[:, 1:2], bs[4 + tg][:, 0:1], 1.0 / DM, ALU.mult)
        for tg in range(4):
            c = st[:, tg * 3:tg * 3 + 3]
            b.tt(c[:, 2:3], c[:, 0:1], c[:, 0:1], ALU.mult)
            b.tt(c[:, 1:2], c[:, 1:2], c[:, 2:3], ALU.subtract)
            b.act(c[:, 1:2], c[:, 1:2], AF.Ln, bias=kit["eps"])
            b.act(c[:, 1:2], c[:, 1:2], AF.Exp, scale=kit["nh"])
            b.tt(c[:, 2:3], c[:, 0:1], c[:, 1:2], ALU.mult)
        for tg in range(4):
            c = st[:, tg * 3:tg * 3 + 3]
            pst = b.bank()
            b.trans(pst[:1, 0:128], c[:, 1:2], self.i128, tag=tag + "T1")
            b.copy(kit["rrow"][:, tg * 128:(tg + 1) * 128], pst[:1, 0:128])
            pst2 = b.bank()
            b.trans(pst2[:1, 0:128], c[:, 2:3], self.i128, tag=tag + "T2")
            b.copy(kit["nrow"][:, tg * 128:(tg + 1) * 128], pst2[:1, 0:128])
        psb = b.bank()
        b.mm(psb[:, 0:TT], self.ones_row, kit["rrow"], tag=tag + "bc1")
        b.copy(kit["rbc"], psb[:, 0:TT])
        psb2 = b.bank()
        b.mm(psb2[:, 0:TT], self.ones_row, kit["nrow"], tag=tag + "bc2")
        b.copy(kit["nbc"], psb2[:, 0:TT])
        for kk in range(16):
            b.tt(out[:, kk * TT:(kk + 1) * TT], acc[:, kk * TT:(kk + 1) * TT],
                 kit["rbc"], ALU.mult, tag=tag + "n1")
        for kk in range(16):
            och = out[:, kk * TT:(kk + 1) * TT]
            b.tt(och, och, kit["nbc"], ALU.subtract, tag=tag + "n2")
        if gamma is not None:
            for kk in range(16):
                och = out[:, kk * TT:(kk + 1) * TT]
                b.ts(och, och, gamma[:, kk:kk + 1], ALU.mult,
                     beta[:, kk:kk + 1], ALU.add, tag=tag + "gb")

    def p3(self):
        b = self.b
        gbuf = b.sb("p3_gbuf", [128, 16 * TT])
        rbuf = b.sb("p3_rbuf", [128, 16 * TT])
        acc = b.sb("p3_acc", [128, 16 * TT])
        wsl = [b.sb(f"p3_w{i}", [128, 2048]) for i in range(2)]
        kit = self.ln_kit("p3")
        for tt2 in range(2):
            for k0 in range(0, 16, 4):
                idx = k0 * 2 + tt2
                dst3 = gbuf[:, k0 * TT:(k0 + 4) * TT].rearrange("p (k c) -> p k c", k=4)
                src3 = self.cc1out[0, k0 * 128:(k0 + 4) * 128, 0:TT].rearrange(
                    "(k p) c -> p k c", p=128)
                def ldfn(e, idx=idx, dst=dst3, s3=src3):
                    v = e.value_load(self.toksb[0:1, idx:idx + 1])
                    import dataclasses as _dc
                    return e.dma_start(out=dst, in_=_dc.replace(s3, offset=v))
                b.s.add("sync", ldfn, ins=[self.toksb, self.cc1out],
                        outs=[gbuf], dma=True, tag="p3g")
            for kk in range(16):
                b.dma("sync", rbuf[:, kk * TT:(kk + 1) * TT],
                      self.xresT[kk * 128:(kk + 1) * 128, tt2 * TT:(tt2 + 1) * TT], tag="p3r")
            for m in range(16):
                w = wsl[m % 2]
                b.dma(b.wq(w.tensor.name), w, self.wout[m], tag="p3w")
                ps = b.bank()
                for kk in range(16):
                    b.mm(ps[:, 0:TT], w[:, kk * 128:(kk + 1) * 128],
                         gbuf[:, kk * TT:(kk + 1) * TT], start=(kk == 0), stop=(kk == 15))
                b.tt(acc[:, m * TT:(m + 1) * TT], ps[:, 0:TT],
                     rbuf[:, m * TT:(m + 1) * TT], ALU.add, tag="p3res")
            self.ln_tile(kit, acc, acc, tag="p3ln")
            for m in range(16):
                b.dma("gpsimd", self.x1d[m * 128:(m + 1) * 128, tt2 * TT:(tt2 + 1) * TT],
                      acc[:, m * TT:(m + 1) * TT], tag="p3o")
        b.free_prefix("p3_")

    def ffn0(self):
        b = self.b
        x1b = b.sb("f0_x1", [128, 16 * TT])
        accb = b.sb("f0_acc", [128, 16 * TT])
        kit = self.ln_kit("f0k")
        gatew_sb = b.sb("f0_gw", [128, 16 * 8])
        for kk in range(16):
            b.dma("sync", gatew_sb[:, kk * 8:(kk + 1) * 8], self.gatew[kk])
        lngam = b.sb("f0_lng", [128, 16])
        lnbet = b.sb("f0_lnb", [128, 16])
        b.dma("sync", lngam, self.lngb[:, 0:16])
        b.dma("sync", lnbet, self.lngb[:, 16:32])
        lg = b.sb("f0_lg", [128, 8])
        eqt = b.sb("f0_eq", [128, 8])
        pfa = b.sb("f0_pfa", [128, 8])
        pfb = b.sb("f0_pfb", [128, 8])
        m1 = b.sb("f0_m1", [128, 1])
        m2 = b.sb("f0_m2", [128, 1])
        srow = b.sb("f0_srow", [1, TT])
        sbc = b.sb("f0_sbc", [128, TT])
        esl = [b.sb(f"f0_ew{i}", [128, 2048]) for i in range(2)]
        usl = [b.sb(f"f0_uw{i}", [128, 2048]) for i in range(2)]
        dsl = [b.sb(f"f0_dw{i}", [128, 2048]) for i in range(2)]
        hbuf = b.sb("f0_h", [128, 16 * TT])
        for tt2 in range(2):
            for kk in range(16):
                b.dma("sync", x1b[:, kk * TT:(kk + 1) * TT],
                      self.x1d[kk * 128:(kk + 1) * 128, tt2 * TT:(tt2 + 1) * TT], tag="f0x")
            for tg in range(4):
                psg = b.bank()
                for kk in range(16):
                    b.mm(psg[:, 0:8], x1b[:, kk * TT + tg * 128:kk * TT + (tg + 1) * 128],
                         gatew_sb[:, kk * 8:(kk + 1) * 8], start=(kk == 0), stop=(kk == 15),
                         tag="f0gmm")
                b.copy(lg, psg[:, 0:8])
                b.reduce(m1, lg, ALU.max)
                b.ts(eqt, lg, m1, ALU.is_ge)
                b.scan(pfa, eqt, eqt, 0.0, ALU.add, ALU.bypass)
                b.tt(pfa, pfa, eqt, ALU.subtract)
                b.ts(pfb, pfa, 0.0, ALU.is_equal)
                b.tt(eqt, eqt, pfb, ALU.mult)
                b.stt(lg, eqt, -3e38, lg, ALU.mult, ALU.add)
                b.reduce(m2, lg, ALU.max)
                b.tt(m1, m1, m2, ALU.add)
                pst = b.bank()
                b.trans(pst[:1, 0:128], m1, self.i128, tag="f0T")
                b.copy(srow[:, tg * 128:(tg + 1) * 128], pst[:1, 0:128])
            psb = b.bank()
            b.mm(psb[:, 0:TT], self.ones_row, srow, tag="f0bc")
            b.copy(sbc, psb[:, 0:TT])
            for m in range(16):
                w = esl[m % 2]
                b.dma(b.wq(w.tensor.name), w, self.expw[m], tag="f0ew")
                ps = b.bank()
                for kk in range(16):
                    b.mm(ps[:, 0:TT], w[:, kk * 128:(kk + 1) * 128],
                         x1b[:, kk * TT:(kk + 1) * TT], start=(kk == 0), stop=(kk == 15))
                b.tt(accb[:, m * TT:(m + 1) * TT], ps[:, 0:TT], sbc, ALU.mult, tag="f0moe")
            for p in range(4):
                for mi in range(16):
                    m = p * 16 + mi
                    w = usl[mi % 2]
                    b.dma(b.wq(w.tensor.name), w, self.upw[m], tag="f0uw")
                    ps = b.bank()
                    for kk in range(16):
                        b.mm(ps[:, 0:TT], w[:, kk * 128:(kk + 1) * 128],
                             x1b[:, kk * TT:(kk + 1) * TT], start=(kk == 0), stop=(kk == 15))
                    hch = hbuf[:, mi * TT:(mi + 1) * TT]
                    b.act(hch, ps[:, 0:TT], AF.Relu, tag="f0relu")
                    b.act(hch, hch, AF.Square, tag="f0sq")
                for mo in range(16):
                    w = dsl[mo % 2]
                    b.dma(b.wq(w.tensor.name), w, self.downw[mo, p], tag="f0dw")
                    ps = b.bank()
                    for kk in range(16):
                        b.mm(ps[:, 0:TT], w[:, kk * 128:(kk + 1) * 128],
                             hbuf[:, kk * TT:(kk + 1) * TT], start=(kk == 0), stop=(kk == 15))
                    och = accb[:, mo * TT:(mo + 1) * TT]
                    b.tt(och, och, ps[:, 0:TT], ALU.add, tag="f0dacc")
            self.ln_tile(kit, accb, accb, gamma=lngam, beta=lnbet, tag="f0l1")
            for kk in range(16):
                b.tt(accb[:, kk * TT:(kk + 1) * TT], accb[:, kk * TT:(kk + 1) * TT],
                     x1b[:, kk * TT:(kk + 1) * TT], ALU.add, tag="f0r")
            self.ln_tile(kit, accb, accb, tag="f0l2")
            for m in range(16):
                b.dma("gpsimd", self.cc2in[tt2][m * 128:(m + 1) * 128, 0:TT],
                      accb[:, m * TT:(m + 1) * TT], tag="f0o")
            if tt2 == 0:
                b.cc("AllGather", self.cc2in[0][:], self.cc2out[0][:], GROUPS4, tag="E2a")
            if "d_x2" in self.dmp:
                for m in range(16):
                    b.dma("gpsimd", self.dmp["d_x2"][m * 128:(m + 1) * 128, tt2 * TT:(tt2 + 1) * TT],
                          accb[:, m * TT:(m + 1) * TT])
        b.free_prefix("f0_", "f0k_")

    def setup_l1(self):
        b = self.b
        self.xg = b.sb("l1_xg", [128, 16 * TT])
        self.wq1s = b.sb("l1_wq", [128, 4 * 2048])
        self.wk1s = b.sb("l1_wk", [128, 4 * 2048])
        self.wv1s = b.sb("l1_wv", [128, 16 * 512])
        for m in range(4):
            b.dma("sync", self.wq1s[:, m * 2048:(m + 1) * 2048], self.wq1[m])
            b.dma("sync", self.wk1s[:, m * 2048:(m + 1) * 2048], self.wk1[m])
        for kk in range(16):
            b.dma("sync", self.wv1s[:, kk * 512:(kk + 1) * 512], self.wv1cols[kk])
        self.qb = b.sb("l1_qb", [128, 4 * TT])
        self.kb = b.sb("l1_kb", [128, 4 * TT])
        self.vb = b.sb("l1_vb", [128, 4 * TT])
        self.cosb = b.sb("l1_cos", [64, TT])
        self.sinb = b.sb("l1_sin", [64, TT])
        self.rot = [b.sb(f"l1_rot{j}", [64, TT]) for j in range(4)]
        self.kv = [b.sb(f"l1_kv{h}", [128, 128]) for h in range(4)]
        for h in range(4):
            b.memset(self.kv[h], 0.0)
        self.qdecs = b.sb("l1_qdec", [128, 4 * 128])
        self.dmks = b.sb("l1_dmk", [128, 4 * 128])
        self.kdecs = b.sb("l1_kdec", [128, 4])
        self.rcs = b.sb("l1_rc", [128, 4])
        for h in range(4):
            b.dma("sync", self.qdecs[:, h * 128:(h + 1) * 128], self.c_qdec[h])
            b.dma("sync", self.dmks[:, h * 128:(h + 1) * 128], self.c_dmaskT[h])
        b.dma("sync", self.kdecs, self.c_kdec)
        b.dma("sync", self.rcs, self.c_rC)
        self.qd_sc = b.sb("l1_qd", [128, 128])
        self.sT_sc = b.sb("l1_sT", [128, 128])
        self.otm_sc = b.sb("l1_otm", [128, 128])
        self.ktm_sc = b.sb("l1_ktm", [128, 128])
        self.o3 = b.sb("l1_o3", [128, 4 * TT])

    def l1_tile(self, t):
        b = self.b
        r, half = t // 2, t % 2
        for kk in range(16):
            b.dma("sync", self.xg[:, kk * TT:(kk + 1) * TT],
                  self.cc2out[half][r * DM + kk * 128: r * DM + (kk + 1) * 128, 0:TT],
                  tag="l1x")
        c0 = t * TT
        b.dma("sync", self.cosb, self.c_rotcos[:, c0:c0 + TT], tag="l1cs")
        b.dma("sync", self.sinb, self.c_rotsin[:, c0:c0 + TT], tag="l1cs")
        for (wsb, dst) in ((self.wq1s, self.qb), (self.wk1s, self.kb)):
            for m in range(4):
                ps = b.bank()
                for kk in range(16):
                    b.mm(ps[:, 0:TT], wsb[:, m * 2048 + kk * 128:m * 2048 + (kk + 1) * 128],
                         self.xg[:, kk * TT:(kk + 1) * TT], start=(kk == 0), stop=(kk == 15),
                         tag="l1qk")
                dch = dst[:, m * TT:(m + 1) * TT]
                b.tt(self.rot[0], ps[0:64, 0:TT], self.cosb, ALU.mult, tag="l1r1")
                b.tt(self.rot[1], ps[64:128, 0:TT], self.sinb, ALU.mult, tag="l1r2")
                b.tt(self.rot[2], ps[64:128, 0:TT], self.cosb, ALU.mult, tag="l1r4")
                b.tt(self.rot[3], ps[0:64, 0:TT], self.sinb, ALU.mult, tag="l1r5")
                b.tt(dch[0:64, :], self.rot[0], self.rot[1], ALU.add, tag="l1r3")
                b.tt(dch[64:128, :], self.rot[2], self.rot[3], ALU.subtract, tag="l1r6")
        for tg in range(4):
            ps = b.bank()
            for kk in range(16):
                b.mm(ps[:, 0:TT], self.xg[:, kk * TT + tg * 128:kk * TT + (tg + 1) * 128],
                     self.wv1s[:, kk * 512:(kk + 1) * 512], start=(kk == 0), stop=(kk == 15),
                     tag="l1v")
            b.copy(self.vb[:, tg * TT:(tg + 1) * TT], ps[:, 0:TT])
        for cj in range(4):
            for h in range(4):
                qf = self.qb[:, h * TT + cj * 128:h * TT + (cj + 1) * 128]
                kf = self.kb[:, h * TT + cj * 128:h * TT + (cj + 1) * 128]
                vtm = self.vb[:, cj * TT + h * 128:cj * TT + (h + 1) * 128]
                kv = self.kv[h]
                b.tt(self.qd_sc, qf, self.qdecs[:, h * 128:(h + 1) * 128], ALU.mult, tag="l1qd")
                psS = b.bank()
                b.mm(psS[:, 0:128], kf, qf, tag="l1sc")
                b.tt(self.sT_sc, psS[:, 0:128], self.dmks[:, h * 128:(h + 1) * 128],
                     ALU.mult, tag="l1dm")
                pso = b.bank()
                b.mm(pso[:, 0:128], self.sT_sc, vtm, start=True, stop=False, tag="l1intra")
                b.mm(pso[:, 0:128], self.qd_sc, kv, start=False, stop=True, tag="l1inter")
                b.copy(self.otm_sc, pso[:, 0:128])
                psT = b.bank()
                b.trans(psT[:, 0:128], kf, self.i128, tag="l1kT")
                b.ts(self.ktm_sc, psT[:, 0:128], self.kdecs[:, h:h + 1], ALU.mult, tag="l1kd")
                psU = b.bank()
                b.mm(psU[:, 0:128], self.ktm_sc, vtm, tag="l1kv")
                b.stt(kv, kv, self.rcs[:, h:h + 1], psU[:, 0:128], ALU.mult, ALU.add, tag="l1kvu")
                psF = b.bank()
                b.trans(psF[:, 0:128], self.otm_sc, self.i128, tag="l1oT")
                b.copy(self.o3[:, h * TT + cj * 128:h * TT + (cj + 1) * 128],
                       psF[:, 0:128], tag="l1oev")
        cin3 = self.cc3in[t // 4]
        cx3 = (t % 4) * TT
        for h in range(4):
            b.dma("gpsimd", cin3[h * 128:(h + 1) * 128, cx3:cx3 + TT],
                  self.o3[:, h * TT:(h + 1) * TT], tag="l1out")
        if "d_l1qkv" in self.dmp:
            for h in range(4):
                b.dma("gpsimd", self.dmp["d_l1qkv"][h * 128:(h + 1) * 128, c0:c0 + TT],
                      self.qb[:, h * TT:(h + 1) * TT])
                b.dma("gpsimd", self.dmp["d_l1qkv"][512 + h * 128:512 + (h + 1) * 128, c0:c0 + TT],
                      self.kb[:, h * TT:(h + 1) * TT])
            for tg in range(4):
                b.dma("gpsimd", self.dmp["d_l1qkv"][1024 + tg * 128:1024 + (tg + 1) * 128, c0:c0 + TT],
                      self.vb[:, tg * TT:(tg + 1) * TT])

    def p7(self):
        b = self.b
        gbuf = b.sb("p7_gb", [128, 16 * TT])
        nrm = b.sb("p7_n", [128, 16 * TT])
        rbuf = b.sb("p7_rb", [128, 16 * TT])
        acc = b.sb("p7_a", [128, 16 * TT])
        wsl = [b.sb(f"p7_w{i}", [128, 2048]) for i in range(2)]
        sq = b.sb("p7_sq", [128, TT])
        stt_ = b.sb("p7_st", [128, 8])
        rrow = b.sb("p7_rrow", [1, TT])
        rbc = b.sb("p7_rbc", [128, TT])
        epsr = b.sb("p7_eps", [128, 1])
        b.memset(epsr, LN_EPS)
        nh = b.sb("p7_nh", [128, 1])
        b.memset(nh, -0.5)
        rmsw_sb = b.sb("p7_rw", [128, 16])
        b.dma("sync", rmsw_sb, self.rmsw)
        kit = self.ln_kit("p7k")
        for tt2 in range(2):
            for k0 in range(0, 16, 4):
                idx = k0 * 2 + tt2
                dst3 = gbuf[:, k0 * TT:(k0 + 4) * TT].rearrange("p (k c) -> p k c", k=4)
                src3 = self.cc3out[0, k0 * 128:(k0 + 4) * 128, 0:TT].rearrange(
                    "(k p) c -> p k c", p=128)
                def ldfn(e, idx=idx, dst=dst3, s3=src3):
                    v = e.value_load(self.toksb[0:1, idx:idx + 1])
                    import dataclasses as _dc
                    return e.dma_start(out=dst, in_=_dc.replace(s3, offset=v))
                b.s.add("sync", ldfn, ins=[self.toksb, self.cc3out],
                        outs=[gbuf], dma=True, tag="p7g")
            bs = [b.bank() for _ in range(4)]
            for kk in range(16):
                ch = gbuf[:, kk * TT:(kk + 1) * TT]
                b.act(sq, ch, AF.Square, tag="p7sq")
                for tg in range(4):
                    b.mm(bs[tg][:, 0:1], sq[:, tg * 128:(tg + 1) * 128], self.ones_col,
                         start=(kk == 0), stop=(kk == 15), tag="p7s")
            for tg in range(4):
                c = stt_[:, tg * 2:tg * 2 + 2]
                b.ts(c[:, 0:1], bs[tg][:, 0:1], 1.0 / DM, ALU.mult)
            for tg in range(4):
                c = stt_[:, tg * 2:tg * 2 + 2]
                b.act(c[:, 0:1], c[:, 0:1], AF.Ln, bias=epsr)
                b.act(c[:, 0:1], c[:, 0:1], AF.Exp, scale=nh)
                pst = b.bank()
                b.trans(pst[:1, 0:128], c[:, 0:1], self.i128, tag="p7T")
                b.copy(rrow[:, tg * 128:(tg + 1) * 128], pst[:1, 0:128])
            psb = b.bank()
            b.mm(psb[:, 0:TT], self.ones_row, rrow, tag="p7bc")
            b.copy(rbc, psb[:, 0:TT])
            for kk in range(16):
                b.tt(nrm[:, kk * TT:(kk + 1) * TT], gbuf[:, kk * TT:(kk + 1) * TT],
                     rbc, ALU.mult, tag="p7n1")
            for kk in range(16):
                b.ts(nrm[:, kk * TT:(kk + 1) * TT], nrm[:, kk * TT:(kk + 1) * TT],
                     rmsw_sb[:, kk:kk + 1], ALU.mult, tag="p7n2")
            for kk in range(16):
                b.dma("sync", rbuf[:, kk * TT:(kk + 1) * TT],
                      self.cc2in[tt2][kk * 128:(kk + 1) * 128, 0:TT], tag="p7r")
            for m in range(16):
                w = wsl[m % 2]
                b.dma(b.wq(w.tensor.name), w, self.wo1[m], tag="p7w")
                ps = b.bank()
                for kk in range(16):
                    b.mm(ps[:, 0:TT], w[:, kk * 128:(kk + 1) * 128],
                         nrm[:, kk * TT:(kk + 1) * TT], start=(kk == 0), stop=(kk == 15))
                b.tt(acc[:, m * TT:(m + 1) * TT], ps[:, 0:TT],
                     rbuf[:, m * TT:(m + 1) * TT], ALU.add, tag="p7res")
            self.ln_tile(kit, acc, acc, tag="p7ln")
            for m in range(16):
                b.dma("gpsimd", self.x3d[m * 128:(m + 1) * 128, tt2 * TT:(tt2 + 1) * TT],
                      acc[:, m * TT:(m + 1) * TT], tag="p7o")
            if "d_x3" in self.dmp:
                for m in range(16):
                    b.dma("gpsimd", self.dmp["d_x3"][m * 128:(m + 1) * 128, tt2 * TT:(tt2 + 1) * TT],
                          acc[:, m * TT:(m + 1) * TT])
        b.free_prefix("p7_", "p7k_")

    def ffn1(self):
        b = self.b
        x3b = b.sb("f1_x3", [128, 16 * TT])
        accb = b.sb("f1_acc", [128, 16 * TT])
        hbuf = b.sb("f1_h", [128, 22 * TT])
        gsl = [b.sb(f"f1_gw{i}", [128, 2048]) for i in range(2)]
        uslb = [b.sb(f"f1_uw{i}", [128, 2048]) for i in range(2)]
        dslb = [b.sb(f"f1_dw{i}", [128, 2816]) for i in range(2)]
        s1g_sb = b.sb("f1_sg", [128, 44])
        s1u_sb = b.sb("f1_su", [128, 44])
        s1d_sb = b.sb("f1_sd", [128, 16])
        b.dma("sync", s1g_sb, self.s1g)
        b.dma("sync", s1u_sb, self.s1u)
        b.dma("sync", s1d_sb, self.s1d)
        tgu = b.sb("f1_tg", [128, TT])
        kit = self.ln_kit("f1k")
        for tt2 in range(2):
            for kk in range(16):
                b.dma("sync", x3b[:, kk * TT:(kk + 1) * TT],
                      self.x3d[kk * 128:(kk + 1) * 128, tt2 * TT:(tt2 + 1) * TT], tag="f1x")
            for p in range(2):
                nmi = 22
                for mi in range(nmi):
                    m = p * 22 + mi
                    wg = gsl[mi % 2]
                    wu = uslb[mi % 2]
                    b.dma(b.wq(wg.tensor.name), wg, self.w1g[m], tag="f1gw")
                    b.dma(b.wq(wu.tensor.name), wu, self.w1u[m], tag="f1uw")
                    psg = b.bank()
                    for kk in range(16):
                        b.mm(psg[:, 0:TT], wg[:, kk * 128:(kk + 1) * 128],
                             x3b[:, kk * TT:(kk + 1) * TT], start=(kk == 0), stop=(kk == 15))
                    psu = b.bank()
                    for kk in range(16):
                        b.mm(psu[:, 0:TT], wu[:, kk * 128:(kk + 1) * 128],
                             x3b[:, kk * TT:(kk + 1) * TT], start=(kk == 0), stop=(kk == 15))
                    hch = hbuf[:, mi * TT:(mi + 1) * TT]
                    b.act(hch, psg[:, 0:TT], AF.Silu, scale=s1g_sb[:, m:m + 1], tag="f1si")
                    b.ts(tgu, psu[:, 0:TT], s1u_sb[:, m:m + 1], ALU.mult, tag="f1us")
                    b.tt(hch, hch, tgu, ALU.mult, tag="f1h")
                for mo in range(16):
                    w = dslb[mo % 2]
                    b.dma(b.wq(w.tensor.name), w, self.w1d[mo, p], tag="f1dw")
                    ps = b.bank()
                    for kk in range(22):
                        b.mm(ps[:, 0:TT], w[:, kk * 128:(kk + 1) * 128],
                             hbuf[:, kk * TT:(kk + 1) * TT], start=(kk == 0), stop=(kk == 21))
                    och = accb[:, mo * TT:(mo + 1) * TT]
                    if p == 0:
                        b.copy(och, ps[:, 0:TT], tag="f1d0")
                    else:
                        b.tt(och, och, ps[:, 0:TT], ALU.add, tag="f1d1")
            for kk in range(16):
                och = accb[:, kk * TT:(kk + 1) * TT]
                b.stt(och, och, s1d_sb[:, kk:kk + 1], x3b[:, kk * TT:(kk + 1) * TT],
                      ALU.mult, ALU.add, tag="f1res")
            self.ln_tile(kit, accb, accb, tag="f1ln")
            for m in range(16):
                b.dma("gpsimd", self.yT[m * 128:(m + 1) * 128, tt2 * TT:(tt2 + 1) * TT],
                      accb[:, m * TT:(m + 1) * TT], tag="f1o")
        b.free_prefix("f1_", "f1k_")

    def build(self, l0_only_tiles=None):
        b = self.b
        self.setup_consts()
        self.setup_l0()
        if l0_only_tiles is not None:
            for t in range(l0_only_tiles):
                self.l0_tile(t)
            return b.s.resolve_and_emit(self.nc)
        for t in range(NT0):
            self.l0_tile(t)
            if t == 3:
                self.b.cc("AllGather", self.cc1in[0][:], self.cc1out[0], GROUPS4, tag="E1a")
        for ap in self.l2ps:
            b.bank_release(ap)
        b.free_prefix("xt", "wqz", "cvin", "zgw", "Sst", "gb_", "negcol4", "onecol4", "normw_sb",
                      "sc128", "bi128", "one11", "e6_11", "q_rs", "q_c1", "q_c2",
                      "colq_", "eg_bc", "u_", "inv128c", "oh64", "ohc", "nohc", "oh128", "sc128_4", "bi128_4", "one4", "e6_4", "tsave",
                      "epsc", "nhalf64", "mix", "conv_sc0", "conv_sc1", "conv_sc2", "conv_sc3", "sq_sc", "wba_sb", "convw_sb",
                      "dtb_sb", "alog_sb", "negexpA")
        self.b.cc("AllGather", self.cc1in[1][:], self.cc1out[1], GROUPS4, tag="E1b")
        b.s.fence()
        self.p3()
        b.s.fence()
        if "d_x1" in self.dmp:
            b.dma("gpsimd", self.dmp["d_x1"], self.x1d)
            b.s.fence()
        self.ffn0()
        b.s.fence()
        self.b.cc("AllGather", self.cc2in[1][:], self.cc2out[1][:], GROUPS4, tag="E2b")
        self.setup_l1()
        for t in range(NT0):
            self.l1_tile(t)
            if t == 3:
                self.b.cc("AllGather", self.cc3in[0][:], self.cc3out[0], GROUPS4, tag="E3a")
        self.b.cc("AllGather", self.cc3in[1][:], self.cc3out[1], GROUPS4, tag="E3b")
        b.free_prefix("l1_")
        b.s.fence()
        self.p7()
        b.s.fence()
        self.ffn1()
        return b.s.resolve_and_emit(self.nc)


def _slab(w):
    K, M = w.shape
    nK, nM = K // 128, M // 128
    return np.ascontiguousarray(
        w.reshape(nK, 128, nM, 128).transpose(2, 1, 0, 3).reshape(nM, 128, nK * 128))


def _build_slopes(n):
    def pow2(nn):
        start = 2.0 ** (-2.0 ** (-(math.log2(nn) - 3)))
        return [start * start ** i for i in range(nn)]
    if math.log2(n).is_integer():
        s = pow2(n)
    else:
        closest = 2 ** math.floor(math.log2(n))
        s = pow2(closest) + pow2(2 * closest)[0::2][:n - closest]
    return np.asarray(s, dtype=np.float64)


def make_in_maps(inputs):
    f32 = np.float32
    x = np.asarray(inputs["L0_attn_hidden_states"], f32)
    Wqkvz = np.asarray(inputs["L0_attn_W_qkvz"], f32)
    Wba = np.asarray(inputs["L0_attn_W_ba"], f32)
    convw = np.asarray(inputs["L0_attn_conv_weight"], f32)
    dtb = np.asarray(inputs["L0_attn_dt_bias"], f32)
    alog = np.asarray(inputs["L0_attn_A_log"], f32)
    normw = np.asarray(inputs["L0_attn_norm_weight"], f32)
    Wout = np.asarray(inputs["L0_attn_W_out"], f32)
    gatew = np.asarray(inputs["L0_ffn_gate_w"], f32)
    upw = np.asarray(inputs["L0_ffn_up_w"], f32)
    downw = np.asarray(inputs["L0_ffn_down_w"], f32)
    expw = np.asarray(inputs["L0_ffn_expert_weight"], f32)
    lng = np.asarray(inputs["L0_ffn_ln_gamma"], f32)
    lnb = np.asarray(inputs["L0_ffn_ln_beta"], f32)
    Wq1 = np.asarray(inputs["L1_attn_Wq"], f32)
    Wk1 = np.asarray(inputs["L1_attn_Wk"], f32)
    Wv1 = np.asarray(inputs["L1_attn_Wv"], f32)
    Wo1 = np.asarray(inputs["L1_attn_Wo"], f32)
    rmsw = np.asarray(inputs["L1_attn_rms_w"], f32)
    g1 = np.asarray(inputs["L1_ffn_gate_w"], f32)
    u1 = np.asarray(inputs["L1_ffn_up_w"], f32)
    d1 = np.asarray(inputs["L1_ffn_down_w"], f32)
    s1g = np.asarray(inputs["L1_ffn_gate_s"], f32)
    s1u = np.asarray(inputs["L1_ffn_up_s"], f32)
    s1d = np.asarray(inputs["L1_ffn_down_s"], f32)

    wout_s = _slab(Wout)
    upw_s = _slab(upw)
    dw = _slab(downw)
    downw_s = np.ascontiguousarray(dw.reshape(16, 128, 4, 2048).transpose(0, 2, 1, 3))
    expw_s = _slab(expw)
    gatew_s = np.ascontiguousarray(gatew.reshape(16, 128, 8))
    lngb = np.concatenate([lng.reshape(16, 128).T, lnb.reshape(16, 128).T], axis=1)
    lngb = np.ascontiguousarray(lngb)
    wo1_s = _slab(Wo1)
    rmsw_h = np.ascontiguousarray(rmsw.reshape(16, 128).T)
    w1g_s = _slab(np.ascontiguousarray(g1.T))
    w1u_s = _slab(np.ascontiguousarray(u1.T))
    dT = _slab(np.ascontiguousarray(d1.T))
    w1d_s = np.ascontiguousarray(dT.reshape(16, 128, 2, 2816).transpose(0, 2, 1, 3))
    s1g_h = np.ascontiguousarray(s1g.reshape(44, 128).T)
    s1u_h = np.ascontiguousarray(s1u.reshape(44, 128).T)
    s1d_h = np.ascontiguousarray(s1d.reshape(16, 128).T)
    idx = np.arange(64)
    maskS = np.where(idx[None, :] > idx[:, None], 0.0, -3e38).astype(f32)
    maskI = np.where(idx[None, :] >= idx[:, None], 0.0, -3e38).astype(f32)
    maskS = np.tile(maskS, (2, 1))
    maskI = np.tile(maskI, (2, 1))
    i128 = np.eye(128, dtype=f32)
    freq = (1.0 / 1024.0) ** np.linspace(0.0, 1.0, 32, dtype=f32)
    freq = np.concatenate([freq, np.zeros(32, dtype=f32)])
    tpos = np.arange(S, dtype=np.float64)
    theta = tpos[:, None] * freq[None, :].astype(np.float64)
    rotcos = np.ascontiguousarray(np.cos(theta).T.astype(f32))
    rotsin = np.ascontiguousarray(np.sin(theta).T.astype(f32))
    slopes = _build_slopes(16)
    rr = np.exp(-slopes)
    ii = np.arange(128, dtype=np.float64)
    xT = [np.ascontiguousarray(x[bb].T) for bb in range(2)]
    oh128c = np.zeros((128, 16), f32)
    for h in range(4):
        oh128c[:, 4 * h + h] = 0.0
        oh128c[:, 4 * h + h] = 1.0
    oh64 = np.zeros((4, 4 * 64), f32)
    ohc = np.zeros((4, 4 * 128), f32)
    for h in range(4):
        oh64[h, h * 64:(h + 1) * 64] = 1.0
        ohc[h, h * 128:(h + 1) * 128] = 1.0

    in_maps = []
    for c in range(NCORE):
        bb, q = c // 4, c % 4
        hg = [4 * q + j for j in range(4)]
        tok0 = 1024 * q
        toktab = np.zeros((1, 32), np.int32)
        half_w = S // 2
        for kk in range(16):
            for tt2 in range(2):
                pos = tok0 + tt2 * TT
                toktab[0, kk * 2 + tt2] = ((pos // half_w) * DM * half_w
                                           + kk * 128 * half_w + pos % half_w)
        selq = Wqkvz[:, 2048 * q:2048 * q + 2048]
        wba_sel = np.concatenate([Wba[:, [2 * h for h in hg]],
                                  Wba[:, [2 * h + 1 for h in hg]]], axis=1)
        convsel = np.stack([convw[tau * 2048 + hh * 128: tau * 2048 + (hh + 1) * 128, :]
                            for j, hh in enumerate(hg) for tau in range(3)])
        qdec = np.stack([np.tile((rr[hh] ** (ii + 1.0)).astype(f32)[None, :], (128, 1))
                         for hh in hg])
        dmT = []
        for hh in hg:
            diff = ii[:, None] - ii[None, :]
            dm = np.where(diff >= 0, rr[hh] ** np.where(diff >= 0, diff, 0.0), 0.0)
            dmT.append(dm.T.astype(f32))
        kdec = np.stack([(rr[hh] ** (127.0 - ii)).astype(f32) for hh in hg], axis=1)
        rc = np.stack([np.full(128, rr[hh] ** 128.0, dtype=f32) for hh in hg], axis=1)
        m = {
            "xT": xT[bb],
            "xresT": np.ascontiguousarray(xT[bb][:, tok0:tok0 + TPC]),
            "tokoff": toktab,
            "wqkvz": _slab(selq),
            "wba": np.ascontiguousarray(wba_sel.reshape(16, 128, 8)),
            "convw": np.ascontiguousarray(convsel),
            "dtb": np.asarray([[dtb[hh]] for hh in hg], f32),
            "alog": np.asarray([[alog[hh]] for hh in hg], f32),
            "normw": np.ascontiguousarray(normw[:, None]),
            "wout": wout_s, "gatew": gatew_s, "upw": upw_s, "downw": downw_s,
            "expw": expw_s, "lngb": lngb,
            "wq1": _slab(Wq1[:, 512 * q:512 * q + 512]),
            "wk1": _slab(Wk1[:, 512 * q:512 * q + 512]),
            "wv1cols": np.ascontiguousarray(
                Wv1[:, 512 * q:512 * q + 512].reshape(16, 128, 512)),
            "wo1": wo1_s, "rmsw": rmsw_h,
            "w1g": w1g_s, "w1u": w1u_s, "w1d": w1d_s,
            "s1g": s1g_h, "s1u": s1u_h, "s1d": s1d_h,
            "c_maskS": maskS, "c_maskI": maskI, "c_i128": i128,
            "c_rotcos": rotcos, "c_rotsin": rotsin,
            "c_qdec": qdec, "c_dmaskT": np.stack(dmT),
            "c_kdec": np.ascontiguousarray(kdec), "c_rC": np.ascontiguousarray(rc),
            "c_oh64": oh64, "c_ohc": ohc, "c_nohc": -ohc, "c_oh128": oh128c,
        }
        in_maps.append(m)
    return in_maps


_NC_CACHE = {}


def build_nc(dump=()):
    key = tuple(sorted(dump))
    if key not in _NC_CACHE:
        nc = bass.Bass("TRN2", target_bir_lowering=False)
        mk = MK(nc, dump=dump)
        mk.build()
        _NC_CACHE[key] = nc
    return _NC_CACHE[key]


def kernel(**inputs):
    in_maps = make_in_maps(inputs)
    nc = build_nc()
    res = run_bass_kernel_spmd(nc, in_maps, core_ids=list(range(NCORE)))
    y = np.zeros((B, S, DM), np.float32)
    for c in range(NCORE):
        bb, q = c // 4, c % 4
        y[bb, 1024 * q:1024 * (q + 1), :] = res.results[c]["yT"].T
    return y


# version 08
# speedup vs baseline: 1.4849x; 1.1126x over previous
import math
import numpy as np
import concourse.bass as bass
import concourse.mybir as mybir
from concourse.bass_utils import run_bass_kernel_spmd

F32 = mybir.dt.float32
AF = mybir.ActivationFunctionType
ALU = mybir.AluOpType

B, S, DM = 2, 4096, 2048
HK = 16
DK = DV = 128
KC = 4
CH = 64
LCH = 128
NCORE = 8
HPC = 4
TPC = 1024
TT = 512
INTER0 = 8192
INTER1 = 5632
NEXP = 8

ENGINES = ("tensor", "vector", "scalar", "gpsimd", "sync")


class Op:
    __slots__ = ("idx", "engine", "fn", "ins", "outs", "dma", "token", "waits",
                 "inc_needed", "tag", "_deps", "pre_drain")

    def __init__(self, idx, engine, fn, ins, outs, dma, tag):
        self.idx = idx
        self.engine = engine
        self.fn = fn
        self.ins = ins
        self.outs = outs
        self.dma = dma
        self.token = None
        self.waits = []
        self.inc_needed = False
        self.pre_drain = False
        self.tag = tag


class Sched:

    def __init__(self):
        self.ops = []
        self.last_writer = {}
        self.readers = {}
        self.fence_set = []

    def fence(self):
        last = {}
        for op in self.ops:
            if op.dma:
                nm = op.outs[0][0] if isinstance(op.outs[0], tuple) else op.outs[0]
                key = ("dma", nm)
            else:
                key = ("c", op.engine)
            last[key] = op
        self.fence_set = list(last.values())

    def add(self, engine, fn, ins=(), outs=(), dma=False, tag="", force_inc=False):
        op = Op(len(self.ops), engine, fn,
                [k for x in ins for k in self._r(x)],
                [k for x in outs for k in self._r(x)], dma, tag)
        op.inc_needed = force_inc
        deps = []
        for r in op.ins:
            w = self.last_writer.get(r)
            if w is not None:
                deps.append(w)
        for r in op.outs:
            w = self.last_writer.get(r)
            if w is not None:
                deps.append(w)
            for rd in self.readers.get(r, ()):
                if rd is not op:
                    deps.append(rd)
        for r in op.ins:
            self.readers.setdefault(r, []).append(op)
        for r in op.outs:
            self.last_writer[r] = op
            self.readers[r] = []
        deps.extend(self.fence_set)
        op._deps = deps
        self.ops.append(op)
        return op

    @staticmethod
    def _r(x):
        if isinstance(x, str):
            return [x]
        t = getattr(x, "tensor", None)
        if t is None:
            return [x.name]
        try:
            sbuf = str(x.space).endswith("SBUF")
        except Exception:
            sbuf = False
        if not sbuf:
            return [t.name]
        try:
            base = x.base_partition()
            npart = x.shape[0]
        except Exception:
            return [t.name]
        keys = []
        if base < 64:
            keys.append((t.name, 0))
        if base + npart > 64:
            keys.append((t.name, 1))
        return keys

    def resolve_and_emit(self, nc):
        sems = {}
        for e in ENGINES:
            sems[("c", e)] = nc.alloc_semaphore(name=f"sc_{e}")
        counts = {}

        for op in self.ops:
            for d in op._deps:
                if d.dma:
                    continue
                if d.engine != op.engine:
                    d.inc_needed = True
        for op in self.ops:
            if op.dma:
                nm = op.outs[0][0] if isinstance(op.outs[0], tuple) else op.outs[0]
                key = ("d", nm)
                if key not in sems:
                    sems[key] = nc.alloc_semaphore(name=f"sd_{nm}"[:30])
                    counts[key] = 0
                counts[key] += 16
                op.token = (key, counts[key])
            else:
                key = ("c", op.engine)
                if op.inc_needed:
                    counts[key] = counts.get(key, 0) + 1
                op.token = (key, counts.get(key, 0))

        last_drain = {}
        running = {}
        for op in self.ops:
            need = {}
            for d in op._deps:
                if d.engine == op.engine and not d.dma:
                    if (not op.dma and op.engine != "tensor"
                            and d.idx >= last_drain.get(op.engine, -1)):
                        op.pre_drain = True
                    continue
                if (d.dma and op.dma and d.token[0] == op.token[0]
                        and op.engine != "gpsimd" and d.engine != "gpsimd"):
                    continue
                k, v = d.token
                if d.dma:
                    v = running.get(k, 0)
                if v > 0 and need.get(k, 0) < v:
                    need[k] = v
            if op.pre_drain:
                last_drain[op.engine] = op.idx
            op.waits = sorted(need.items())
            if op.dma:
                running[op.token[0]] = op.token[1]

        hi = {}
        for op in self.ops:
            w2 = []
            for k, v in op.waits:
                kk = (op.engine, k)
                if hi.get(kk, -1) >= v:
                    continue
                hi[kk] = v
                w2.append((k, v))
            op.waits = w2

        by_eng = {e: [] for e in ENGINES}
        for op in self.ops:
            by_eng[op.engine].append(op)

        dma_finals = [(k, v) for k, v in counts.items() if k[0] == "d"]

        with nc.Block() as block:
            def mk(e):
                def run(eng):
                    for op in by_eng[e]:
                        if op.pre_drain:
                            eng.drain()
                        for k, v in op.waits:
                            eng.wait_ge(sems[k], v)
                        ins = op.fn(eng)
                        if op.dma:
                            ins.then_inc(sems[op.token[0]], 16)
                        elif op.inc_needed:
                            ins.then_inc(sems[("c", e)], 1)
                    if e == "gpsimd":
                        for k, v in dma_finals:
                            eng.wait_ge(sems[k], v)
                return run
            for e in ENGINES:
                if by_eng[e] or e == "gpsimd":
                    getattr(block, e)(mk(e))
        return sems, counts

def repack_lhsT(w):
    K, M = w.shape
    nK, nM = K // 128, M // 128
    t = w.reshape(nK, 128, nM, 128)
    return np.ascontiguousarray(t.transpose(2, 0, 1, 3))


class Bld:

    def __init__(self, nc):
        self.nc = nc
        self.s = Sched()
        self.t = {}
        self.guards = {}
        self._wq_map = {}
        self.uid = 0
        self.banks = [nc.alloc_psum_tensor(f"bank{i}", [128, 512], F32).ap()
                      for i in range(8)]
        self.bank_i = 0
        self.held = set()

    def bank(self):
        for _ in range(16):
            ap = self.banks[self.bank_i % 8]
            self.bank_i += 1
            if ap.tensor.name not in self.held:
                return ap
        raise RuntimeError("all banks held")

    def bank_hold(self):
        ap = self.bank()
        self.held.add(ap.tensor.name)
        return ap

    def bank_release(self, ap):
        self.held.discard(ap.tensor.name)

    _WQ = ("sync", "scalar", "gpsimd")

    def wq(self, tensor_name=None):
        if tensor_name is not None:
            if tensor_name not in self._wq_map:
                self._wq_map[tensor_name] = self._WQ[len(self._wq_map) % 3]
            return self._wq_map[tensor_name]
        self.uid += 1
        return self._WQ[self.uid % 3]

    def sb(self, name, shape, dtype=F32):
        g = self.nc.sbuf_tensor(name, list(shape), dtype)
        h = g.__enter__()
        self.guards[name] = (self.uid, g)
        self.uid += 1
        ap = h.ap()
        self.t[name] = ap
        return ap

    def free(self, *names):
        for n in sorted(names, key=lambda n: -self.guards[n][0]):
            _, g = self.guards.pop(n)
            g.__exit__(None, None, None)

    def free_prefix(self, *prefixes):
        self.free(*[n for n in self.guards if any(n.startswith(p) for p in prefixes)])

    def dram(self, name, shape, dtype=F32, kind="Internal"):
        ap = self.nc.dram_tensor(name, list(shape), dtype, kind=kind).ap()
        self.t[name] = ap
        return ap

    def inp(self, name, shape, dtype=F32):
        return self.dram(name, shape, dtype, kind="ExternalInput")

    def out(self, name, shape, dtype=F32):
        return self.dram(name, shape, dtype, kind="ExternalOutput")

    def dma(self, q, out, in_, tag=""):
        self.s.add(q, lambda e, o=out, i=in_: e.dma_start(out=o, in_=i),
                   ins=[in_], outs=[out], dma=True, tag=tag)

    def mm(self, out, lhsT, rhs, start=True, stop=True, tag="", tp=None):
        self.s.add("tensor",
                   lambda e, o=out, l=lhsT, r=rhs, st=start, sp=stop, t=tp:
                       e.matmul(o, l, r, start=st, stop=sp, tile_position=t),
                   ins=[lhsT, rhs], outs=[out], tag=tag)

    def trans(self, out_psum, in_sb, ident, tag="", tp=None):
        self.s.add("tensor",
                   lambda e, o=out_psum, i=in_sb, d=ident, t=tp: e.transpose(o, i, d, tile_position=t),
                   ins=[in_sb, ident], outs=[out_psum], tag=tag)

    def act(self, out, in_, func, bias=None, scale=None, accum_out=None, eng="scalar", tag=""):
        kw = {}
        if bias is not None:
            kw["bias"] = bias
        if scale is not None:
            kw["scale"] = scale
        if accum_out is not None:
            kw["accum_out"] = accum_out
        ins = [in_] + [x for x in (bias, scale) if hasattr(x, "tensor")]
        outs = [out] + ([accum_out] if accum_out is not None else [])
        self.s.add(eng, lambda e, o=out, i=in_, f=func, k=dict(kw): e.activation(out=o, in_=i, func=f, **k),
                   ins=ins, outs=outs, tag=tag)

    def tt(self, out, a, b, op, eng="vector", tag=""):
        self.s.add(eng, lambda e, o=out, x=a, y=b, p=op: e.tensor_tensor(out=o, in0=x, in1=y, op=p),
                   ins=[a, b], outs=[out], tag=tag)

    def ts(self, out, a, s1, op0, s2=None, op1=None, eng="vector", tag=""):
        ins = [a] + [x for x in (s1, s2) if hasattr(x, "tensor")]
        if op1 is None:
            self.s.add(eng, lambda e, o=out, x=a, s=s1, p=op0:
                       e.tensor_single_scalar(out=o, in_=x, scalar=s, op=p),
                       ins=ins, outs=[out], tag=tag)
        else:
            self.s.add(eng, lambda e, o=out, x=a, sa=s1, sb=s2, p0=op0, p1=op1:
                       e.tensor_scalar(out=o, in0=x, scalar1=sa, scalar2=sb, op0=p0, op1=p1),
                       ins=ins, outs=[out], tag=tag)

    def stt(self, out, in0, scalar, in1, op0, op1, eng="vector", tag=""):
        ins = [in0, in1] + ([scalar] if hasattr(scalar, "tensor") else [])
        self.s.add(eng, lambda e, o=out, a=in0, s=scalar, b=in1, p0=op0, p1=op1:
                   e.scalar_tensor_tensor(out=o, in0=a, scalar=s, in1=b, op0=p0, op1=p1),
                   ins=ins, outs=[out], tag=tag)

    def copy(self, out, in_, eng="vector", tag=""):
        self.s.add(eng, lambda e, o=out, i=in_: e.tensor_copy(out=o, in_=i),
                   ins=[in_], outs=[out], tag=tag)

    def scopy(self, out, in_, tag=""):
        self.s.add("scalar", lambda e, o=out, i=in_: e.copy(out=o, in_=i),
                   ins=[in_], outs=[out], tag=tag)

    def memset(self, ap, val, eng="vector", tag=""):
        self.s.add(eng, lambda e, o=ap, v=val: e.memset(o, v), ins=[], outs=[ap], tag=tag)

    def scan(self, out, d0, d1, initial, op0, op1, tag=""):
        self.s.add("vector", lambda e, o=out, a=d0, b=d1, i=initial, p0=op0, p1=op1:
                   e.tensor_tensor_scan(out=o, data0=a, data1=b, initial=i, op0=p0, op1=p1),
                   ins=[d0, d1], outs=[out], tag=tag)

    def reduce(self, out, in_, op, eng="vector", tag=""):
        self.s.add(eng, lambda e, o=out, i=in_, p=op:
                   e.tensor_reduce(out=o, in_=i, axis=mybir.AxisListType.X, op=p),
                   ins=[in_], outs=[out], tag=tag)

    def cc(self, kind, ins_ap, outs_ap, groups, tag=""):
        self.s.add("gpsimd",
                   lambda e, k=kind, i=ins_ap, o=outs_ap, g=groups:
                   e.collective_compute(k, ALU.bypass, replica_groups=g, ins=[i], outs=[o]),
                   ins=[ins_ap], outs=[outs_ap], tag=tag, force_inc=True)


GROUPS4 = [[0, 1, 2, 3], [4, 5, 6, 7]]
NT0 = S // TT
LN_EPS = 1e-5


class MK:

    def __init__(self, nc, dump=()):
        self.nc = nc
        self.b = Bld(nc)
        self.dump = set(dump)
        b = self.b
        self.xT = b.inp("xT", [DM, S])
        self.xresT = b.inp("xresT", [DM, TPC])
        self.tokoff = b.inp("tokoff", [1, 32], mybir.dt.int32)
        self.wqkvz = b.inp("wqkvz", [16, 128, 2048])
        self.wba = b.inp("wba", [16, 128, 8])
        self.convw = b.inp("convw", [12, 128, 4])
        self.dtb = b.inp("dtb", [4, 1])
        self.alog = b.inp("alog", [4, 1])
        self.normw = b.inp("normw", [128, 1])
        self.wout = b.inp("wout", [16, 128, 2048])
        self.gatew = b.inp("gatew", [16, 128, 8])
        self.upw = b.inp("upw", [64, 128, 2048])
        self.downw = b.inp("downw", [16, 4, 128, 2048])
        self.expw = b.inp("expw", [16, 128, 2048])
        self.lngb = b.inp("lngb", [128, 32])
        self.wq1 = b.inp("wq1", [4, 128, 2048])
        self.wk1 = b.inp("wk1", [4, 128, 2048])
        self.wv1cols = b.inp("wv1cols", [16, 128, 512])
        self.wo1 = b.inp("wo1", [16, 128, 2048])
        self.rmsw = b.inp("rmsw", [128, 16])
        self.w1g = b.inp("w1g", [44, 128, 2048])
        self.w1u = b.inp("w1u", [44, 128, 2048])
        self.w1d = b.inp("w1d", [16, 2, 128, 2816])
        self.s1g = b.inp("s1g", [128, 44])
        self.s1u = b.inp("s1u", [128, 44])
        self.s1d = b.inp("s1d", [128, 16])
        self.c_maskS = b.inp("c_maskS", [128, 64])
        self.c_maskI = b.inp("c_maskI", [128, 64])
        self.c_i128 = b.inp("c_i128", [128, 128])
        self.c_rotcos = b.inp("c_rotcos", [64, S])
        self.c_rotsin = b.inp("c_rotsin", [64, S])
        self.c_qdec = b.inp("c_qdec", [4, 128, 128])
        self.c_dmaskT = b.inp("c_dmaskT", [4, 128, 128])
        self.c_kdec = b.inp("c_kdec", [128, 4])
        self.c_rC = b.inp("c_rC", [128, 4])
        self.c_oh64 = b.inp("c_oh64", [4, 4 * 64])
        self.c_ohc = b.inp("c_ohc", [4, 4 * 128])
        self.c_nohc = b.inp("c_nohc", [4, 4 * 128])
        self.c_oh128 = b.inp("c_oh128", [128, 16])
        self.cc1in = [b.dram("cc1inA", [512, S // 2]), b.dram("cc1inB", [512, S // 2])]
        self.cc1out = b.dram("cc1out", [2, DM, S // 2])
        self.x1d = b.dram("x1d", [DM, TPC])
        self.cc2in = [b.dram("cc2inA", [DM, TT]), b.dram("cc2inB", [DM, TT])]
        self.cc2out = [b.dram("cc2outA", [4 * DM, TT]), b.dram("cc2outB", [4 * DM, TT])]
        self.cc3in = [b.dram("cc3inA", [512, S // 2]), b.dram("cc3inB", [512, S // 2])]
        self.cc3out = b.dram("cc3out", [2, DM, S // 2])
        self.x3d = b.dram("x3d", [DM, TPC])
        self.yT = b.out("yT", [DM, TPC])
        self.dmp = {}
        for name, shape in [
            ("d_conv", [1536, S]), ("d_gates", [8, S]), ("d_mix", [512, S]),
            ("d_x1", [DM, TPC]), ("d_x2", [DM, TPC]), ("d_l1qkv", [3 * 512, S]),
            ("d_l1mix", [512, S]), ("d_x3", [DM, TPC]),
        ]:
            if name in self.dump:
                self.dmp[name] = b.out(name, shape)

    def setup_consts(self):
        b = self.b
        self.ones_col = b.sb("ones_col", [128, 1])
        b.memset(self.ones_col, 1.0)
        self.ones_row = b.sb("ones_row", [1, 128])
        b.memset(self.ones_row, 1.0)
        self.zero_11 = b.sb("zero_11", [1, 1])
        b.memset(self.zero_11, 0.0)
        self.half_neg = b.sb("half_neg", [128, 1])
        b.memset(self.half_neg, -0.5)
        self.neg_col = b.sb("neg_col", [128, 1])
        b.memset(self.neg_col, -1.0)
        self.eps_l2 = b.sb("eps_l2", [128, 1])
        b.memset(self.eps_l2, 128 * 1e-6)
        self.i128 = b.sb("i128", [128, 128])
        b.dma("sync", self.i128, self.c_i128)
        self.maskS = b.sb("maskS", [128, 64])
        b.dma("sync", self.maskS, self.c_maskS)
        self.maskI = b.sb("maskI", [128, 64])
        b.dma("sync", self.maskI, self.c_maskI)
        self.toksb = b.sb("toksb", [1, 32], mybir.dt.int32)
        b.dma("sync", self.toksb, self.tokoff)
        self.dtb_sb = b.sb("dtb_sb", [4, 1])
        b.dma("sync", self.dtb_sb, self.dtb)
        self.alog_sb = b.sb("alog_sb", [4, 1])
        b.dma("sync", self.alog_sb, self.alog)
        self.negexpA = b.sb("negexpA", [4, 1])
        b.act(self.negexpA, self.alog_sb, AF.Exp)
        b.ts(self.negexpA, self.negexpA, -1.0, ALU.mult)
        self.normw_sb = b.sb("normw_sb", [128, 1])
        b.dma("sync", self.normw_sb, self.normw)

    def setup_l0(self):
        b = self.b
        self.xt = [b.sb(f"xt{i}", [128, 16 * TT]) for i in range(2)]
        self.wqz_sl = [b.sb(f"wqz{i}", [128, 16 * 128]) for i in range(2)]
        self.wba_sb = b.sb("wba_sb", [128, 16 * 8])
        for k in range(16):
            b.dma("sync", self.wba_sb[:, k * 8:(k + 1) * 8], self.wba[k])
        self.convw_sb = b.sb("convw_sb", [128, 12 * 4])
        for c in range(12):
            b.dma("sync", self.convw_sb[:, c * 4:(c + 1) * 4], self.convw[c])
        self.cvin = [b.sb(f"cvin{c}", [128, TT + 3]) for c in range(12)]
        for c in range(12):
            b.memset(self.cvin[c][:, 0:3], 0.0)
        self.zgw = b.sb("zgw", [128, 4 * TT])
        self.Sst = [b.sb(f"Sst{h}", [128, 128]) for h in range(4)]
        for h in range(4):
            b.memset(self.Sst[h], 0.0)
        for nm in ["gb_b", "gb_a", "gb_lnb", "gb_beta", "gb_gc",
                   "gb_lnq", "gb_lnk", "gb_rowA", "gb_rowB",
                   "gb_rowC", "gb_t0", "gb_t1", "q_rs", "q_c1", "q_c2"]:
            setattr(self, nm, b.sb(nm, [4, TT]))
        self.oh64 = b.sb("oh64", [4, 4 * 64])
        b.dma("sync", self.oh64, self.c_oh64)
        self.ohc = b.sb("ohc", [4, 4 * 128])
        b.dma("sync", self.ohc, self.c_ohc)
        self.nohc = b.sb("nohc", [4, 4 * 128])
        b.dma("sync", self.nohc, self.c_nohc)
        self.oh128 = b.sb("oh128", [128, 16])
        b.dma("sync", self.oh128, self.c_oh128)
        self.sc128_4 = b.sb("sc128_4", [4, 1]); b.memset(self.sc128_4, 128.0)
        self.bi128_4 = b.sb("bi128_4", [4, 1]); b.memset(self.bi128_4, 128 * 1e-6)
        self.one4 = b.sb("one4", [4, 1]); b.memset(self.one4, 1.0)
        self.e6_4 = b.sb("e6_4", [4, 1]); b.memset(self.e6_4, 1e-6)
        self.negcol4 = b.sb("negcol4", [4, 1]); b.memset(self.negcol4, -1.0)
        self.onecol4 = b.sb("onecol4", [4, 1]); b.memset(self.onecol4, 1.0)
        self.sc128 = b.sb("sc128", [1, 1]); b.memset(self.sc128, 128.0)
        self.bi128 = b.sb("bi128", [1, 1]); b.memset(self.bi128, 128 * 1e-6)
        self.one11 = b.sb("one11", [1, 1]); b.memset(self.one11, 1.0)
        self.e6_11 = b.sb("e6_11", [1, 1]); b.memset(self.e6_11, 1e-6)
        self.colq = {q: b.sb(f"colq_{q}", [128, 16]) for q in ("rs", "c1", "c2", "be")}
        self.eg_bc = [b.sb(f"eg_bc{h}", [128, 8]) for h in range(4)]
        self.conv_sc = [b.sb(f"conv_sc{j}", [128, TT]) for j in range(4)]
        self.l2ps = [self.b.bank_hold(), self.b.bank_hold()]
        self.tsave = b.sb("tsave", [128, 36])
        self.sq_sc = b.sb("sq_sc", [128, TT])
        self.t = {}
        for nm, w in (("pwN", 384), ("pwP", 384), ("Dt", 64), ("Da", 64), ("sc64", 64),
                      ("TmT", 64), ("attnT", 64), ("vtm", 128), ("ktm", 128),
                      ("y0", 128), ("y1", 128), ("out", 128), ("sq", 128), ("vv", 3)):
            for s in range(2):
                self.t[f"u_{nm}{s}"] = b.sb(f"u_{nm}{s}", [128, w])
        self.inv128c = b.sb("inv128c", [128, 1]); b.memset(self.inv128c, 1.0 / 128.0)
        self.epsc = b.sb("epsc", [128, 1]); b.memset(self.epsc, 1e-6)
        self.nhalf64 = b.sb("nhalf64", [128, 1]); b.memset(self.nhalf64, -0.5)
        self.mix = b.sb("mix", [128, 4 * TT])

    def l0_tile(self, t):
        b = self.b
        xt = self.xt[t % 2]
        c0 = t * TT
        for k in range(16):
            b.dma("sync", xt[:, k * TT:(k + 1) * TT],
                  self.xT[k * 128:(k + 1) * 128, c0:c0 + TT], tag=f"xt{t}")
        psbb = b.bank()
        for k in range(16):
            b.mm(psbb[:4, 0:TT], self.wba_sb[:, k * 8:k * 8 + 4],
                 xt[:, k * TT:(k + 1) * TT], start=(k == 0), stop=(k == 15))
        psba = b.bank()
        for k in range(16):
            b.mm(psba[:4, 0:TT], self.wba_sb[:, k * 8 + 4:k * 8 + 8],
                 xt[:, k * TT:(k + 1) * TT], start=(k == 0), stop=(k == 15))
        b.copy(self.gb_b[:], psbb[0:4, 0:TT])
        b.copy(self.gb_a[:], psba[0:4, 0:TT])
        gb = self
        b.act(gb.gb_t0, gb.gb_b, AF.Abs)
        b.act(gb.gb_t0, gb.gb_t0, AF.Exp, scale=self.negcol4)
        b.act(gb.gb_t0, gb.gb_t0, AF.Ln, bias=self.onecol4)
        b.ts(gb.gb_t1, gb.gb_b, 0.0, ALU.min)
        b.tt(gb.gb_lnb, gb.gb_t1, gb.gb_t0, ALU.subtract)
        b.act(gb.gb_beta, gb.gb_lnb, AF.Exp)
        b.ts(gb.gb_t1, gb.gb_a, self.dtb_sb, ALU.add)
        b.act(gb.gb_t0, gb.gb_t1, AF.Abs)
        b.act(gb.gb_t0, gb.gb_t0, AF.Exp, scale=self.negcol4)
        b.act(gb.gb_t0, gb.gb_t0, AF.Ln, bias=self.onecol4)
        b.ts(gb.gb_t1, gb.gb_t1, 0.0, ALU.max)
        b.tt(gb.gb_t1, gb.gb_t1, gb.gb_t0, ALU.add)
        b.ts(gb.gb_t1, gb.gb_t1, self.negexpA, ALU.mult)
        b.scan(gb.gb_gc, gb.gb_t1, gb.gb_t1, 0.0, ALU.add, ALU.bypass)
        for h in range(4):
            pse = b.bank()
            b.mm(pse[:, 0:8], self.ohc[:, h * 128:(h + 1) * 128],
                 gb.gb_gc[:, 63::64], start=True, stop=False, tag="egl1")
            b.mm(pse[:, 1:8], self.nohc[:, h * 128:(h + 1) * 128],
                 gb.gb_gc[:, 63:TT - 64:64], start=False, stop=True, tag="egl2")
            b.act(self.eg_bc[h], pse[:, 0:8], AF.Exp, tag="egl3")
        cw = self.convw_sb
        for m in range(16):
            wsl = self.wqz_sl[m % 2]
            b.dma(b.wq(wsl.tensor.name), wsl, self.wqkvz[m], tag=f"wq{t}_{m}")
            ps = b.bank()
            for k in range(16):
                b.mm(ps[:, 0:TT], wsl[:, k * 128:(k + 1) * 128],
                     xt[:, k * TT:(k + 1) * TT], start=(k == 0), stop=(k == 15))
            h, tau = m // 4, m % 4
            if tau < 3:
                cc = 3 * h + tau
                if t > 0:
                    b.copy(self.cvin[cc][:, 0:3], self.tsave[:, cc * 3:cc * 3 + 3], tag="halo")
                b.copy(self.cvin[cc][:, 3:TT + 3], ps[:, 0:TT], tag="cvev")
                b.copy(self.tsave[:, cc * 3:cc * 3 + 3], self.cvin[cc][:, TT:TT + 3], tag="tsv")
            else:
                b.act(self.zgw[:, h * TT:(h + 1) * TT], ps[:, 0:TT], AF.Silu, tag="zsilu")
                b.ts(self.zgw[:, h * TT:(h + 1) * TT], self.zgw[:, h * TT:(h + 1) * TT],
                     self.normw_sb, ALU.mult, tag="zw")
                ccs = [3 * h + 0, 3 * h + 1, 3 * h + 2]
                for j, c in enumerate(ccs):
                    b.ts(self.conv_sc[j], self.cvin[c][:, 0:TT],
                         cw[:, c * 4 + 0:c * 4 + 1], ALU.mult, tag="cv0")
                for tap in (1, 2, 3):
                    for j, c in enumerate(ccs):
                        b.stt(self.conv_sc[j], self.cvin[c][:, tap:TT + tap],
                              cw[:, c * 4 + tap:c * 4 + tap + 1], self.conv_sc[j],
                              ALU.mult, ALU.add, tag=f"cv{tap}")
                for j, c in enumerate(ccs):
                    b.act(self.cvin[c][:, 3:TT + 3], self.conv_sc[j], AF.Silu, tag="cvsilu")
                for (tau2, blk, sc_, bi_) in ((0, gb.gb_lnq, self.sc128_4, self.bi128_4),
                                              (1, gb.gb_lnk, self.one4, self.e6_4)):
                    cc2 = 3 * h + tau2
                    b.act(self.sq_sc, self.cvin[cc2][:, 3:TT + 3], AF.Square, tag="l2sq")
                    psn = self.l2ps[tau2]
                    b.mm(psn[0:4, 0:TT], self.oh128[:, 4 * h:4 * h + 4], self.sq_sc,
                         start=(h == 0), stop=(h == 3), tag="l2mm")
                    if h == 3:
                        b.act(blk, psn[0:4, 0:TT], AF.Ln, scale=sc_, bias=bi_, tag="l2ln")
        b.ts(gb.gb_lnq, gb.gb_lnq, -0.5, ALU.mult)
        b.ts(gb.gb_lnk, gb.gb_lnk, -0.5, ALU.mult)
        b.stt(gb.gb_rowA, gb.gb_gc, -1.0, gb.gb_lnk, ALU.mult, ALU.add)
        b.tt(gb.gb_rowB, gb.gb_gc, gb.gb_lnq, ALU.add)
        b.tt(gb.gb_t0, gb.gb_gc, gb.gb_lnk, ALU.add)
        b.tt(gb.gb_rowC, gb.gb_t0, gb.gb_lnb, ALU.add)
        for h in range(4):
            pse = b.bank()
            b.mm(pse[:, 0:8], self.ohc[:, h * 128:(h + 1) * 128],
                 gb.gb_gc[:, 63::64], start=True, stop=False, tag="egl1")
            b.mm(pse[:, 1:8], self.nohc[:, h * 128:(h + 1) * 128],
                 gb.gb_gc[:, 63:TT - 64:64], start=False, stop=True, tag="egl2")
            b.act(self.eg_bc[h], pse[:, 0:8], AF.Exp, tag="egl3")
        for nl in range(8):
            cs = slice(nl * 64, nl * 64 + 64)
            gl = gb.gb_gc[:, nl * 64 + 63:nl * 64 + 64]
            if nl == 0:
                b.copy(self.q_rs[:, cs], gb.gb_rowB[:, cs])
                b.copy(self.q_c1[:, cs], gb.gb_rowC[:, cs])
            else:
                gp = gb.gb_gc[:, nl * 64 - 1:nl * 64]
                b.ts(self.q_rs[:, cs], gb.gb_rowB[:, cs], gp, ALU.subtract)
                b.ts(self.q_c1[:, cs], gb.gb_rowC[:, cs], gp, ALU.subtract)
            b.ts(self.q_c2[:, cs], gb.gb_rowA[:, cs], gl, ALU.add)
        for q in ("rs", "c1", "c2"):
            blk = {"rs": self.q_rs, "c1": self.q_c1, "c2": self.q_c2}[q]
            b.act(blk, blk, AF.Exp, tag="qexp")
        for nl in range(8):
            for q in ("rs", "c1", "c2", "be"):
                blk = {"rs": self.q_rs, "c1": self.q_c1, "c2": self.q_c2,
                       "be": self.gb_beta}[q]
                pst = b.bank()
                b.mm(pst[0:64, 0:4], blk[:, nl * 64:(nl + 1) * 64],
                     self.i128[:4, :4], tag="stgT", tp=(0, 0))
                b.mm(pst[64:128, 4:8], blk[:, nl * 64:(nl + 1) * 64],
                     self.i128[:4, :4], tag="stgT2", tp=(0, 64))
                cq = self.colq[q]
                for s in range(2):
                    b.copy(cq[0:64, 2 * nl + s:2 * nl + s + 1],
                           pst[0:64, 2 * s:2 * s + 1], tag="cqc")
                    b.copy(cq[64:128, 2 * nl + s:2 * nl + s + 1],
                           pst[64:128, 4 + 2 * s + 1:4 + 2 * s + 2], tag="cqc")
        for nl in range(8):
            gens = [self.delta_unit(t, hp, nl) for hp in range(2)]
            while gens:
                for g in gens[:]:
                    try:
                        next(g)
                    except StopIteration:
                        gens.remove(g)
        cin = self.cc1in[t // 4]
        cx = (t % 4) * TT
        for h in range(4):
            b.dma("gpsimd", cin[h * 128:(h + 1) * 128, cx:cx + TT],
                  self.mix[:, h * TT:(h + 1) * TT], tag="mixout")
        if "d_mix" in self.dmp:
            for h in range(4):
                b.dma("gpsimd", self.dmp["d_mix"][h * 128:(h + 1) * 128, c0:c0 + TT],
                      self.mix[:, h * TT:(h + 1) * TT])
        if "d_conv" in self.dmp:
            for c in range(12):
                b.dma("gpsimd", self.dmp["d_conv"][c * 128:(c + 1) * 128, c0:c0 + TT],
                      self.cvin[c][:, 3:TT + 3])
        if "d_gates" in self.dmp:
            b.dma("gpsimd", self.dmp["d_gates"][0:4, c0:c0 + TT], self.gb_beta[:])
            b.dma("gpsimd", self.dmp["d_gates"][4:8, c0:c0 + TT], self.gb_gc[:])

    def delta_unit(self, t, hp, nl):
        b = self.b
        h0, h1 = 2 * hp, 2 * hp + 1
        cs = slice(nl * 64, nl * 64 + 64)
        qf = [self.cvin[3 * h + 0][:, 3 + nl * 64:3 + nl * 64 + 64] for h in (h0, h1)]
        kf = [self.cvin[3 * h + 1][:, 3 + nl * 64:3 + nl * 64 + 64] for h in (h0, h1)]
        vf = [self.cvin[3 * h + 2][:, 3 + nl * 64:3 + nl * 64 + 64] for h in (h0, h1)]
        def cv(q):
            return self.colq[q][:, 2 * nl + hp:2 * nl + hp + 1]
        S = [self.Sst[h] for h in (h0, h1)]
        halves = (slice(0, 64), slice(64, 128))
        offs = (0, 64)
        pwN = self.t[f"u_pwN{hp}"]
        pwP = self.t[f"u_pwP{hp}"]
        Dt = self.t[f"u_Dt{hp}"]; Da = self.t[f"u_Da{hp}"]; sc64 = self.t[f"u_sc64{hp}"]
        TmT = self.t[f"u_TmT{hp}"]; attnT = self.t[f"u_attnT{hp}"]
        vtm = self.t[f"u_vtm{hp}"]; ktm = self.t[f"u_ktm{hp}"]
        y0 = self.t[f"u_y0{hp}"]; y1 = self.t[f"u_y1{hp}"]
        out_sb = self.t[f"u_out{hp}"]; sq = self.t[f"u_sq{hp}"]; vv = self.t[f"u_vv{hp}"]
        psMt = b.bank()
        psMa = b.bank()
        for j, h in enumerate((h0, h1)):
            oh = self.oh64[:, h * 64:(h + 1) * 64]
            b.mm(psMt[halves[j], 0:64], self.gb_rowA[:, cs], oh, start=True, stop=False,
                 tp=(0, offs[j]))
            b.mm(psMt[halves[j], 0:64], oh, self.gb_rowC[:, cs], start=False, stop=True,
                 tp=(0, offs[j]))
            b.mm(psMa[halves[j], 0:64], self.gb_rowA[:, cs], oh, start=True, stop=False,
                 tp=(0, offs[j]))
            b.mm(psMa[halves[j], 0:64], oh, self.gb_rowB[:, cs], start=False, stop=True,
                 tp=(0, offs[j]))
        yield
        b.tt(sc64, psMt[:, 0:64], self.maskS, ALU.add)
        b.act(Dt, sc64, AF.Exp)
        yield
        b.tt(sc64, psMa[:, 0:64], self.maskI, ALU.add)
        b.act(Da, sc64, AF.Exp)
        yield
        psG = b.bank()
        for j in range(2):
            b.mm(psG[halves[j], 0:64], kf[j], kf[j], tag="gramK", tp=(0, offs[j]))
        yield
        b.stt(TmT, psG[:, 0:64], -1.0, Dt, ALU.mult, ALU.mult)
        psA = b.bank()
        for j in range(2):
            b.mm(psA[halves[j], 0:64], kf[j], qf[j], tag="gramA", tp=(0, offs[j]))
        yield
        b.stt(attnT, psA[:, 0:64], -1.0, Da, ALU.mult, ALU.mult)
        yield
        psT = b.bank()
        for j in range(2):
            b.mm(psT[halves[j], 0:64], TmT[halves[j], :], self.i128[halves[j], halves[j]],
                 tag="TmTt", tp=(offs[j], offs[j]))
        b.copy(pwN[:, 0:64], psT[:, 0:64])
        b.copy(pwP[:, 0:64], TmT)
        for k in range(5):
            yield
            ps1 = b.bank()
            ps2 = b.bank()
            for j in range(2):
                Nk = pwN[halves[j], k * 64:(k + 1) * 64]
                Pk = pwP[halves[j], k * 64:(k + 1) * 64]
                b.mm(ps1[halves[j], 0:64], Pk, Nk, tag="powN", tp=(offs[j], offs[j]))
                b.mm(ps2[halves[j], 0:64], Nk, Pk, tag="powP", tp=(offs[j], offs[j]))
            if k < 4:
                b.copy(pwN[:, (k + 1) * 64:(k + 2) * 64], ps1[:, 0:64])
            b.copy(pwP[:, (k + 1) * 64:(k + 2) * 64], ps2[:, 0:64])
        yield
        psv = b.bank()
        for j in range(2):
            b.mm(psv[halves[j], 0:128], vf[j], self.i128, tag="vT", tp=(0, offs[j]))
        b.ts(vtm, psv[:, 0:128], cv("be"), ALU.mult)
        yield
        psk = b.bank()
        for j in range(2):
            b.mm(psk[halves[j], 0:128], kf[j], S[j], tag="kS", tp=(0, offs[j]))
        b.stt(y0, psk[:, 0:128], cv("c1"), vtm, ALU.mult, ALU.subtract)
        cur, nxt = y0, y1
        for k in range(5, -1, -1):
            yield
            psy = b.bank()
            for j in range(2):
                Pk = pwP[halves[j], k * 64:(k + 1) * 64]
                b.mm(psy[halves[j], 0:128], Pk, cur[halves[j], :], tag="applyP",
                     tp=(offs[j], offs[j]))
            b.tt(nxt, psy[:, 0:128], cur, ALU.add)
            cur, nxt = nxt, cur
        ym = cur
        yield
        pso = b.bank()
        for j in range(2):
            b.mm(pso[halves[j], 0:128], attnT[halves[j], :], ym[halves[j], :],
                 tag="attnV", tp=(offs[j], offs[j]))
        b.copy(out_sb, pso[:, 0:128])
        yield
        psq = b.bank()
        for j in range(2):
            b.mm(psq[halves[j], 0:128], qf[j], S[j], tag="qS", tp=(0, offs[j]))
        b.stt(out_sb, psq[:, 0:128], cv("rs"), out_sb, ALU.mult, ALU.add)
        yield
        b.act(sq, out_sb, AF.Square, accum_out=vv[:, 0:1])
        b.act(vv[:, 1:2], vv[:, 0:1], AF.Ln, scale=self.inv128c, bias=self.epsc)
        b.act(vv[:, 2:3], vv[:, 1:2], AF.Exp, scale=self.nhalf64)
        yield
        b.ts(sq, out_sb, vv[:, 2:3], ALU.mult)
        psn = b.bank()
        psn2 = b.bank()
        for j, h in enumerate((h0, h1)):
            pp = psn if j == 0 else psn2
            b.mm(pp[:, 0:64], sq[halves[j], :], self.i128[halves[j], halves[j]],
                 tag="outT", tp=(offs[j], 0))
            b.tt(self.mix[:, h * TT + nl * 64:h * TT + nl * 64 + 64], pp[:, 0:64],
                 self.zgw[:, h * TT + nl * 64:h * TT + nl * 64 + 64], ALU.mult, tag="mixw")
        yield
        pskt = b.bank()
        for j in range(2):
            b.mm(pskt[halves[j], 0:128], kf[j], self.i128, tag="kT", tp=(0, offs[j]))
        b.ts(ktm, pskt[:, 0:128], cv("c2"), ALU.mult)
        yield
        psu = b.bank()
        psu2 = b.bank()
        for j, h in enumerate((h0, h1)):
            pp = psu if j == 0 else psu2
            b.mm(pp[:, 0:128], ktm[halves[j], :], ym[halves[j], :], tag="Supd",
                 tp=(offs[j], 0))
            egl = self.eg_bc[h][:, nl:nl + 1]
            b.stt(S[j], S[j], egl, pp[:, 0:128], ALU.mult, ALU.subtract)

    def ln_kit(self, pref):
        b = self.b
        k = {}
        k["sq"] = [b.sb(f"{pref}_sq{i}", [128, TT]) for i in range(2)]
        k["st"] = b.sb(f"{pref}_st", [128, 12])
        k["rrow"] = b.sb(f"{pref}_rrow", [1, TT])
        k["nrow"] = b.sb(f"{pref}_nrow", [1, TT])
        k["rbc"] = b.sb(f"{pref}_rbc", [128, TT])
        k["nbc"] = b.sb(f"{pref}_nbc", [128, TT])
        k["eps"] = b.sb(f"{pref}_eps", [128, 1])
        b.memset(k["eps"], LN_EPS)
        k["nh"] = b.sb(f"{pref}_nh", [128, 1])
        b.memset(k["nh"], -0.5)
        k["inv"] = b.sb(f"{pref}_inv", [128, 1])
        b.memset(k["inv"], 1.0 / DM)
        return k

    def ln_tile(self, kit, acc, out, gamma=None, beta=None, tag="ln"):
        b = self.b
        bs = [b.bank() for _ in range(8)]
        for kk in range(16):
            ch = acc[:, kk * TT:(kk + 1) * TT]
            sq = kit["sq"][kk % 2]
            b.act(sq, ch, AF.Square, tag=tag + "sq")
            for tg in range(4):
                b.mm(bs[tg][:, 0:1], ch[:, tg * 128:(tg + 1) * 128], self.ones_col,
                     start=(kk == 0), stop=(kk == 15), tag=tag + "s")
                b.mm(bs[4 + tg][:, 0:1], sq[:, tg * 128:(tg + 1) * 128], self.ones_col,
                     start=(kk == 0), stop=(kk == 15), tag=tag + "q")
        st = kit["st"]
        for tg in range(4):
            c = st[:, tg * 3:tg * 3 + 3]
            b.ts(c[:, 0:1], bs[tg][:, 0:1], 1.0 / DM, ALU.mult)
            b.ts(c[:, 1:2], bs[4 + tg][:, 0:1], 1.0 / DM, ALU.mult)
        for tg in range(4):
            c = st[:, tg * 3:tg * 3 + 3]
            b.tt(c[:, 2:3], c[:, 0:1], c[:, 0:1], ALU.mult)
            b.tt(c[:, 1:2], c[:, 1:2], c[:, 2:3], ALU.subtract)
            b.act(c[:, 1:2], c[:, 1:2], AF.Ln, bias=kit["eps"])
            b.act(c[:, 1:2], c[:, 1:2], AF.Exp, scale=kit["nh"])
            b.tt(c[:, 2:3], c[:, 0:1], c[:, 1:2], ALU.mult)
        for tg in range(4):
            c = st[:, tg * 3:tg * 3 + 3]
            pst = b.bank()
            b.trans(pst[:1, 0:128], c[:, 1:2], self.i128, tag=tag + "T1")
            b.copy(kit["rrow"][:, tg * 128:(tg + 1) * 128], pst[:1, 0:128])
            pst2 = b.bank()
            b.trans(pst2[:1, 0:128], c[:, 2:3], self.i128, tag=tag + "T2")
            b.copy(kit["nrow"][:, tg * 128:(tg + 1) * 128], pst2[:1, 0:128])
        psb = b.bank()
        b.mm(psb[:, 0:TT], self.ones_row, kit["rrow"], tag=tag + "bc1")
        b.copy(kit["rbc"], psb[:, 0:TT])
        psb2 = b.bank()
        b.mm(psb2[:, 0:TT], self.ones_row, kit["nrow"], tag=tag + "bc2")
        b.copy(kit["nbc"], psb2[:, 0:TT])
        for kk in range(16):
            b.tt(out[:, kk * TT:(kk + 1) * TT], acc[:, kk * TT:(kk + 1) * TT],
                 kit["rbc"], ALU.mult, tag=tag + "n1")
        for kk in range(16):
            och = out[:, kk * TT:(kk + 1) * TT]
            b.tt(och, och, kit["nbc"], ALU.subtract, tag=tag + "n2")
        if gamma is not None:
            for kk in range(16):
                och = out[:, kk * TT:(kk + 1) * TT]
                b.ts(och, och, gamma[:, kk:kk + 1], ALU.mult,
                     beta[:, kk:kk + 1], ALU.add, tag=tag + "gb")

    def p3(self):
        b = self.b
        gbuf = b.sb("p3_gbuf", [128, 16 * TT])
        rbuf = b.sb("p3_rbuf", [128, 16 * TT])
        acc = b.sb("p3_acc", [128, 16 * TT])
        wsl = [b.sb(f"p3_w{i}", [128, 2048]) for i in range(2)]
        kit = self.ln_kit("p3")
        for tt2 in range(2):
            for k0 in range(0, 16, 4):
                idx = k0 * 2 + tt2
                dst3 = gbuf[:, k0 * TT:(k0 + 4) * TT].rearrange("p (k c) -> p k c", k=4)
                src3 = self.cc1out[0, k0 * 128:(k0 + 4) * 128, 0:TT].rearrange(
                    "(k p) c -> p k c", p=128)
                def ldfn(e, idx=idx, dst=dst3, s3=src3):
                    v = e.value_load(self.toksb[0:1, idx:idx + 1])
                    import dataclasses as _dc
                    return e.dma_start(out=dst, in_=_dc.replace(s3, offset=v))
                b.s.add("sync", ldfn, ins=[self.toksb, self.cc1out],
                        outs=[gbuf], dma=True, tag="p3g")
            for kk in range(16):
                b.dma("sync", rbuf[:, kk * TT:(kk + 1) * TT],
                      self.xresT[kk * 128:(kk + 1) * 128, tt2 * TT:(tt2 + 1) * TT], tag="p3r")
            for m in range(16):
                w = wsl[m % 2]
                b.dma(b.wq(w.tensor.name), w, self.wout[m], tag="p3w")
                ps = b.bank()
                for kk in range(16):
                    b.mm(ps[:, 0:TT], w[:, kk * 128:(kk + 1) * 128],
                         gbuf[:, kk * TT:(kk + 1) * TT], start=(kk == 0), stop=(kk == 15))
                b.tt(acc[:, m * TT:(m + 1) * TT], ps[:, 0:TT],
                     rbuf[:, m * TT:(m + 1) * TT], ALU.add, tag="p3res")
            self.ln_tile(kit, acc, acc, tag="p3ln")
            for m in range(16):
                b.dma("gpsimd", self.x1d[m * 128:(m + 1) * 128, tt2 * TT:(tt2 + 1) * TT],
                      acc[:, m * TT:(m + 1) * TT], tag="p3o")
        b.free_prefix("p3_")

    def ffn0(self):
        b = self.b
        x1b = b.sb("f0_x1", [128, 16 * TT])
        accb = b.sb("f0_acc", [128, 16 * TT])
        kit = self.ln_kit("f0k")
        gatew_sb = b.sb("f0_gw", [128, 16 * 8])
        for kk in range(16):
            b.dma("sync", gatew_sb[:, kk * 8:(kk + 1) * 8], self.gatew[kk])
        lngam = b.sb("f0_lng", [128, 16])
        lnbet = b.sb("f0_lnb", [128, 16])
        b.dma("sync", lngam, self.lngb[:, 0:16])
        b.dma("sync", lnbet, self.lngb[:, 16:32])
        lg = b.sb("f0_lg", [128, 8])
        eqt = b.sb("f0_eq", [128, 8])
        pfa = b.sb("f0_pfa", [128, 8])
        pfb = b.sb("f0_pfb", [128, 8])
        m1 = b.sb("f0_m1", [128, 1])
        m2 = b.sb("f0_m2", [128, 1])
        srow = b.sb("f0_srow", [1, TT])
        sbc = b.sb("f0_sbc", [128, TT])
        esl = [b.sb(f"f0_ew{i}", [128, 2048]) for i in range(2)]
        usl = [b.sb(f"f0_uw{i}", [128, 2048]) for i in range(2)]
        dsl = [b.sb(f"f0_dw{i}", [128, 2048]) for i in range(2)]
        hbuf = b.sb("f0_h", [128, 16 * TT])
        for tt2 in range(2):
            for kk in range(16):
                b.dma("sync", x1b[:, kk * TT:(kk + 1) * TT],
                      self.x1d[kk * 128:(kk + 1) * 128, tt2 * TT:(tt2 + 1) * TT], tag="f0x")
            for tg in range(4):
                psg = b.bank()
                for kk in range(16):
                    b.mm(psg[:, 0:8], x1b[:, kk * TT + tg * 128:kk * TT + (tg + 1) * 128],
                         gatew_sb[:, kk * 8:(kk + 1) * 8], start=(kk == 0), stop=(kk == 15),
                         tag="f0gmm")
                b.copy(lg, psg[:, 0:8])
                b.reduce(m1, lg, ALU.max)
                b.ts(eqt, lg, m1, ALU.is_ge)
                b.scan(pfa, eqt, eqt, 0.0, ALU.add, ALU.bypass)
                b.tt(pfa, pfa, eqt, ALU.subtract)
                b.ts(pfb, pfa, 0.0, ALU.is_equal)
                b.tt(eqt, eqt, pfb, ALU.mult)
                b.stt(lg, eqt, -3e38, lg, ALU.mult, ALU.add)
                b.reduce(m2, lg, ALU.max)
                b.tt(m1, m1, m2, ALU.add)
                pst = b.bank()
                b.trans(pst[:1, 0:128], m1, self.i128, tag="f0T")
                b.copy(srow[:, tg * 128:(tg + 1) * 128], pst[:1, 0:128])
            psb = b.bank()
            b.mm(psb[:, 0:TT], self.ones_row, srow, tag="f0bc")
            b.copy(sbc, psb[:, 0:TT])
            for m in range(16):
                w = esl[m % 2]
                b.dma(b.wq(w.tensor.name), w, self.expw[m], tag="f0ew")
                ps = b.bank()
                for kk in range(16):
                    b.mm(ps[:, 0:TT], w[:, kk * 128:(kk + 1) * 128],
                         x1b[:, kk * TT:(kk + 1) * TT], start=(kk == 0), stop=(kk == 15))
                b.tt(accb[:, m * TT:(m + 1) * TT], ps[:, 0:TT], sbc, ALU.mult, tag="f0moe")
            for p in range(4):
                for mi in range(16):
                    m = p * 16 + mi
                    w = usl[mi % 2]
                    b.dma(b.wq(w.tensor.name), w, self.upw[m], tag="f0uw")
                    ps = b.bank()
                    for kk in range(16):
                        b.mm(ps[:, 0:TT], w[:, kk * 128:(kk + 1) * 128],
                             x1b[:, kk * TT:(kk + 1) * TT], start=(kk == 0), stop=(kk == 15))
                    hch = hbuf[:, mi * TT:(mi + 1) * TT]
                    b.act(hch, ps[:, 0:TT], AF.Relu, tag="f0relu")
                    b.act(hch, hch, AF.Square, tag="f0sq")
                for mo in range(16):
                    w = dsl[mo % 2]
                    b.dma(b.wq(w.tensor.name), w, self.downw[mo, p], tag="f0dw")
                    ps = b.bank()
                    for kk in range(16):
                        b.mm(ps[:, 0:TT], w[:, kk * 128:(kk + 1) * 128],
                             hbuf[:, kk * TT:(kk + 1) * TT], start=(kk == 0), stop=(kk == 15))
                    och = accb[:, mo * TT:(mo + 1) * TT]
                    b.tt(och, och, ps[:, 0:TT], ALU.add, tag="f0dacc")
            self.ln_tile(kit, accb, accb, gamma=lngam, beta=lnbet, tag="f0l1")
            for kk in range(16):
                b.tt(accb[:, kk * TT:(kk + 1) * TT], accb[:, kk * TT:(kk + 1) * TT],
                     x1b[:, kk * TT:(kk + 1) * TT], ALU.add, tag="f0r")
            self.ln_tile(kit, accb, accb, tag="f0l2")
            for m in range(16):
                b.dma("gpsimd", self.cc2in[tt2][m * 128:(m + 1) * 128, 0:TT],
                      accb[:, m * TT:(m + 1) * TT], tag="f0o")
            if tt2 == 0:
                b.cc("AllGather", self.cc2in[0][:], self.cc2out[0][:], GROUPS4, tag="E2a")
            if "d_x2" in self.dmp:
                for m in range(16):
                    b.dma("gpsimd", self.dmp["d_x2"][m * 128:(m + 1) * 128, tt2 * TT:(tt2 + 1) * TT],
                          accb[:, m * TT:(m + 1) * TT])
        b.free_prefix("f0_", "f0k_")

    def setup_l1(self):
        b = self.b
        self.xg = b.sb("l1_xg", [128, 16 * TT])
        self.wq1s = b.sb("l1_wq", [128, 4 * 2048])
        self.wk1s = b.sb("l1_wk", [128, 4 * 2048])
        self.wv1s = b.sb("l1_wv", [128, 16 * 512])
        for m in range(4):
            b.dma("sync", self.wq1s[:, m * 2048:(m + 1) * 2048], self.wq1[m])
            b.dma("sync", self.wk1s[:, m * 2048:(m + 1) * 2048], self.wk1[m])
        for kk in range(16):
            b.dma("sync", self.wv1s[:, kk * 512:(kk + 1) * 512], self.wv1cols[kk])
        self.qb = b.sb("l1_qb", [128, 4 * TT])
        self.kb = b.sb("l1_kb", [128, 4 * TT])
        self.vb = b.sb("l1_vb", [128, 4 * TT])
        self.cosb = b.sb("l1_cos", [64, TT])
        self.sinb = b.sb("l1_sin", [64, TT])
        self.rot = [b.sb(f"l1_rot{j}", [64, TT]) for j in range(4)]
        self.kv = [b.sb(f"l1_kv{h}", [128, 128]) for h in range(4)]
        for h in range(4):
            b.memset(self.kv[h], 0.0)
        self.qdecs = b.sb("l1_qdec", [128, 4 * 128])
        self.dmks = b.sb("l1_dmk", [128, 4 * 128])
        self.kdecs = b.sb("l1_kdec", [128, 4])
        self.rcs = b.sb("l1_rc", [128, 4])
        for h in range(4):
            b.dma("sync", self.qdecs[:, h * 128:(h + 1) * 128], self.c_qdec[h])
            b.dma("sync", self.dmks[:, h * 128:(h + 1) * 128], self.c_dmaskT[h])
        b.dma("sync", self.kdecs, self.c_kdec)
        b.dma("sync", self.rcs, self.c_rC)
        self.qd_sc = b.sb("l1_qd", [128, 128])
        self.sT_sc = b.sb("l1_sT", [128, 128])
        self.otm_sc = b.sb("l1_otm", [128, 128])
        self.ktm_sc = b.sb("l1_ktm", [128, 128])
        self.o3 = b.sb("l1_o3", [128, 4 * TT])

    def l1_tile(self, t):
        b = self.b
        r, half = t // 2, t % 2
        for kk in range(16):
            b.dma("sync", self.xg[:, kk * TT:(kk + 1) * TT],
                  self.cc2out[half][r * DM + kk * 128: r * DM + (kk + 1) * 128, 0:TT],
                  tag="l1x")
        c0 = t * TT
        b.dma("sync", self.cosb, self.c_rotcos[:, c0:c0 + TT], tag="l1cs")
        b.dma("sync", self.sinb, self.c_rotsin[:, c0:c0 + TT], tag="l1cs")
        for (wsb, dst) in ((self.wq1s, self.qb), (self.wk1s, self.kb)):
            for m in range(4):
                ps = b.bank()
                for kk in range(16):
                    b.mm(ps[:, 0:TT], wsb[:, m * 2048 + kk * 128:m * 2048 + (kk + 1) * 128],
                         self.xg[:, kk * TT:(kk + 1) * TT], start=(kk == 0), stop=(kk == 15),
                         tag="l1qk")
                dch = dst[:, m * TT:(m + 1) * TT]
                b.tt(self.rot[0], ps[0:64, 0:TT], self.cosb, ALU.mult, tag="l1r1")
                b.tt(self.rot[1], ps[64:128, 0:TT], self.sinb, ALU.mult, tag="l1r2")
                b.tt(self.rot[2], ps[64:128, 0:TT], self.cosb, ALU.mult, tag="l1r4")
                b.tt(self.rot[3], ps[0:64, 0:TT], self.sinb, ALU.mult, tag="l1r5")
                b.tt(dch[0:64, :], self.rot[0], self.rot[1], ALU.add, tag="l1r3")
                b.tt(dch[64:128, :], self.rot[2], self.rot[3], ALU.subtract, tag="l1r6")
        for tg in range(4):
            ps = b.bank()
            for kk in range(16):
                b.mm(ps[:, 0:TT], self.xg[:, kk * TT + tg * 128:kk * TT + (tg + 1) * 128],
                     self.wv1s[:, kk * 512:(kk + 1) * 512], start=(kk == 0), stop=(kk == 15),
                     tag="l1v")
            b.copy(self.vb[:, tg * TT:(tg + 1) * TT], ps[:, 0:TT])
        for cj in range(4):
            for h in range(4):
                qf = self.qb[:, h * TT + cj * 128:h * TT + (cj + 1) * 128]
                kf = self.kb[:, h * TT + cj * 128:h * TT + (cj + 1) * 128]
                vtm = self.vb[:, cj * TT + h * 128:cj * TT + (h + 1) * 128]
                kv = self.kv[h]
                b.tt(self.qd_sc, qf, self.qdecs[:, h * 128:(h + 1) * 128], ALU.mult, tag="l1qd")
                psS = b.bank()
                b.mm(psS[:, 0:128], kf, qf, tag="l1sc")
                b.tt(self.sT_sc, psS[:, 0:128], self.dmks[:, h * 128:(h + 1) * 128],
                     ALU.mult, tag="l1dm")
                pso = b.bank()
                b.mm(pso[:, 0:128], self.sT_sc, vtm, start=True, stop=False, tag="l1intra")
                b.mm(pso[:, 0:128], self.qd_sc, kv, start=False, stop=True, tag="l1inter")
                b.copy(self.otm_sc, pso[:, 0:128])
                psT = b.bank()
                b.trans(psT[:, 0:128], kf, self.i128, tag="l1kT")
                b.ts(self.ktm_sc, psT[:, 0:128], self.kdecs[:, h:h + 1], ALU.mult, tag="l1kd")
                psU = b.bank()
                b.mm(psU[:, 0:128], self.ktm_sc, vtm, tag="l1kv")
                b.stt(kv, kv, self.rcs[:, h:h + 1], psU[:, 0:128], ALU.mult, ALU.add, tag="l1kvu")
                psF = b.bank()
                b.trans(psF[:, 0:128], self.otm_sc, self.i128, tag="l1oT")
                b.copy(self.o3[:, h * TT + cj * 128:h * TT + (cj + 1) * 128],
                       psF[:, 0:128], tag="l1oev")
        cin3 = self.cc3in[t // 4]
        cx3 = (t % 4) * TT
        for h in range(4):
            b.dma("gpsimd", cin3[h * 128:(h + 1) * 128, cx3:cx3 + TT],
                  self.o3[:, h * TT:(h + 1) * TT], tag="l1out")
        if "d_l1qkv" in self.dmp:
            for h in range(4):
                b.dma("gpsimd", self.dmp["d_l1qkv"][h * 128:(h + 1) * 128, c0:c0 + TT],
                      self.qb[:, h * TT:(h + 1) * TT])
                b.dma("gpsimd", self.dmp["d_l1qkv"][512 + h * 128:512 + (h + 1) * 128, c0:c0 + TT],
                      self.kb[:, h * TT:(h + 1) * TT])
            for tg in range(4):
                b.dma("gpsimd", self.dmp["d_l1qkv"][1024 + tg * 128:1024 + (tg + 1) * 128, c0:c0 + TT],
                      self.vb[:, tg * TT:(tg + 1) * TT])

    def p7(self):
        b = self.b
        gbuf = b.sb("p7_gb", [128, 16 * TT])
        nrm = b.sb("p7_n", [128, 16 * TT])
        rbuf = b.sb("p7_rb", [128, 16 * TT])
        acc = b.sb("p7_a", [128, 16 * TT])
        wsl = [b.sb(f"p7_w{i}", [128, 2048]) for i in range(2)]
        sq = b.sb("p7_sq", [128, TT])
        stt_ = b.sb("p7_st", [128, 8])
        rrow = b.sb("p7_rrow", [1, TT])
        rbc = b.sb("p7_rbc", [128, TT])
        epsr = b.sb("p7_eps", [128, 1])
        b.memset(epsr, LN_EPS)
        nh = b.sb("p7_nh", [128, 1])
        b.memset(nh, -0.5)
        rmsw_sb = b.sb("p7_rw", [128, 16])
        b.dma("sync", rmsw_sb, self.rmsw)
        kit = self.ln_kit("p7k")
        for tt2 in range(2):
            for k0 in range(0, 16, 4):
                idx = k0 * 2 + tt2
                dst3 = gbuf[:, k0 * TT:(k0 + 4) * TT].rearrange("p (k c) -> p k c", k=4)
                src3 = self.cc3out[0, k0 * 128:(k0 + 4) * 128, 0:TT].rearrange(
                    "(k p) c -> p k c", p=128)
                def ldfn(e, idx=idx, dst=dst3, s3=src3):
                    v = e.value_load(self.toksb[0:1, idx:idx + 1])
                    import dataclasses as _dc
                    return e.dma_start(out=dst, in_=_dc.replace(s3, offset=v))
                b.s.add("sync", ldfn, ins=[self.toksb, self.cc3out],
                        outs=[gbuf], dma=True, tag="p7g")
            bs = [b.bank() for _ in range(4)]
            for kk in range(16):
                ch = gbuf[:, kk * TT:(kk + 1) * TT]
                b.act(sq, ch, AF.Square, tag="p7sq")
                for tg in range(4):
                    b.mm(bs[tg][:, 0:1], sq[:, tg * 128:(tg + 1) * 128], self.ones_col,
                         start=(kk == 0), stop=(kk == 15), tag="p7s")
            for tg in range(4):
                c = stt_[:, tg * 2:tg * 2 + 2]
                b.ts(c[:, 0:1], bs[tg][:, 0:1], 1.0 / DM, ALU.mult)
            for tg in range(4):
                c = stt_[:, tg * 2:tg * 2 + 2]
                b.act(c[:, 0:1], c[:, 0:1], AF.Ln, bias=epsr)
                b.act(c[:, 0:1], c[:, 0:1], AF.Exp, scale=nh)
                pst = b.bank()
                b.trans(pst[:1, 0:128], c[:, 0:1], self.i128, tag="p7T")
                b.copy(rrow[:, tg * 128:(tg + 1) * 128], pst[:1, 0:128])
            psb = b.bank()
            b.mm(psb[:, 0:TT], self.ones_row, rrow, tag="p7bc")
            b.copy(rbc, psb[:, 0:TT])
            for kk in range(16):
                b.tt(nrm[:, kk * TT:(kk + 1) * TT], gbuf[:, kk * TT:(kk + 1) * TT],
                     rbc, ALU.mult, tag="p7n1")
            for kk in range(16):
                b.ts(nrm[:, kk * TT:(kk + 1) * TT], nrm[:, kk * TT:(kk + 1) * TT],
                     rmsw_sb[:, kk:kk + 1], ALU.mult, tag="p7n2")
            for kk in range(16):
                b.dma("sync", rbuf[:, kk * TT:(kk + 1) * TT],
                      self.cc2in[tt2][kk * 128:(kk + 1) * 128, 0:TT], tag="p7r")
            for m in range(16):
                w = wsl[m % 2]
                b.dma(b.wq(w.tensor.name), w, self.wo1[m], tag="p7w")
                ps = b.bank()
                for kk in range(16):
                    b.mm(ps[:, 0:TT], w[:, kk * 128:(kk + 1) * 128],
                         nrm[:, kk * TT:(kk + 1) * TT], start=(kk == 0), stop=(kk == 15))
                b.tt(acc[:, m * TT:(m + 1) * TT], ps[:, 0:TT],
                     rbuf[:, m * TT:(m + 1) * TT], ALU.add, tag="p7res")
            self.ln_tile(kit, acc, acc, tag="p7ln")
            for m in range(16):
                b.dma("gpsimd", self.x3d[m * 128:(m + 1) * 128, tt2 * TT:(tt2 + 1) * TT],
                      acc[:, m * TT:(m + 1) * TT], tag="p7o")
            if "d_x3" in self.dmp:
                for m in range(16):
                    b.dma("gpsimd", self.dmp["d_x3"][m * 128:(m + 1) * 128, tt2 * TT:(tt2 + 1) * TT],
                          acc[:, m * TT:(m + 1) * TT])
        b.free_prefix("p7_", "p7k_")

    def ffn1(self):
        b = self.b
        x3b = b.sb("f1_x3", [128, 16 * TT])
        accb = b.sb("f1_acc", [128, 16 * TT])
        hbuf = b.sb("f1_h", [128, 22 * TT])
        gsl = [b.sb(f"f1_gw{i}", [128, 2048]) for i in range(2)]
        uslb = [b.sb(f"f1_uw{i}", [128, 2048]) for i in range(2)]
        dslb = [b.sb(f"f1_dw{i}", [128, 2816]) for i in range(2)]
        s1g_sb = b.sb("f1_sg", [128, 44])
        s1u_sb = b.sb("f1_su", [128, 44])
        s1d_sb = b.sb("f1_sd", [128, 16])
        b.dma("sync", s1g_sb, self.s1g)
        b.dma("sync", s1u_sb, self.s1u)
        b.dma("sync", s1d_sb, self.s1d)
        tgu = b.sb("f1_tg", [128, TT])
        kit = self.ln_kit("f1k")
        for tt2 in range(2):
            for kk in range(16):
                b.dma("sync", x3b[:, kk * TT:(kk + 1) * TT],
                      self.x3d[kk * 128:(kk + 1) * 128, tt2 * TT:(tt2 + 1) * TT], tag="f1x")
            for p in range(2):
                nmi = 22
                for mi in range(nmi):
                    m = p * 22 + mi
                    wg = gsl[mi % 2]
                    wu = uslb[mi % 2]
                    b.dma(b.wq(wg.tensor.name), wg, self.w1g[m], tag="f1gw")
                    b.dma(b.wq(wu.tensor.name), wu, self.w1u[m], tag="f1uw")
                    psg = b.bank()
                    for kk in range(16):
                        b.mm(psg[:, 0:TT], wg[:, kk * 128:(kk + 1) * 128],
                             x3b[:, kk * TT:(kk + 1) * TT], start=(kk == 0), stop=(kk == 15))
                    psu = b.bank()
                    for kk in range(16):
                        b.mm(psu[:, 0:TT], wu[:, kk * 128:(kk + 1) * 128],
                             x3b[:, kk * TT:(kk + 1) * TT], start=(kk == 0), stop=(kk == 15))
                    hch = hbuf[:, mi * TT:(mi + 1) * TT]
                    b.act(hch, psg[:, 0:TT], AF.Silu, scale=s1g_sb[:, m:m + 1], tag="f1si")
                    b.ts(tgu, psu[:, 0:TT], s1u_sb[:, m:m + 1], ALU.mult, tag="f1us")
                    b.tt(hch, hch, tgu, ALU.mult, tag="f1h")
                for mo in range(16):
                    w = dslb[mo % 2]
                    b.dma(b.wq(w.tensor.name), w, self.w1d[mo, p], tag="f1dw")
                    ps = b.bank()
                    for kk in range(22):
                        b.mm(ps[:, 0:TT], w[:, kk * 128:(kk + 1) * 128],
                             hbuf[:, kk * TT:(kk + 1) * TT], start=(kk == 0), stop=(kk == 21))
                    och = accb[:, mo * TT:(mo + 1) * TT]
                    if p == 0:
                        b.copy(och, ps[:, 0:TT], tag="f1d0")
                    else:
                        b.tt(och, och, ps[:, 0:TT], ALU.add, tag="f1d1")
            for kk in range(16):
                och = accb[:, kk * TT:(kk + 1) * TT]
                b.stt(och, och, s1d_sb[:, kk:kk + 1], x3b[:, kk * TT:(kk + 1) * TT],
                      ALU.mult, ALU.add, tag="f1res")
            self.ln_tile(kit, accb, accb, tag="f1ln")
            for m in range(16):
                b.dma("gpsimd", self.yT[m * 128:(m + 1) * 128, tt2 * TT:(tt2 + 1) * TT],
                      accb[:, m * TT:(m + 1) * TT], tag="f1o")
        b.free_prefix("f1_", "f1k_")

    def build(self, l0_only_tiles=None):
        b = self.b
        self.setup_consts()
        self.setup_l0()
        if l0_only_tiles is not None:
            for t in range(l0_only_tiles):
                self.l0_tile(t)
            return b.s.resolve_and_emit(self.nc)
        for t in range(NT0):
            self.l0_tile(t)
            if t == 3:
                self.b.cc("AllGather", self.cc1in[0][:], self.cc1out[0], GROUPS4, tag="E1a")
        for ap in self.l2ps:
            b.bank_release(ap)
        b.free_prefix("xt", "wqz", "cvin", "zgw", "Sst", "gb_", "negcol4", "onecol4", "normw_sb",
                      "sc128", "bi128", "one11", "e6_11", "q_rs", "q_c1", "q_c2",
                      "colq_", "eg_bc", "u_", "inv128c", "oh64", "ohc", "nohc", "oh128", "sc128_4", "bi128_4", "one4", "e6_4", "tsave",
                      "epsc", "nhalf64", "mix", "conv_sc0", "conv_sc1", "conv_sc2", "conv_sc3", "sq_sc", "wba_sb", "convw_sb",
                      "dtb_sb", "alog_sb", "negexpA")
        self.b.cc("AllGather", self.cc1in[1][:], self.cc1out[1], GROUPS4, tag="E1b")
        b.s.fence()
        self.p3()
        b.s.fence()
        if "d_x1" in self.dmp:
            b.dma("gpsimd", self.dmp["d_x1"], self.x1d)
            b.s.fence()
        self.ffn0()
        b.s.fence()
        self.b.cc("AllGather", self.cc2in[1][:], self.cc2out[1][:], GROUPS4, tag="E2b")
        self.setup_l1()
        for t in range(NT0):
            self.l1_tile(t)
            if t == 3:
                self.b.cc("AllGather", self.cc3in[0][:], self.cc3out[0], GROUPS4, tag="E3a")
        self.b.cc("AllGather", self.cc3in[1][:], self.cc3out[1], GROUPS4, tag="E3b")
        b.free_prefix("l1_")
        b.s.fence()
        self.p7()
        b.s.fence()
        self.ffn1()
        return b.s.resolve_and_emit(self.nc)


def _slab(w):
    K, M = w.shape
    nK, nM = K // 128, M // 128
    return np.ascontiguousarray(
        w.reshape(nK, 128, nM, 128).transpose(2, 1, 0, 3).reshape(nM, 128, nK * 128))


def _build_slopes(n):
    def pow2(nn):
        start = 2.0 ** (-2.0 ** (-(math.log2(nn) - 3)))
        return [start * start ** i for i in range(nn)]
    if math.log2(n).is_integer():
        s = pow2(n)
    else:
        closest = 2 ** math.floor(math.log2(n))
        s = pow2(closest) + pow2(2 * closest)[0::2][:n - closest]
    return np.asarray(s, dtype=np.float64)


def make_in_maps(inputs):
    f32 = np.float32
    x = np.asarray(inputs["L0_attn_hidden_states"], f32)
    Wqkvz = np.asarray(inputs["L0_attn_W_qkvz"], f32)
    Wba = np.asarray(inputs["L0_attn_W_ba"], f32)
    convw = np.asarray(inputs["L0_attn_conv_weight"], f32)
    dtb = np.asarray(inputs["L0_attn_dt_bias"], f32)
    alog = np.asarray(inputs["L0_attn_A_log"], f32)
    normw = np.asarray(inputs["L0_attn_norm_weight"], f32)
    Wout = np.asarray(inputs["L0_attn_W_out"], f32)
    gatew = np.asarray(inputs["L0_ffn_gate_w"], f32)
    upw = np.asarray(inputs["L0_ffn_up_w"], f32)
    downw = np.asarray(inputs["L0_ffn_down_w"], f32)
    expw = np.asarray(inputs["L0_ffn_expert_weight"], f32)
    lng = np.asarray(inputs["L0_ffn_ln_gamma"], f32)
    lnb = np.asarray(inputs["L0_ffn_ln_beta"], f32)
    Wq1 = np.asarray(inputs["L1_attn_Wq"], f32)
    Wk1 = np.asarray(inputs["L1_attn_Wk"], f32)
    Wv1 = np.asarray(inputs["L1_attn_Wv"], f32)
    Wo1 = np.asarray(inputs["L1_attn_Wo"], f32)
    rmsw = np.asarray(inputs["L1_attn_rms_w"], f32)
    g1 = np.asarray(inputs["L1_ffn_gate_w"], f32)
    u1 = np.asarray(inputs["L1_ffn_up_w"], f32)
    d1 = np.asarray(inputs["L1_ffn_down_w"], f32)
    s1g = np.asarray(inputs["L1_ffn_gate_s"], f32)
    s1u = np.asarray(inputs["L1_ffn_up_s"], f32)
    s1d = np.asarray(inputs["L1_ffn_down_s"], f32)

    wout_s = _slab(Wout)
    upw_s = _slab(upw)
    dw = _slab(downw)
    downw_s = np.ascontiguousarray(dw.reshape(16, 128, 4, 2048).transpose(0, 2, 1, 3))
    expw_s = _slab(expw)
    gatew_s = np.ascontiguousarray(gatew.reshape(16, 128, 8))
    lngb = np.concatenate([lng.reshape(16, 128).T, lnb.reshape(16, 128).T], axis=1)
    lngb = np.ascontiguousarray(lngb)
    wo1_s = _slab(Wo1)
    rmsw_h = np.ascontiguousarray(rmsw.reshape(16, 128).T)
    w1g_s = _slab(np.ascontiguousarray(g1.T))
    w1u_s = _slab(np.ascontiguousarray(u1.T))
    dT = _slab(np.ascontiguousarray(d1.T))
    w1d_s = np.ascontiguousarray(dT.reshape(16, 128, 2, 2816).transpose(0, 2, 1, 3))
    s1g_h = np.ascontiguousarray(s1g.reshape(44, 128).T)
    s1u_h = np.ascontiguousarray(s1u.reshape(44, 128).T)
    s1d_h = np.ascontiguousarray(s1d.reshape(16, 128).T)
    idx = np.arange(64)
    maskS = np.where(idx[None, :] > idx[:, None], 0.0, -3e38).astype(f32)
    maskI = np.where(idx[None, :] >= idx[:, None], 0.0, -3e38).astype(f32)
    maskS = np.tile(maskS, (2, 1))
    maskI = np.tile(maskI, (2, 1))
    i128 = np.eye(128, dtype=f32)
    freq = (1.0 / 1024.0) ** np.linspace(0.0, 1.0, 32, dtype=f32)
    freq = np.concatenate([freq, np.zeros(32, dtype=f32)])
    tpos = np.arange(S, dtype=np.float64)
    theta = tpos[:, None] * freq[None, :].astype(np.float64)
    rotcos = np.ascontiguousarray(np.cos(theta).T.astype(f32))
    rotsin = np.ascontiguousarray(np.sin(theta).T.astype(f32))
    slopes = _build_slopes(16)
    rr = np.exp(-slopes)
    ii = np.arange(128, dtype=np.float64)
    xT = [np.ascontiguousarray(x[bb].T) for bb in range(2)]
    oh128c = np.zeros((128, 16), f32)
    for h in range(4):
        oh128c[:, 4 * h + h] = 0.0
        oh128c[:, 4 * h + h] = 1.0
    oh64 = np.zeros((4, 4 * 64), f32)
    ohc = np.zeros((4, 4 * 128), f32)
    for h in range(4):
        oh64[h, h * 64:(h + 1) * 64] = 1.0
        ohc[h, h * 128:(h + 1) * 128] = 1.0

    in_maps = []
    for c in range(NCORE):
        bb, q = c // 4, c % 4
        hg = [4 * q + j for j in range(4)]
        tok0 = 1024 * q
        toktab = np.zeros((1, 32), np.int32)
        half_w = S // 2
        for kk in range(16):
            for tt2 in range(2):
                pos = tok0 + tt2 * TT
                toktab[0, kk * 2 + tt2] = ((pos // half_w) * DM * half_w
                                           + kk * 128 * half_w + pos % half_w)
        selq = Wqkvz[:, 2048 * q:2048 * q + 2048]
        wba_sel = np.concatenate([Wba[:, [2 * h for h in hg]],
                                  Wba[:, [2 * h + 1 for h in hg]]], axis=1)
        convsel = np.stack([convw[tau * 2048 + hh * 128: tau * 2048 + (hh + 1) * 128, :]
                            for j, hh in enumerate(hg) for tau in range(3)])
        qdec = np.stack([np.tile((rr[hh] ** (ii + 1.0)).astype(f32)[None, :], (128, 1))
                         for hh in hg])
        dmT = []
        for hh in hg:
            diff = ii[:, None] - ii[None, :]
            dm = np.where(diff >= 0, rr[hh] ** np.where(diff >= 0, diff, 0.0), 0.0)
            dmT.append(dm.T.astype(f32))
        kdec = np.stack([(rr[hh] ** (127.0 - ii)).astype(f32) for hh in hg], axis=1)
        rc = np.stack([np.full(128, rr[hh] ** 128.0, dtype=f32) for hh in hg], axis=1)
        m = {
            "xT": xT[bb],
            "xresT": np.ascontiguousarray(xT[bb][:, tok0:tok0 + TPC]),
            "tokoff": toktab,
            "wqkvz": _slab(selq),
            "wba": np.ascontiguousarray(wba_sel.reshape(16, 128, 8)),
            "convw": np.ascontiguousarray(convsel),
            "dtb": np.asarray([[dtb[hh]] for hh in hg], f32),
            "alog": np.asarray([[alog[hh]] for hh in hg], f32),
            "normw": np.ascontiguousarray(normw[:, None]),
            "wout": wout_s, "gatew": gatew_s, "upw": upw_s, "downw": downw_s,
            "expw": expw_s, "lngb": lngb,
            "wq1": _slab(Wq1[:, 512 * q:512 * q + 512]),
            "wk1": _slab(Wk1[:, 512 * q:512 * q + 512]),
            "wv1cols": np.ascontiguousarray(
                Wv1[:, 512 * q:512 * q + 512].reshape(16, 128, 512)),
            "wo1": wo1_s, "rmsw": rmsw_h,
            "w1g": w1g_s, "w1u": w1u_s, "w1d": w1d_s,
            "s1g": s1g_h, "s1u": s1u_h, "s1d": s1d_h,
            "c_maskS": maskS, "c_maskI": maskI, "c_i128": i128,
            "c_rotcos": rotcos, "c_rotsin": rotsin,
            "c_qdec": qdec, "c_dmaskT": np.stack(dmT),
            "c_kdec": np.ascontiguousarray(kdec), "c_rC": np.ascontiguousarray(rc),
            "c_oh64": oh64, "c_ohc": ohc, "c_nohc": -ohc, "c_oh128": oh128c,
        }
        in_maps.append(m)
    return in_maps


_NC_CACHE = {}


def build_nc(dump=()):
    key = tuple(sorted(dump))
    if key not in _NC_CACHE:
        nc = bass.Bass("TRN2", target_bir_lowering=False)
        mk = MK(nc, dump=dump)
        mk.build()
        _NC_CACHE[key] = nc
    return _NC_CACHE[key]


def kernel(**inputs):
    in_maps = make_in_maps(inputs)
    nc = build_nc()
    res = run_bass_kernel_spmd(nc, in_maps, core_ids=list(range(NCORE)))
    y = np.zeros((B, S, DM), np.float32)
    for c in range(NCORE):
        bb, q = c // 4, c % 4
        y[bb, 1024 * q:1024 * (q + 1), :] = res.results[c]["yT"].T
    return y
